# version 01
# baseline (speedup 1.0000x reference)
import contextlib
import numpy as np
import ml_dtypes
import concourse.bass as bass
import concourse.mybir as mybir
from concourse.bass_utils import run_bass_kernel_spmd

F32 = mybir.dt.float32
BF16 = mybir.dt.bfloat16
AF = mybir.ActivationFunctionType
ALU = mybir.AluOpType
AX = mybir.AxisListType
BF16NP = ml_dtypes.bfloat16

NCORES = 8
SEQ = 4
T = 2048
DM = 1024
NCH = DM // 128
NTOK = SEQ * T
DH = 4096
HCH = DH // 128

LN_EPS_OUTER = 1e-5
LN_EPS_A = 1e-5
LN_EPS_B = 1e-6
CTX_EPS = 1e-6
RSQRT2 = 2.0 ** (-0.5)
KAPPA = RSQRT2 / (1.0 + RSQRT2)

ENGINES = ("tensor", "vector", "scalar", "gpsimd", "sync")
AUTOSEM = {"tensor": ("T", 1), "vector": ("V", 1), "scalar": ("S", 1),
           "gpsimd": ("G", 16), "sync": ("Y", 16)}
NOINC_OPS = {"wait_ge", "nop", "engine_nop"}


class EQ:
    def __init__(self, ctx, name):
        object.__setattr__(self, "_ctx", ctx)
        object.__setattr__(self, "_name", name)

    def __getattr__(self, opname):
        ctx = self._ctx
        name = self._name

        def call(*args, wait=None, tinc=True, **kwargs):
            if opname == "wait_ge":
                s, v = args

                def run_wait(eng, s=s, v=v):
                    eng.wait_ge(ctx.sems[s], v)
                ctx.q[name].append(run_wait)
                return None
            sem, step = AUTOSEM[name]
            do_inc = opname not in NOINC_OPS and (name != "tensor" or tinc)
            if do_inc:
                ctx.cnt[sem] = ctx.cnt.get(sem, 0) + step
            mark = (sem, ctx.cnt.get(sem, 0)) if do_inc else None

            def run(eng, args=args, kwargs=kwargs, wait=wait, mark=mark):
                if wait:
                    seen = {}
                    for s, v in wait:
                        if v > 0 and seen.get(s, -1) < v:
                            seen[s] = v
                    for s, v in seen.items():
                        eng.wait_ge(ctx.sems[s], v)
                inst = getattr(eng, opname)(*args, **kwargs)
                if mark is not None:
                    inst.then_inc(ctx.sems[mark[0]], AUTOSEM[name][1])
                return inst

            ctx.q[name].append(run)
            return mark

        return call


class Ctx:
    def __init__(self, nc, es):
        self.nc = nc
        self.es = es
        self.scopes = [es]
        self.q = {e: [] for e in ENGINES}
        self.sems = {}
        self.cnt = {}
        for s in ("T", "V", "S", "G", "Y"):
            self.sems[s] = es.enter_context(nc.semaphore(s))
            self.cnt[s] = 0
        self.tensor = EQ(self, "tensor")
        self.vector = EQ(self, "vector")
        self.scalar = EQ(self, "scalar")
        self.gpsimd = EQ(self, "gpsimd")
        self.sync = EQ(self, "sync")

    def c(self, sem):
        return self.cnt.get(sem, 0)

    def snap(self, *sems):
        return [(s, self.cnt.get(s, 0)) for s in sems]

    def flush(self, name=None):
        with self.nc.Block(name) as block:
            for e in ENGINES:
                ops = self.q[e]
                if ops:
                    getattr(block, e)(lambda eng, ops=tuple(ops): [f(eng) for f in ops])
        for e in ENGINES:
            self.q[e] = []

    def push_scope(self):
        self.scopes.append(contextlib.ExitStack())

    def pop_scope(self):
        self.scopes.pop().close()

    def _uname(self, name):
        self._uid = getattr(self, "_uid", 0) + 1
        return f"{name}_{self._uid}"

    def sbuf(self, name, shape, dtype):
        return self.scopes[-1].enter_context(
            self.nc.sbuf_tensor(self._uname(name), list(shape), dtype))

    def psum(self, name, shape, dtype=F32):
        return self.scopes[-1].enter_context(
            self.nc.psum_tensor(self._uname(name), list(shape), dtype))

    def init_epscols(self):
        self.eps_sb = self.sbuf("epsconsts", (128, 4), F32)
        self.eps_cols = {}
        for i, v in enumerate((1e-5, 1e-6)):
            self.vector.memset(self.eps_sb[:, i:i + 1], v)
            self.eps_cols[v] = i

    def epscol(self, v):
        return self.eps_sb[:, self.eps_cols[v]:self.eps_cols[v] + 1]


def tiles_lhsT(wT):
    I_, O_ = wT.shape
    KC, Q = I_ // 128, O_ // 128
    t = wT.reshape(KC, 128, Q, 128).transpose(2, 1, 0, 3).reshape(Q, 128, KC * 128)
    return np.ascontiguousarray(t)


def vec2d(v):
    return np.ascontiguousarray(np.asarray(v).reshape(-1, 128).T)


def emit_ln_stats(ctx, NTt, eps, val_chunks, sq_chunks, ones_sb, ps_s1, ps_s2,
                  stat_sb, dep):
    for c in range(NCH):
        ctx.tensor.matmul(ps_s1[:, 0:NTt], ones_sb[:, :], val_chunks(c),
                          start=(c == 0), stop=(c == NCH - 1),
                          wait=(dep if c == 0 else None), tinc=False)
        ctx.tensor.matmul(ps_s2[:, 0:NTt], ones_sb[:, :], sq_chunks(c),
                          start=(c == 0), stop=(c == NCH - 1),
                          tinc=(c == NCH - 1))
    stats_mm = ctx.snap("T")
    mu = stat_sb[:, 0 * NTt:1 * NTt]
    musq = stat_sb[:, 1 * NTt:2 * NTt]
    var = stat_sb[:, 2 * NTt:3 * NTt]
    rstd = stat_sb[:, 3 * NTt:4 * NTt]
    m_mu = ctx.scalar.activation(out=mu, in_=ps_s1[:, 0:NTt], func=AF.Copy,
                                 scale=1.0 / DM, wait=stats_mm + dep)
    m_musq = ctx.scalar.activation(out=musq, in_=mu, func=AF.Square,
                                   wait=[m_mu])
    m_var = ctx.vector.scalar_tensor_tensor(var, ps_s2[:, 0:NTt], 1.0 / DM,
                                            musq, ALU.mult, ALU.subtract,
                                            wait=[m_musq] + stats_mm)
    m_ln = ctx.scalar.activation(out=var, in_=var, func=AF.Ln,
                                 bias=ctx.epscol(eps), wait=[m_var])
    ctx.scalar.activation(out=rstd, in_=var, func=AF.Exp, scale=-0.5,
                          wait=[m_ln])
    return mu, rstd


def emit_mm_stage(ctx, tag, w_ap, scale_sb, scale_col0, src_slab, dst_slab,
                  KC, Q, act, extra=None):
    NT = 512
    NGRP = 2048
    ngroups = NTOK // NGRP
    ntiles = NGRP // NT

    w_sb = [ctx.sbuf(f"{tag}w{i}", (128, KC * 128), BF16) for i in range(2)]
    src_sb = ctx.sbuf(f"{tag}src", (128, KC * NGRP), BF16)
    t1_sb = [ctx.sbuf(f"{tag}t1_{i}", (128, NGRP), BF16) for i in range(2)] \
        if extra is not None else None
    dst_sb = [ctx.sbuf(f"{tag}dst{i}", (128, NGRP), BF16) for i in range(2)]
    ps = [ctx.psum(f"{tag}ps{i}", (128, NT), F32) for i in range(2)]

    mm_after_user = []
    ev_after_user = []
    st_after_user = []
    ev_after_tile = []
    mm_after_group = []

    for g in range(ngroups):
        n0 = g * NGRP
        w = mm_after_group[g - 1] if g >= 1 else None
        for k in range(KC):
            ctx.sync.dma_start(out=src_sb[:, k * NGRP:(k + 1) * NGRP],
                               in_=src_slab(k, n0, NGRP),
                               wait=(w if k == 0 else None))
        sl_ready = ctx.snap("Y")

        for m in range(Q):
            u = g * Q + m
            wslot = m % 2
            w = mm_after_user[u - 2] if u >= 2 else None
            ctx.gpsimd.dma_start(out=w_sb[wslot][:, :], in_=w_ap[m], wait=w)
            wl_ready = ctx.snap("G")

            if extra is not None:
                w = ev_after_user[u - 2] if u >= 2 else None
                ctx.sync.dma_start(out=t1_sb[m % 2][:, :],
                                   in_=extra["t1"](m, n0, NGRP), wait=w)
            sl_ready2 = ctx.snap("Y")

            for ni in range(ntiles):
                pslot = (u * ntiles + ni) % 2
                tile_idx = u * ntiles + ni
                for k in range(KC):
                    w = None
                    if k == 0:
                        w = wl_ready + sl_ready2
                        if tile_idx >= 2:
                            w = w + [ev_after_tile[tile_idx - 2]]
                    ctx.tensor.matmul(
                        ps[pslot][:, :],
                        w_sb[wslot][:, k * 128:(k + 1) * 128],
                        src_sb[:, k * NGRP + ni * NT: k * NGRP + (ni + 1) * NT],
                        start=(k == 0), stop=(k == KC - 1),
                        wait=w, tinc=(k == KC - 1))
                mm_now = ctx.snap("T")
                dcols = dst_sb[m % 2][:, ni * NT:(ni + 1) * NT]
                wlist = list(mm_now)
                if ni == 0 and u >= 2:
                    wlist.append(st_after_user[u - 2])
                if extra is None:
                    mk = ctx.scalar.activation(
                        out=dcols, in_=ps[pslot][:, :], func=act,
                        scale=scale_sb[:, scale_col0 + m:scale_col0 + m + 1],
                        wait=wlist)
                else:
                    mk = ctx.vector.scalar_tensor_tensor(
                        dcols, ps[pslot][:, :],
                        scale_sb[:, scale_col0 + m:scale_col0 + m + 1],
                        t1_sb[m % 2][:, ni * NT:(ni + 1) * NT],
                        ALU.mult, ALU.mult, wait=wlist)
                ev_after_tile.append(mk)
            mm_after_user.append(ctx.snap("T"))
            ev_after_user.append([ev_after_tile[-1]])
            mk = ctx.sync.dma_start(out=dst_slab(m, n0, NGRP),
                                    in_=dst_sb[m % 2][:, :],
                                    wait=[ev_after_tile[-1]])
            st_after_user.append(mk)
        mm_after_group.append(ctx.snap("T"))
    ctx.sync.wait_ge("Y", ctx.c("Y"))


def emit_phase_D(ctx, W, stream_in, actA, actB, y_fm, consts):
    sc = ctx.sbuf("Dscales", (128, 160), F32)
    ctx.sync.dma_start(out=sc[:, 0:32], in_=W["w1_s2d"])
    ctx.sync.dma_start(out=sc[:, 32:64], in_=W["w2_s2d"])
    ctx.sync.dma_start(out=sc[:, 64:96], in_=W["h1_s2d"])
    ctx.sync.dma_start(out=sc[:, 96:128], in_=W["h2_s2d"])
    ctx.sync.dma_start(out=sc[:, 128:136], in_=W["w3_s2d"])
    ctx.sync.wait_ge("Y", ctx.c("Y"))
    ctx.flush("D_scales")

    def stream_slab(k, n0, nn):
        s, t0 = n0 // T, n0 % T
        return stream_in[s, k, :, t0:t0 + nn]

    def mk_slab(dram):
        return lambda m, n0, nn: dram[m, :, n0:n0 + nn]

    stages = [
        ("D_w1", lambda: emit_mm_stage(ctx, "Dw1", W["w1_T"], sc, 0, stream_slab,
                                       mk_slab(actA), KC=NCH, Q=HCH, act=AF.Silu)),
        ("D_w2", lambda: emit_mm_stage(ctx, "Dw2", W["w2_T"], sc, 32, stream_slab,
                                       mk_slab(actB), KC=NCH, Q=HCH, act=None,
                                       extra={"t1": mk_slab(actA)})),
        ("D_h1", lambda: emit_mm_stage(ctx, "Dh1", W["h1_T"], sc, 64, mk_slab(actB),
                                       mk_slab(actA), KC=HCH, Q=HCH, act=AF.Gelu)),
        ("D_h2", lambda: emit_mm_stage(ctx, "Dh2", W["h2_T"], sc, 96, mk_slab(actA),
                                       mk_slab(actB), KC=HCH, Q=HCH, act=AF.Gelu)),
        ("D_tail", lambda: emit_phase_D_tail(ctx, W, stream_in, actB, y_fm, sc, consts)),
    ]
    for name, fn in stages:
        ctx.push_scope()
        fn()
        ctx.flush(name)
        ctx.pop_scope()


def emit_phase_D_tail(ctx, W, stream_in, actB, y_fm, sc, consts):
    NT = 256
    KC, Q = HCH, NCH
    w_ap = W["w3_T"]

    w_sb = ctx.sbuf("Dt_w", (128, Q * KC * 128), BF16)
    for m in range(Q):
        ctx.gpsimd.dma_start(out=w_sb[:, m * KC * 128:(m + 1) * KC * 128],
                             in_=w_ap[m])
    wl_ready = ctx.snap("G")

    src_sb = [ctx.sbuf(f"Dt_src{i}", (128, KC * NT), BF16) for i in range(2)]
    x_sb = [ctx.sbuf(f"Dt_x{i}", (128, NCH * NT), BF16) for i in range(2)]
    u_sb = ctx.sbuf("Dt_u", (128, NCH * NT), F32)
    ubf_sb = ctx.sbuf("Dt_ubf", (128, NCH * NT), BF16)
    usq_sb = ctx.sbuf("Dt_usq", (128, NCH * NT), BF16)
    stat_sb = ctx.sbuf("Dt_stat", (128, 4 * NT), F32)
    out_sb = [ctx.sbuf(f"Dt_out{i}", (128, NCH * NT), F32) for i in range(2)]
    ones_sb = ctx.sbuf("Dt_ones", (128, 128), BF16)
    ctx.sync.dma_start(out=ones_sb[:, :], in_=consts["ones128"])

    ps = [ctx.psum(f"Dt_ps{i}", (128, NT), F32) for i in range(2)]
    ps_s1 = ctx.psum("Dt_s1", (128, NT), F32)
    ps_s2 = ctx.psum("Dt_s2", (128, NT), F32)

    ntiles = NTOK // NT
    mm_after_tile = []
    norm_done_tile = []
    st_after_tile = []
    stats_mm_prev = None

    for ni in range(ntiles):
        n0 = ni * NT
        s, t0 = n0 // T, n0 % T
        slot = ni % 2
        w = None
        if ni >= 2:
            w = mm_after_tile[ni - 2] + norm_done_tile[ni - 2]
        for k in range(KC):
            ctx.sync.dma_start(out=src_sb[slot][:, k * NT:(k + 1) * NT],
                               in_=actB[k, :, n0:n0 + NT],
                               wait=(w if k == 0 else None))
        for c in range(NCH):
            ctx.sync.dma_start(out=x_sb[slot][:, c * NT:(c + 1) * NT],
                               in_=stream_in[s, c, :, t0:t0 + NT])
        sl_ready = ctx.snap("Y")

        tile_ev_u = []
        for m in range(Q):
            pslot = m % 2
            for k in range(KC):
                w = None
                if k == 0:
                    w = sl_ready + wl_ready
                    if ni >= 1 and m == 0:
                        w = w + norm_done_tile[ni - 1]
                    if m >= 2:
                        w = w + [tile_ev_u[m - 2]]
                ctx.tensor.matmul(ps[pslot][:, :],
                                  w_sb[:, (m * KC + k) * 128:(m * KC + k + 1) * 128],
                                  src_sb[slot][:, k * NT:(k + 1) * NT],
                                  start=(k == 0), stop=(k == KC - 1),
                                  wait=w, tinc=(k == KC - 1))
            mm_now = ctx.snap("T")
            mk = ctx.vector.scalar_tensor_tensor(
                u_sb[:, m * NT:(m + 1) * NT], ps[pslot][:, :],
                sc[:, 128 + m:128 + m + 1], x_sb[slot][:, m * NT:(m + 1) * NT],
                ALU.mult, ALU.add, wait=mm_now)
            tile_ev_u.append(mk)
        mm_after_tile.append(ctx.snap("T"))

        dep_u = [tile_ev_u[-1]]
        cm = []
        for c in range(NCH):
            w = list(dep_u)
            if stats_mm_prev and c == 0:
                w += stats_mm_prev
            cm.append(ctx.vector.tensor_copy(
                ubf_sb[:, c * NT:(c + 1) * NT], u_sb[:, c * NT:(c + 1) * NT],
                wait=(w if c == 0 else dep_u)))
            cm.append(ctx.scalar.activation(
                out=usq_sb[:, c * NT:(c + 1) * NT],
                in_=u_sb[:, c * NT:(c + 1) * NT], func=AF.Square,
                wait=(w if c == 0 else dep_u)))
        dep = [cm[-2], cm[-1]]
        mu, rstd = emit_ln_stats(ctx, NT, LN_EPS_OUTER,
                                 lambda c: ubf_sb[:, c * NT:(c + 1) * NT],
                                 lambda c: usq_sb[:, c * NT:(c + 1) * NT],
                                 ones_sb, ps_s1, ps_s2, stat_sb, dep)
        stats_mm_prev = ctx.snap("T")
        rstd_done = ctx.snap("S")
        oslot = ni % 2
        last = None
        for c in range(NCH):
            ucols = u_sb[:, c * NT:(c + 1) * NT]
            w = list(rstd_done) if c == 0 else []
            if ni >= 2 and c == 0:
                w.append(st_after_tile[ni - 2])
            mk = ctx.vector.tensor_sub(ucols, ucols, mu, wait=(w or None))
            last = ctx.vector.tensor_mul(out_sb[oslot][:, c * NT:(c + 1) * NT],
                                         ucols, rstd, wait=[mk])
        norm_done_tile.append([last])
        stmk = None
        for c in range(NCH):
            stmk = ctx.sync.dma_start(out=y_fm[s, c, :, t0:t0 + NT],
                                      in_=out_sb[oslot][:, c * NT:(c + 1) * NT],
                                      wait=([last] if c == 0 else None))
        st_after_tile.append(stmk)
    ctx.sync.wait_ge("Y", ctx.c("Y"))


B_BIG = 30000.0


def emit_phase_B(ctx, W, stream_in, stream_out, consts, dbg=None):
    TM = 1024
    NS = TM // 512
    npass = NTOK // TM
    NE = 8

    gate_w = ctx.sbuf("Bgatew", (128, 64), BF16)
    identf = ctx.sbuf("Bidf", (128, 128), F32)
    ones_sb = ctx.sbuf("Bones", (128, 128), BF16)
    sel8 = ctx.sbuf("Bsel8", (8, 1024), BF16)
    gb2d = ctx.sbuf("Bgb", (128, 448), F32)
    gamma2d = gb2d[:, 0:64]
    beta2d = gb2d[:, 64:128]
    b1_2d = gb2d[:, 128:384]
    b2_2d = gb2d[:, 384:448]
    ctx.gpsimd.dma_start(out=gate_w[:, :], in_=W["gate_w"])
    ctx.sync.dma_start(out=identf[:, :], in_=consts["ident128f"])
    ctx.sync.dma_start(out=ones_sb[:, :], in_=consts["ones128"])
    ctx.sync.dma_start(out=sel8[:, :], in_=consts["sel8"])
    ctx.sync.dma_start(out=gb2d[:, 0:64], in_=W["gamma2d"])
    ctx.sync.dma_start(out=gb2d[:, 64:128], in_=W["beta2d"])
    ctx.sync.dma_start(out=gb2d[:, 128:384], in_=W["b1_2d"])
    ctx.sync.dma_start(out=gb2d[:, 384:448], in_=W["b2_2d"])
    consts_ready = ctx.snap("Y", "G")

    x_sb = ctx.sbuf("Bx", (128, NCH * TM), BF16)
    sq_sb = ctx.sbuf("Bsq", (128, NCH * 512), BF16)
    xhat_sb = ctx.sbuf("Bxhat", (128, NCH * TM), BF16)
    xh_scr = ctx.sbuf("Bxhscr", (128, 512), F32)
    stat_sb = ctx.sbuf("Bstat", (128, 4 * 512), F32)
    nrm_sb = ctx.sbuf("Bnrm", (128, NCH * TM), BF16)
    h_sb = ctx.sbuf("Bh", (128, HCH * TM), BF16)
    u_sb = h_sb.bitcast(F32)
    y_sb = ctx.sbuf("By", (128, NCH * TM), F32)
    tmp_sb = ctx.sbuf("Btmp", (128, 512), F32)
    cwb_sb = ctx.sbuf("Bcwb", (128, TM), BF16)
    scores_sb = ctx.sbuf("Bscores", (8, TM), F32)
    s_tm = ctx.sbuf("Bstm", (128, 64), F32)
    g_scr = ctx.sbuf("Bgscr", (128, 48), F32)
    cw_tm = ctx.sbuf("Bcwtm", (128, 64), F32)
    cw_fm = ctx.sbuf("Bcwfm", (8, TM), BF16)
    w1slab = [ctx.sbuf(f"Bw1s{i}", (128, 1024), BF16) for i in range(2)]
    w2slab = [ctx.sbuf(f"Bw2s{i}", (128, 4096), BF16) for i in range(2)]

    ps_h = [ctx.psum(f"Bpsh{i}", (128, 512), F32) for i in range(2)]
    ps_y = [ctx.psum(f"Bpsy{i}", (128, 512), F32) for i in range(2)]
    ps_cw = ctx.psum("Bpscw", (128, 1024), F32)
    ps_s1 = ctx.psum("Bps1", (128, 512), F32)
    ps_s2 = ctx.psum("Bps2", (128, 512), F32)

    free_h = [None, None]
    free_y = [None, None]
    free_cw = None
    w1_users = []
    w2_users = []
    u_read_x = None
    st_prev = None
    stats_mm_prev = None
    evac_prev_e = None

    for p in range(npass):
        s, t0 = p // (T // TM), (p % (T // TM)) * TM

        for c in range(NCH):
            w = list(consts_ready)
            if p >= 1 and c == 0:
                w.append(u_read_x)
            ctx.sync.dma_start(out=x_sb[:, c * TM:(c + 1) * TM],
                               in_=stream_in[s, c, :, t0:t0 + TM],
                               wait=(w if c == 0 else None))
        x_ready = ctx.snap("Y")

        for ns in range(NS):
            for k in range(NCH):
                w = None
                if k == 0:
                    w = x_ready + consts_ready
                    if free_y[ns]:
                        w = w + [free_y[ns]]
                ctx.tensor.matmul(ps_y[ns][0:8, :], gate_w[:, k * 8:(k + 1) * 8],
                                  x_sb[:, k * TM + ns * 512: k * TM + (ns + 1) * 512],
                                  start=(k == 0), stop=(k == NCH - 1),
                                  wait=w, tinc=(k == NCH - 1))
            mm_now = ctx.snap("T")
            free_y[ns] = ctx.scalar.activation(
                out=scores_sb[0:8, ns * 512:(ns + 1) * 512],
                in_=ps_y[ns][0:8, :], func=AF.Copy, wait=mm_now)

        for tau in range(8):
            w = None
            if tau == 0:
                w = [free_y[0], free_y[1]]
                if free_h[0]:
                    w.append(free_h[0])
            ctx.tensor.transpose(ps_h[0][:, tau * 8:(tau + 1) * 8],
                                 scores_sb[0:8, tau * 128:(tau + 1) * 128],
                                 identf[0:8, 0:8], wait=w, tinc=(tau == 7))
        mm_now = ctx.snap("T")
        stm_mk = ctx.vector.tensor_copy(s_tm[:, :], ps_h[0][:, 0:64], wait=mm_now)
        free_h[0] = stm_mk

        m1n = g_scr[:, 0:1]
        m2n = g_scr[:, 1:2]
        den = g_scr[:, 2:3]
        den2 = g_scr[:, 4:5]
        rden = g_scr[:, 3:4]
        sh = g_scr[:, 8:16]
        gmask = g_scr[:, 16:24]
        gm = g_scr[:, 24:32]
        gem = g_scr[:, 32:40]
        gex = g_scr[:, 40:48]
        mk = stm_mk
        for tau in range(8):
            S = s_tm[:, tau * 8:(tau + 1) * 8]
            mk1 = ctx.vector.tensor_reduce(m1n, S, AX.X, ALU.max, negate=True,
                                           wait=[mk])
            mk_gex = ctx.scalar.activation(out=gex, in_=S, func=AF.Exp,
                                           bias=m1n, wait=[mk1])
            mk_sh = ctx.scalar.activation(out=sh, in_=S, func=AF.Identity,
                                          bias=m1n, wait=[mk1])
            mk = ctx.vector.tensor_single_scalar(gmask, sh, 0.0, ALU.is_ge,
                                                 wait=[mk_sh])
            mk = ctx.vector.scalar_tensor_tensor(gm, gmask, -B_BIG, sh,
                                                 ALU.mult, ALU.add, wait=[mk])
            mk2 = ctx.vector.tensor_reduce(m2n, gm, AX.X, ALU.max, negate=True,
                                           wait=[mk])
            mk_gd = ctx.scalar.activation(out=gm, in_=sh, func=AF.Identity,
                                          bias=m2n, wait=[mk2, mk_gex])
            mk = ctx.vector.tensor_single_scalar(gmask, gm, 0.0, ALU.is_ge,
                                                 wait=[mk_gd])
            mk = ctx.vector.tensor_mul(gem, gex, gmask, wait=[mk])
            mkd = ctx.vector.tensor_reduce(den, gem, AX.X, ALU.add, wait=[mk])
            mk_d2 = ctx.scalar.activation(out=den2, in_=den, func=AF.Identity,
                                          wait=[mkd])
            mk_rd = ctx.vector.reciprocal(rden, den2, wait=[mk_d2])
            mk = ctx.scalar.activation(out=cw_tm[:, tau * 8:(tau + 1) * 8],
                                       in_=gem, func=AF.Identity, scale=rden,
                                       wait=[mk_rd, mkd])
        cw_done = mk
        if dbg is not None:
            ctx.sync.dma_start(out=dbg["cwtm"][p], in_=cw_tm[:, :], wait=[cw_done])
            ctx.sync.dma_start(out=dbg["stm"][p], in_=s_tm[:, :], wait=[cw_done])

        for tau in range(8):
            w = None
            if tau == 0:
                w = [cw_done]
                if free_cw:
                    w.append(free_cw)
            ctx.tensor.transpose(ps_cw[0:8, tau * 128:(tau + 1) * 128],
                                 cw_tm[:, tau * 8:(tau + 1) * 8],
                                 identf[:, :], wait=w, tinc=(tau == 7))
        mm_now = ctx.snap("T")
        free_cw = ctx.vector.tensor_copy(cw_fm[0:8, :], ps_cw[0:8, :], wait=mm_now)
        if dbg is not None:
            ctx.sync.dma_start(out=dbg["cw"][p], in_=cw_fm[0:8, :], wait=[free_cw])

        xhat_last = None
        for half in range(2):
            sq_last = None
            for c in range(NCH):
                xc = x_sb[:, c * TM + half * 512: c * TM + (half + 1) * 512]
                w = None
                if c == 0:
                    w = list(x_ready) + (stats_mm_prev or [])
                    if half == 0 and p >= 1 and st_prev:
                        w.append(st_prev)
                sq_last = ctx.scalar.activation(
                    out=sq_sb[:, c * 512:(c + 1) * 512], in_=xc, func=AF.Square,
                    wait=w)
            dep = [sq_last] + x_ready + (stats_mm_prev or [])
            mu, rstd = emit_ln_stats(
                ctx, 512, LN_EPS_B,
                lambda c: x_sb[:, c * TM + half * 512: c * TM + (half + 1) * 512],
                lambda c: sq_sb[:, c * 512:(c + 1) * 512],
                ones_sb, ps_s1, ps_s2, stat_sb, dep)
            stats_mm_prev = ctx.snap("T")
            rstd_done = ctx.snap("S")
            for c in range(NCH):
                xc = x_sb[:, c * TM + half * 512: c * TM + (half + 1) * 512]
                w = list(rstd_done) if c == 0 else None
                if c == 0 and p >= 1 and st_prev:
                    w.append(st_prev)
                mk = ctx.vector.tensor_sub(xh_scr[:, :], xc, mu, wait=w)
                xhat_last = ctx.vector.tensor_mul(
                    xhat_sb[:, c * TM + half * 512: c * TM + (half + 1) * 512],
                    xh_scr[:, :], rstd, wait=[mk])
        xhat_done = [xhat_last]
        if dbg is not None:
            for c in range(NCH):
                ctx.sync.dma_start(out=dbg["xhat"][p, c],
                                   in_=xhat_sb[:, c * TM:(c + 1) * TM],
                                   wait=(xhat_done if c == 0 else None))

        mm_w1_done = None
        y_marks = [None] * (NCH * NS)
        for e in range(NE):
            for c in range(NCH):
                w = None
                if c == 0:
                    w = list(xhat_done)
                    if mm_w1_done:
                        w += mm_w1_done
                    if stats_mm_prev:
                        w += stats_mm_prev
                ctx.vector.tensor_scalar(nrm_sb[:, c * TM:(c + 1) * TM],
                                         xhat_sb[:, c * TM:(c + 1) * TM],
                                         gamma2d[:, e * 8 + c:e * 8 + c + 1],
                                         beta2d[:, e * 8 + c:e * 8 + c + 1],
                                         ALU.mult, ALU.add, wait=w)
            nrm_done = ctx.snap("V")

            for ns in range(NS):
                w = None
                if ns == 0:
                    w = [free_cw]
                    if evac_prev_e:
                        w.append(evac_prev_e)
                ctx.tensor.matmul(ps_cw[:, ns * 512:(ns + 1) * 512],
                                  sel8[0:8, e * 128:(e + 1) * 128],
                                  cw_fm[0:8, ns * 512:(ns + 1) * 512],
                                  start=True, stop=True, wait=w, tinc=(ns == NS - 1))
            mm_now = ctx.snap("T")
            free_cw = ctx.scalar.activation(out=cwb_sb[:, :], in_=ps_cw[:, :],
                                            func=AF.Copy, wait=mm_now)

            for m in range(HCH):
                slot = m % 2
                iu = len(w1_users)
                w = w1_users[iu - 2] if iu >= 2 else None
                ctx.gpsimd.dma_start(out=w1slab[slot][:, :], in_=W["w1T"][e, m],
                                     wait=w)
                wl_ready = ctx.snap("G")
                for ns in range(NS):
                    for k in range(NCH):
                        w = None
                        if k == 0:
                            w = wl_ready + nrm_done
                            if free_h[ns]:
                                w = w + [free_h[ns]]
                        ctx.tensor.matmul(
                            ps_h[ns][:, :], w1slab[slot][:, k * 128:(k + 1) * 128],
                            nrm_sb[:, k * TM + ns * 512: k * TM + (ns + 1) * 512],
                            start=(k == 0), stop=(k == NCH - 1),
                            wait=w, tinc=(k == NCH - 1))
                    mm_now = ctx.snap("T")
                    free_h[ns] = ctx.scalar.activation(
                        out=h_sb[:, m * TM + ns * 512: m * TM + (ns + 1) * 512],
                        in_=ps_h[ns][:, :], func=AF.Relu,
                        bias=b1_2d[:, e * HCH + m:e * HCH + m + 1], wait=mm_now)
                w1_users.append(ctx.snap("T"))
            mm_w1_done = ctx.snap("T")
            h_done = ctx.snap("S")

            for j in range(NCH):
                slot = j % 2
                iu = len(w2_users)
                w = w2_users[iu - 2] if iu >= 2 else None
                ctx.gpsimd.dma_start(out=w2slab[slot][:, :], in_=W["w2T"][e, j],
                                     wait=w)
                wl_ready = ctx.snap("G")
                for ns in range(NS):
                    for k in range(HCH):
                        w = None
                        if k == 0:
                            w = wl_ready + h_done
                            if free_y[ns]:
                                w = w + [free_y[ns]]
                        ctx.tensor.matmul(
                            ps_y[ns][:, :], w2slab[slot][:, k * 128:(k + 1) * 128],
                            h_sb[:, k * TM + ns * 512: k * TM + (ns + 1) * 512],
                            start=(k == 0), stop=(k == HCH - 1),
                            wait=w, tinc=(k == HCH - 1))
                    mm_now = ctx.snap("T")
                    ycols = y_sb[:, j * TM + ns * 512: j * TM + (ns + 1) * 512]
                    b2col = b2_2d[:, e * NCH + j:e * NCH + j + 1]
                    cwcols = cwb_sb[:, ns * 512:(ns + 1) * 512]
                    if e == 0:
                        mk = ctx.vector.scalar_tensor_tensor(
                            ycols, ps_y[ns][:, :], b2col, cwcols,
                            ALU.add, ALU.mult, wait=mm_now + [free_cw])
                    else:
                        mk = ctx.vector.scalar_tensor_tensor(
                            tmp_sb[:, :], ps_y[ns][:, :], b2col, cwcols,
                            ALU.add, ALU.mult,
                            wait=mm_now + [free_cw, y_marks[j * NS + ns]])
                        mk = ctx.vector.tensor_add(ycols, ycols, tmp_sb[:, :],
                                                   wait=[mk])
                    free_y[ns] = mk
                    y_marks[j * NS + ns] = mk
                w2_users.append(ctx.snap("T"))
            evac_prev_e = free_y[1]
        mm_all = ctx.snap("T")
        if dbg is not None:
            for c in range(NCH):
                ctx.sync.dma_start(out=dbg["y"][p, c],
                                   in_=y_sb[:, c * TM:(c + 1) * TM],
                                   wait=([free_y[0], free_y[1]] if c == 0 else None))

        u_last = None
        for c in range(NCH):
            w = [y_marks[c * NS], y_marks[c * NS + 1]]
            if c == 0:
                w += mm_all
            u_last = ctx.vector.scalar_tensor_tensor(
                u_sb[:, c * TM:(c + 1) * TM], y_sb[:, c * TM:(c + 1) * TM],
                0.5, x_sb[:, c * TM:(c + 1) * TM], ALU.mult, ALU.add, wait=w)
        u_read_x = u_last

        for half in range(2):
            cm = []
            for c in range(NCH):
                ucols = u_sb[:, c * TM + half * 512: c * TM + (half + 1) * 512]
                w = [u_read_x]
                if c == 0:
                    w = w + stats_mm_prev
                cm.append(ctx.vector.tensor_copy(
                    nrm_sb[:, c * 512:(c + 1) * 512], ucols,
                    wait=(w if c == 0 else [u_read_x])))
                cm.append(ctx.scalar.activation(
                    out=sq_sb[:, c * 512:(c + 1) * 512], in_=ucols,
                    func=AF.Square, wait=(w if c == 0 else [u_read_x])))
            dep = [cm[-2], cm[-1]] + stats_mm_prev
            mu, rstd = emit_ln_stats(
                ctx, 512, LN_EPS_OUTER,
                lambda c: nrm_sb[:, c * 512:(c + 1) * 512],
                lambda c: sq_sb[:, c * 512:(c + 1) * 512],
                ones_sb, ps_s1, ps_s2, stat_sb, dep)
            stats_mm_prev = ctx.snap("T")
            rstd_done = ctx.snap("S")
            for c in range(NCH):
                ucols = u_sb[:, c * TM + half * 512: c * TM + (half + 1) * 512]
                ocols = xhat_sb[:, c * TM + half * 512: c * TM + (half + 1) * 512]
                w = list(rstd_done) if c == 0 else None
                mk = ctx.vector.tensor_sub(ucols, ucols, mu, wait=w)
                mk = ctx.vector.tensor_mul(ocols, ucols, rstd, wait=[mk])
                st_prev = ctx.sync.dma_start(
                    out=stream_out[s, c, :, t0 + half * 512: t0 + (half + 1) * 512],
                    in_=ocols, wait=[mk])
    ctx.sync.wait_ge("Y", ctx.c("Y"))


def prep_B(inputs):
    W = {}
    W1 = inputs["L0_ffn_W1"]
    W2 = inputs["L0_ffn_W2"]
    W["w1T"] = np.stack([tiles_lhsT(W1[e]) for e in range(8)])
    W["w2T"] = np.stack([tiles_lhsT(W2[e]) for e in range(8)])
    Wg = inputs["L0_ffn_Wg"]
    W["gate_w"] = np.ascontiguousarray(
        Wg.reshape(8, 128, 8).transpose(1, 0, 2).reshape(128, 64))
    W["gamma2d"] = np.ascontiguousarray(
        inputs["L0_ffn_gamma"].reshape(8, 8, 128).transpose(2, 0, 1).reshape(128, 64))
    W["beta2d"] = np.ascontiguousarray(
        inputs["L0_ffn_beta"].reshape(8, 8, 128).transpose(2, 0, 1).reshape(128, 64))
    W["b1_2d"] = np.ascontiguousarray(
        inputs["L0_ffn_b1"].reshape(8, 32, 128).transpose(2, 0, 1).reshape(128, 256))
    W["b2_2d"] = np.ascontiguousarray(
        inputs["L0_ffn_b2"].reshape(8, 8, 128).transpose(2, 0, 1).reshape(128, 64))
    return W


def declare_B_weights(nc):
    aps = {}
    aps["w1T"] = nc.dram_tensor("Bw1T", [8, 32, 128, 1024], F32, kind="ExternalInput").ap()
    aps["w2T"] = nc.dram_tensor("Bw2T", [8, 8, 128, 4096], F32, kind="ExternalInput").ap()
    aps["gate_w"] = nc.dram_tensor("Bgate_w", [128, 64], F32, kind="ExternalInput").ap()
    for n, cols in (("gamma2d", 64), ("beta2d", 64), ("b1_2d", 256), ("b2_2d", 64)):
        aps[n] = nc.dram_tensor("B" + n, [128, cols], F32, kind="ExternalInput").ap()
    return aps


def build_B_standalone(debug=False):
    nc = bass.Bass("TRN2", target_bir_lowering=False)
    stream_in = nc.dram_tensor("stream_in", [SEQ, NCH, 128, T], BF16,
                               kind="ExternalInput").ap()
    stream_out = nc.dram_tensor("stream_out", [SEQ, NCH, 128, T], BF16,
                                kind="ExternalOutput").ap()
    dbg = None
    if debug:
        dbg = {
            "cwtm": nc.dram_tensor("dbg_cwtm", [8, 128, 64], F32, kind="ExternalOutput").ap(),
            "stm": nc.dram_tensor("dbg_stm", [8, 128, 64], F32, kind="ExternalOutput").ap(),
            "cw": nc.dram_tensor("dbg_cw", [8, 8, 1024], BF16, kind="ExternalOutput").ap(),
            "xhat": nc.dram_tensor("dbg_xhat", [8, NCH, 128, 1024], BF16, kind="ExternalOutput").ap(),
            "y": nc.dram_tensor("dbg_y", [8, NCH, 128, 1024], F32, kind="ExternalOutput").ap(),
        }
    with contextlib.ExitStack() as es:
        ctx = Ctx(nc, es)
        ctx.init_epscols()
        W = declare_B_weights(nc)
        consts = declare_consts(nc)
        ctx.push_scope()
        emit_phase_B(ctx, W, stream_in, stream_out, consts, dbg=dbg)
        ctx.flush("B")
        ctx.pop_scope()
    return nc


def run_B(inputs, stream_bf16_percore, debug=False):
    nc = build_B_standalone(debug)
    W = dmap_prefix(prep_B(inputs), "B")
    in_maps = []
    for c in range(NCORES):
        m = dict(W)
        m.update(consts_np())
        m["stream_in"] = stream_bf16_percore[c]
        in_maps.append(m)
    res = run_bass_kernel_spmd(nc, in_maps, core_ids=list(range(NCORES)))
    if debug:
        return res.results
    return [r["stream_out"] for r in res.results]



C_ES = 64.0 ** (-0.25)
C_SECTIONS = {"p1", "p2", "p3"}
C_WAVES = (0, 1, 2, 3)
C_INJECT = False
NBUK = T // 64
NTT = T // 128


def emit_phase_C_consts(ctx, W, consts):
    H = {}
    H["selq16"] = ctx.sbuf("Cselq", (128, 128), BF16)
    H["selb16"] = ctx.sbuf("Cselb", (16, 1024), BF16)
    H["b2ind"] = ctx.sbuf("Cb2i", (128, 2), BF16)
    H["ones"] = ctx.sbuf("Cones", (128, 128), BF16)
    H["ones32"] = ctx.sbuf("Cones32", (128, 32), F32)
    H["bo2d"] = ctx.sbuf("Cbo", (128, 8), F32)
    ctx.sync.dma_start(out=H["selq16"][:, :], in_=consts["selq16"])
    ctx.sync.dma_start(out=H["selb16"][:, :], in_=consts["selb16"])
    ctx.sync.dma_start(out=H["b2ind"][:, :], in_=consts["b2ind"])
    ctx.sync.dma_start(out=H["ones"][:, :], in_=consts["ones128"])
    ctx.sync.dma_start(out=H["bo2d"][:, :], in_=W["bo2d"])
    ctx.vector.memset(H["ones32"][:, :], 1.0)
    H["ready"] = ctx.snap("Y", "V")
    return H


def emit_phase_C_seq(ctx, W, H, stream_in, stream_out, s, dbg=None):
    attn_sb = ctx.sbuf("Cattn", (128, NCH * T), BF16)
    ksum_sb = ctx.sbuf("Cks", (128, 256), F32)
    kcum_sb = ctx.sbuf("Ckc", (128, 256), F32)
    rkc_sb = ctx.sbuf("Crkc", (128, 256), F32)

    ps_a = ctx.psum("Cpsa", (128, 512), F32)
    ps_b = ctx.psum("Cpsb", (128, 512), F32)

    ctx.push_scope()
    qe_sb = ctx.sbuf("Cqe", (128, NCH * T), BF16)
    fk_sb = ctx.sbuf("Cfk", (128, NTT * 1024), BF16)
    v_sb = ctx.sbuf("Cv", (128, NTT * 1024), BF16)

    ctx.push_scope()
    x_sb = ctx.sbuf("Cx", (128, NCH * T), BF16)
    wq_sb = [ctx.sbuf(f"Cwq{i}", (128, 1024), BF16) for i in range(2)]
    wkv_sb = ctx.sbuf("Cwkv", (128, NCH * 1024), BF16)
    rq_sb = ctx.sbuf("Crq", (16, T), BF16)
    rq32_sb = ctx.sbuf("Crq32", (16, 512), F32)
    ks_scr = ctx.sbuf("Ckss", (128, 6 * 512), F32)

    for c in range(NCH):
        ctx.sync.dma_start(out=x_sb[:, c * T:(c + 1) * T],
                           in_=stream_in[s, c, :, 0:T],
                           wait=(H["ready"] if c == 0 else None))
    x_ready = ctx.snap("Y")

    if "p1" in C_SECTIONS:
        evac_prev = [None, None]
        for m in range(NCH):
            slot = m % 2
            w = evac_prev[slot]
            ctx.gpsimd.dma_start(out=wq_sb[slot][:, :], in_=W["wqT"][m],
                                 wait=([w] if w else None))
            wl = ctx.snap("G")
            for ns in range(4):
                pslot = ns % 2
                pp = [ps_a, ps_b][pslot]
                for k in range(NCH):
                    w = None
                    if k == 0:
                        w = wl + x_ready
                        if evac_prev[pslot]:
                            w = w + [evac_prev[pslot]]
                    ctx.tensor.matmul(pp[:, :], wq_sb[slot][:, k * 128:(k + 1) * 128],
                                      x_sb[:, k * T + ns * 512: k * T + (ns + 1) * 512],
                                      start=(k == 0), stop=(k == NCH - 1),
                                      wait=w, tinc=(k == NCH - 1))
                mm_now = ctx.snap("T")
                evac_prev[pslot] = ctx.scalar.activation(
                    out=qe_sb[:, m * T + ns * 512: m * T + (ns + 1) * 512],
                    in_=pp[:, :], func=AF.Exp, scale=C_ES, wait=mm_now)
        qe_done = ctx.snap("S")

        rq_marks = []
        for ns in range(4):
            pslot = ns % 2
            pp = [ps_a, ps_b][pslot]
            for c in range(NCH):
                w = None
                if c == 0:
                    w = list(qe_done)
                    if ns < 2:
                        w += [evac_prev[0], evac_prev[1]]
                    if ns >= 2:
                        w += [rq_marks[ns - 2]]
                ctx.tensor.matmul(pp[0:16, :], H["selq16"][:, c * 16:(c + 1) * 16],
                                  qe_sb[:, c * T + ns * 512: c * T + (ns + 1) * 512],
                                  start=(c == 0), stop=(c == NCH - 1),
                                  wait=w, tinc=(c == NCH - 1))
            mm_now = ctx.snap("T")
            w = list(mm_now)
            if rq_marks:
                w.append(rq_marks[-1])
            mkr = ctx.vector.reciprocal(rq32_sb[0:16, :], pp[0:16, :], wait=w)
            rq_marks.append(ctx.scalar.activation(
                out=rq_sb[0:16, ns * 512:(ns + 1) * 512], in_=rq32_sb[0:16, :],
                func=AF.Identity, wait=[mkr]))

        rb_sb = ctx.sbuf("Crb", (128, 2 * 512), BF16)
        sq_marks = []
        cp_marks = []
        for c in range(NCH):
            for ns in range(4):
                pslot = ns % 2
                pp = [ps_a, ps_b][pslot]
                idx = c * 4 + ns
                w = [rq_marks[ns]]
                if idx >= 2:
                    w.append(cp_marks[idx - 2])
                else:
                    w.append(rq_marks[ns + 2])
                ctx.tensor.matmul(pp[:, :], H["selb16"][0:16, c * 128:(c + 1) * 128],
                                  rq_sb[0:16, ns * 512:(ns + 1) * 512],
                                  start=True, stop=True, wait=w, tinc=True)
                mm_now = ctx.snap("T")
                rbc = rb_sb[:, pslot * 512:(pslot + 1) * 512]
                w2 = list(mm_now)
                if idx >= 2:
                    w2.append(sq_marks[idx - 2])
                cp = ctx.scalar.activation(out=rbc, in_=pp[:, :], func=AF.Identity,
                                           wait=w2)
                cp_marks.append(cp)
                cols = qe_sb[:, c * T + ns * 512: c * T + (ns + 1) * 512]
                sq_marks.append(ctx.vector.tensor_mul(cols, cols, rbc, wait=[cp]))
        sq_done = [sq_marks[-1], sq_marks[-2]]

        if dbg is not None and s == 0:
            ctx.sync.dma_start(out=dbg["rq"], in_=rq_sb[0:16, :], wait=sq_done)
            for c in range(NCH):
                ctx.sync.dma_start(out=dbg["sq"][c], in_=qe_sb[:, c * T:(c + 1) * T],
                                   wait=(sq_done if c == 0 else None))

    else:
        sq_done = list(x_ready)
    if "p2" in C_SECTIONS:
        for which, wsrc, dst in (("k", W["wkT_rhs"], fk_sb), ("v", W["wvT_rhs"], v_sb)):
            wv_gate = ctx.snap("T") if which == "v" else None
            for c in range(NCH):
                ctx.gpsimd.dma_start(out=wkv_sb[:, c * 1024:(c + 1) * 1024],
                                     in_=wsrc[c],
                                     wait=(wv_gate if c == 0 else None))
            wl = ctx.snap("G")
            ev_pr = [None, None]
            for m in range(NTT):
                for osl in range(2):
                    pslot = (m * 2 + osl) % 2
                    pp = [ps_a, ps_b][pslot]
                    for k in range(NCH):
                        w = None
                        if k == 0:
                            w = wl + x_ready
                            if which == "k" and m == 0 and osl == 0:
                                w = w + sq_done
                            if ev_pr[pslot]:
                                w = w + [ev_pr[pslot]]
                        ctx.tensor.matmul(
                            pp[:, :],
                            x_sb[:, k * T + m * 128: k * T + (m + 1) * 128],
                            wkv_sb[:, k * 1024 + osl * 512: k * 1024 + (osl + 1) * 512],
                            start=(k == 0), stop=(k == NCH - 1),
                            wait=w, tinc=(k == NCH - 1))
                    mm_now = ctx.snap("T")
                    dcols = dst[:, m * 1024 + osl * 512: m * 1024 + (osl + 1) * 512]
                    if which == "v":
                        ev_pr[pslot] = ctx.scalar.activation(
                            out=dcols, in_=pp[:, :], func=AF.Identity, wait=mm_now)
                    else:
                        off = pslot * 1536
                        ks = ks_scr[:, off:off + 512]
                        t2 = ks_scr[:, off + 512:off + 1024]
                        t3 = ks_scr[:, off + 1024:off + 1536]
                        a1 = ctx.scalar.activation(out=ks, in_=pp[:, :],
                                                   func=AF.Identity, scale=C_ES,
                                                   wait=mm_now + ([ev_pr[pslot]] if ev_pr[pslot] else []))
                        a2 = ctx.vector.tensor_single_scalar(t2, ks, 0.0, ALU.min,
                                                             wait=[a1])
                        a3 = ctx.scalar.activation(out=t3, in_=t2, func=AF.Exp,
                                                   wait=[a2])
                        a4 = ctx.vector.tensor_single_scalar(t2, ks, 0.0, ALU.max,
                                                             wait=[a1, a3])
                        ev_pr[pslot] = ctx.vector.tensor_add(dcols, t3, t2,
                                                             wait=[a4, a3])
            if which == "k":
                fk_done = [ev_pr[0], ev_pr[1]]
            else:
                v_done = [ev_pr[0], ev_pr[1]]

    else:
        fk_done = list(x_ready)
        v_done = list(x_ready)
    if dbg is not None and s == 0:
        for m in range(NTT):
            ctx.sync.dma_start(out=dbg["fk"][m], in_=fk_sb[:, m * 1024:(m + 1) * 1024],
                               wait=(fk_done + v_done if m == 0 else None))
            ctx.sync.dma_start(out=dbg["v"][m], in_=v_sb[:, m * 1024:(m + 1) * 1024])
    ctx.pop_scope()

    if "p3" in C_SECTIONS:
        for c in range(NCH):
            for m in range(NTT):
                w = None
                if c == 0 and m == 0:
                    w = fk_done + v_done
                ctx.tensor.matmul(ps_a[:, c * 32 + 2 * m: c * 32 + 2 * m + 2],
                                  fk_sb[:, m * 1024 + c * 128: m * 1024 + (c + 1) * 128],
                                  H["b2ind"][:, :], start=True, stop=True,
                                  wait=w, tinc=(c == NCH - 1 and m == NTT - 1))
        mm_now = ctx.snap("T")
        mkk = ctx.scalar.activation(out=ksum_sb[:, :], in_=ps_a[:, 0:256],
                                    func=AF.Identity, wait=mm_now)
        scan_mk = mkk
        for c in range(NCH):
            scan_mk = ctx.vector.tensor_tensor_scan(
                kcum_sb[:, c * 32:(c + 1) * 32], ksum_sb[:, c * 32:(c + 1) * 32],
                H["ones32"][:, 0:32], 0.0, ALU.add, ALU.mult,
                wait=([mkk] if c == 0 else None))
        mke = ctx.vector.tensor_single_scalar(kcum_sb[:, :], kcum_sb[:, :], CTX_EPS,
                                              ALU.add, wait=[scan_mk])
        mkr = ctx.vector.reciprocal(rkc_sb[:, :], kcum_sb[:, :], wait=[mke])
        rkc_done = [mkr]
        if dbg is not None and s == 0:
            ctx.sync.dma_start(out=dbg["rkc"], in_=rkc_sb[:, :], wait=[mkr])


    else:
        rkc_done = list(x_ready)
    if dbg is not None and C_INJECT and s == 0:
        mk = None
        w0 = sq_done + fk_done + v_done + rkc_done
        for c in range(NCH):
            mk = ctx.gpsimd.dma_start(out=qe_sb[:, c * T:(c + 1) * T],
                                      in_=dbg["inj_sq"][c],
                                      wait=(w0 if c == 0 else None))
        for m in range(NTT):
            mk = ctx.gpsimd.dma_start(out=fk_sb[:, m * 1024:(m + 1) * 1024],
                                      in_=dbg["inj_fk"][m])
            mk = ctx.gpsimd.dma_start(out=v_sb[:, m * 1024:(m + 1) * 1024],
                                      in_=dbg["inj_v"][m])
        mk2 = ctx.sync.dma_start(out=rkc_sb[:, :], in_=dbg["inj_rkc"])
        sq_done = [mk, mk2]
        fk_done = [mk]
        v_done = [mk]
        rkc_done = [mk2]
    z_marks = []
    for c in range(NCH):
        z_marks.append(ctx.vector.memset(attn_sb[:, c * T:c * T + 64], 0.0))
    zero_done = [z_marks[-1]]

    ctx.push_scope()
    ctx_ps = [ctx.psum(f"Cctx{i}", (128, 64), F32) for i in range(4)]
    ps_c = ctx.psum("Cpsc", (128, 512), F32)
    ps_d = ctx.psum("Cpsd", (128, 512), F32)
    aps = [ps_a, ps_b, ps_c, ps_d]
    ctxt_sb = ctx.sbuf("Cctxt", (128, 4 * 64), BF16)
    free_attn = [None, None, None, None]
    wave_end = []

    for wave in C_WAVES:
        chunks = [2 * wave, 2 * wave + 1]
        heads = [2 * c + par for c in chunks for par in range(2)]
        ev_ctx = {h: None for h in heads}
        mm_ctx = {h: None for h in heads}
        mm_attn_prev = {h: None for h in heads}
        for u in range(NBUK):
            for hi, h in enumerate(heads):
                c = h // 2
                ci = chunks.index(c)
                par = h % 2
                bank = ctx_ps[ci * 2 + par]
                rowsl = slice(par * 64, par * 64 + 64)
                tcols = slice((u // 2) * 1024 + h * 64, (u // 2) * 1024 + h * 64 + 64)
                trows = slice((u % 2) * 64, (u % 2) * 64 + 64)
                w = []
                if ev_ctx[h]:
                    w.append(ev_ctx[h])
                if u == 0:
                    w = w + fk_done + v_done + zero_done
                    if wave_end:
                        w = w + wave_end
                mm_ctx[h] = ctx.tensor.matmul(
                    bank[rowsl, :],
                    fk_sb[trows, tcols], v_sb[trows, tcols],
                    start=(u == 0), stop=(u == NBUK - 1),
                    wait=(w or None), tinc=True,
                    tile_position=((u % 2) * 64, par * 64))
                if u < NBUK - 1:
                    ev_ctx[h] = ctx.scalar.activation(
                        out=ctxt_sb[rowsl, (ci * 2 + par) * 64:
                                    (ci * 2 + par) * 64 + 64],
                        in_=bank[rowsl, :], func=AF.Identity,
                        scale=rkc_sb[rowsl.start:rowsl.start + 64,
                                     c * 32 + u: c * 32 + u + 1],
                        wait=[mm_ctx[h]] + (rkc_done if u == 0 else []))
                    if dbg is not None and s == 0 and wave == 0 and h == 0 and u == 0:
                        dmk = ctx.sync.dma_start(out=dbg["ctxt0"],
                                                 in_=ctxt_sb[0:64, 0:64],
                                                 wait=[ev_ctx[h]])
                        ev_ctx[h] = dmk
                    au = u + 1
                    aslot = ((au - 1) // 8) % 2
                    pa = aps[ci * 2 + aslot]
                    w = [ev_ctx[h]]
                    if (au - 1) % 8 == 0 and par == 0 and free_attn[ci * 2 + aslot]:
                        w.append(free_attn[ci * 2 + aslot])
                    if hi == 0 and u == 0 and not wave_end:
                        w = w + sq_done + rkc_done
                    mm_attn_prev[h] = ctx.tensor.matmul(
                        pa[rowsl, ((au - 1) % 8) * 64: ((au - 1) % 8) * 64 + 64],
                        ctxt_sb[rowsl, (ci * 2 + par) * 64:
                                (ci * 2 + par) * 64 + 64],
                        qe_sb[rowsl, c * T + au * 64: c * T + (au + 1) * 64],
                        start=True, stop=True, wait=w, tinc=True,
                        tile_position=(par * 64, par * 64))
            if u >= 8 and (u % 8 == 0):
                g = u // 8 - 1
                aslot = g % 2
                for ci, c in enumerate(chunks):
                    pa = aps[ci * 2 + aslot]
                    wv = [mm_attn_prev[2 * c], mm_attn_prev[2 * c + 1]]
                    free_attn[ci * 2 + aslot] = ctx.vector.tensor_copy(
                        attn_sb[:, c * T + (8 * g + 1) * 64: c * T + (8 * g + 9) * 64],
                        pa[:, :], wait=wv)
        g = 3
        aslot = g % 2
        wave_end = []
        for ci, c in enumerate(chunks):
            pa = aps[ci * 2 + aslot]
            wv = [mm_attn_prev[2 * c], mm_attn_prev[2 * c + 1]]
            last = ctx.vector.tensor_copy(
                attn_sb[:, c * T + (8 * g + 1) * 64: c * T + (8 * g + 8) * 64],
                pa[:, 0:448], wait=wv)
            free_attn[ci * 2 + aslot] = last
            wave_end.append(last)
        wave_end = wave_end + [mm_ctx[h] for h in heads]
    ctx.pop_scope()
    if dbg is not None and s == 0:
        for c in range(NCH):
            ctx.sync.dma_start(out=dbg["attn"][c], in_=attn_sb[:, c * T:(c + 1) * T],
                               wait=(ctx.snap("V") if c == 0 else None))
    ctx.pop_scope()
    attn_done = ctx.snap("V", "T")

    ctx.push_scope()
    wo_sb = ctx.sbuf("Cwo", (128, NCH * 1024), BF16)
    u_sb = ctx.sbuf("Cu", (128, NCH * 512), F32)
    xt_sb = ctx.sbuf("Cxt", (128, NCH * 512), BF16)
    ubf_sb = ctx.sbuf("Cubf", (128, NCH * 512), BF16)
    usq_sb = ctx.sbuf("Cusq", (128, NCH * 512), BF16)
    stat_sb = ctx.sbuf("Cstat", (128, 4 * 512), F32)
    out_sb = ctx.sbuf("Cout", (128, NCH * 512), BF16)
    for m in range(NCH):
        ctx.gpsimd.dma_start(out=wo_sb[:, m * 1024:(m + 1) * 1024],
                             in_=W["woT"][m],
                             wait=(attn_done if m == 0 else None))
    wl = ctx.snap("G")
    stats_mm_prev = None
    norm_prev = None
    st_prev = None
    for ns in range(4):
        xw = list(attn_done)
        if st_prev:
            xw = xw + [st_prev]
        for c in range(NCH):
            ctx.sync.dma_start(out=xt_sb[:, c * 512:(c + 1) * 512],
                               in_=stream_in[s, c, :, ns * 512:(ns + 1) * 512],
                               wait=(xw if c == 0 else None))
        xt_ready = ctx.snap("Y")
        u_marks = []
        for m in range(NCH):
            pslot = m % 2
            pp = [ps_a, ps_b][pslot]
            for k in range(NCH):
                w = None
                if k == 0:
                    w = wl + xt_ready + attn_done
                    if m >= 2:
                        w = w + [u_marks[m - 2]]
                    elif norm_prev:
                        w = w + norm_prev
                ctx.tensor.matmul(pp[:, :],
                                  wo_sb[:, m * 1024 + k * 128: m * 1024 + (k + 1) * 128],
                                  attn_sb[:, k * T + ns * 512: k * T + (ns + 1) * 512],
                                  start=(k == 0), stop=(k == NCH - 1),
                                  wait=w, tinc=(k == NCH - 1))
            mm_now = ctx.snap("T")
            u_marks.append(ctx.vector.scalar_tensor_tensor(
                u_sb[:, m * 512:(m + 1) * 512], pp[:, :],
                H["bo2d"][:, m:m + 1], xt_sb[:, m * 512:(m + 1) * 512],
                ALU.add, ALU.add, wait=mm_now))
        dep_u = [u_marks[-1]]
        cm = []
        for c in range(NCH):
            ucols = u_sb[:, c * 512:(c + 1) * 512]
            w = list(dep_u)
            if c == 0 and stats_mm_prev:
                w += stats_mm_prev
            cm.append(ctx.vector.tensor_copy(ubf_sb[:, c * 512:(c + 1) * 512],
                                             ucols, wait=(w if c == 0 else dep_u)))
            cm.append(ctx.scalar.activation(out=usq_sb[:, c * 512:(c + 1) * 512],
                                            in_=ucols, func=AF.Square,
                                            wait=(w if c == 0 else dep_u)))
        dep = [cm[-2], cm[-1]]
        mu, rstd = emit_ln_stats(ctx, 512, LN_EPS_OUTER,
                                 lambda c: ubf_sb[:, c * 512:(c + 1) * 512],
                                 lambda c: usq_sb[:, c * 512:(c + 1) * 512],
                                 H["ones"], ps_a, ps_b, stat_sb, dep)
        stats_mm_prev = ctx.snap("T")
        rstd_done = ctx.snap("S")
        last = None
        for c in range(NCH):
            ucols = u_sb[:, c * 512:(c + 1) * 512]
            w = list(rstd_done) if c == 0 else None
            mk = ctx.vector.tensor_sub(ucols, ucols, mu, wait=w)
            last = ctx.vector.tensor_mul(out_sb[:, c * 512:(c + 1) * 512],
                                         ucols, rstd, wait=[mk])
        norm_prev = [last] + stats_mm_prev
        for c in range(NCH):
            st_prev = ctx.sync.dma_start(
                out=stream_out[s, c, :, ns * 512:(ns + 1) * 512],
                in_=out_sb[:, c * 512:(c + 1) * 512],
                wait=([last] if c == 0 else None))
    ctx.pop_scope()
    ctx.sync.wait_ge("Y", ctx.c("Y"))


def prep_C(inputs):
    W = {}
    W["wqT"] = tiles_lhsT(np.ascontiguousarray(inputs["L1_attn_Wq"].T))
    W["woT"] = tiles_lhsT(np.ascontiguousarray(inputs["L1_attn_Wo"].T))
    W["wkT_rhs"] = np.ascontiguousarray(inputs["L1_attn_Wk"].T).reshape(8, 128, 1024)
    W["wvT_rhs"] = np.ascontiguousarray(inputs["L1_attn_Wv"].T).reshape(8, 128, 1024)
    W["bo2d"] = vec2d(inputs["L1_attn_bo"])
    return W


def consts_C_np():
    selq = np.zeros((128, 128), dtype=BF16NP)
    selb = np.zeros((16, 1024), dtype=BF16NP)
    for c in range(8):
        for p in range(128):
            h = 2 * c + (p // 64)
            selq[p, c * 16 + h] = 1
            selb[h, c * 128 + p] = 1
    b2 = np.zeros((128, 2), dtype=BF16NP)
    b2[0:64, 0] = 1
    b2[64:128, 1] = 1
    return {"selq16": selq, "selb16": selb, "b2ind": b2}


def declare_C_consts(nc):
    return {"selq16": nc.dram_tensor("selq16", [128, 128], BF16, kind="ExternalInput").ap(),
            "selb16": nc.dram_tensor("selb16", [16, 1024], BF16, kind="ExternalInput").ap(),
            "b2ind": nc.dram_tensor("b2ind", [128, 2], BF16, kind="ExternalInput").ap()}


def declare_C_weights(nc):
    aps = {}
    aps["wqT"] = nc.dram_tensor("CwqT", [8, 128, 1024], F32, kind="ExternalInput").ap()
    aps["woT"] = nc.dram_tensor("CwoT", [8, 128, 1024], F32, kind="ExternalInput").ap()
    aps["wkT_rhs"] = nc.dram_tensor("CwkT_rhs", [8, 128, 1024], F32, kind="ExternalInput").ap()
    aps["wvT_rhs"] = nc.dram_tensor("CwvT_rhs", [8, 128, 1024], F32, kind="ExternalInput").ap()
    aps["bo2d"] = nc.dram_tensor("Cbo2d", [128, 8], F32, kind="ExternalInput").ap()
    return aps


def build_C_standalone(debug=False):
    nc = bass.Bass("TRN2", target_bir_lowering=False)
    stream_in = nc.dram_tensor("stream_in", [SEQ, NCH, 128, T], BF16,
                               kind="ExternalInput").ap()
    stream_out = nc.dram_tensor("stream_out", [SEQ, NCH, 128, T], BF16,
                                kind="ExternalOutput").ap()
    dbg = None
    if debug:
        dbg = {
            "sq": nc.dram_tensor("dbg_sq", [NCH, 128, T], BF16, kind="ExternalOutput").ap(),
            "fk": nc.dram_tensor("dbg_fk", [NTT, 128, 1024], BF16, kind="ExternalOutput").ap(),
            "v": nc.dram_tensor("dbg_v", [NTT, 128, 1024], BF16, kind="ExternalOutput").ap(),
            "rkc": nc.dram_tensor("dbg_rkc", [128, 256], F32, kind="ExternalOutput").ap(),
            "rq": nc.dram_tensor("dbg_rq", [16, T], BF16, kind="ExternalOutput").ap(),
            "ctxt0": nc.dram_tensor("dbg_ctxt0", [64, 64], BF16, kind="ExternalOutput").ap(),
            "inj_sq": nc.dram_tensor("inj_sq", [NCH, 128, T], BF16, kind="ExternalInput").ap(),
            "inj_fk": nc.dram_tensor("inj_fk", [NTT, 128, 1024], BF16, kind="ExternalInput").ap(),
            "inj_v": nc.dram_tensor("inj_v", [NTT, 128, 1024], BF16, kind="ExternalInput").ap(),
            "inj_rkc": nc.dram_tensor("inj_rkc", [128, 256], F32, kind="ExternalInput").ap(),

            "attn": nc.dram_tensor("dbg_attn", [NCH, 128, T], BF16, kind="ExternalOutput").ap(),
        }
    with contextlib.ExitStack() as es:
        ctx = Ctx(nc, es)
        ctx.init_epscols()
        W = declare_C_weights(nc)
        consts = declare_consts(nc)
        consts.update(declare_C_consts(nc))
        ctx.push_scope()
        H = emit_phase_C_consts(ctx, W, consts)
        for s in range(SEQ):
            ctx.push_scope()
            emit_phase_C_seq(ctx, W, H, stream_in, stream_out, s, dbg=dbg)
            ctx.flush(f"C_s{s}")
            ctx.pop_scope()
        ctx.pop_scope()
    return nc


def run_C(inputs, stream_bf16_percore, debug=False):
    nc = build_C_standalone(debug)
    W = dmap_prefix(prep_C(inputs), "C")
    in_maps = []
    for c in range(NCORES):
        m = dict(W)
        m.update(consts_np())
        m.update(consts_C_np())
        m["stream_in"] = stream_bf16_percore[c]
        if debug:
            m["inj_sq"] = np.zeros((NCH, 128, T), BF16NP)
            m["inj_fk"] = np.zeros((NTT, 128, 1024), BF16NP)
            m["inj_v"] = np.zeros((NTT, 128, 1024), BF16NP)
            m["inj_rkc"] = np.zeros((128, 256), np.float32)
        in_maps.append(m)
    res = run_bass_kernel_spmd(nc, in_maps, core_ids=list(range(NCORES)))
    if debug:
        return res.results
    return [r["stream_out"] for r in res.results]



A_SCALE = 64.0 ** (-0.5)


def emit_phase_A_consts(ctx, W, consts):
    H = {}
    H["ident"] = ctx.sbuf("Aid", (128, 128), BF16)
    H["ones"] = ctx.sbuf("Aones", (128, 128), BF16)
    H["b2d"] = ctx.sbuf("Ab2d", (128, 8 + 8 + 8 + 8), F32)
    H["bqk2d"] = H["b2d"][:, 0:8]
    H["bout2d"] = H["b2d"][:, 8:16]
    H["g2d"] = H["b2d"][:, 16:24]
    H["be2d"] = H["b2d"][:, 24:32]
    H["bvrow"] = ctx.sbuf("Abv", (1, 512), BF16)
    H["onesk1"] = ctx.sbuf("Ak1", (1, 128), BF16)
    H["bvrep"] = ctx.sbuf("Abvr", (128, 512), F32)
    ctx.sync.dma_start(out=H["ident"][:, :], in_=consts["ident128b"])
    ctx.sync.dma_start(out=H["ones"][:, :], in_=consts["ones128"])
    ctx.sync.dma_start(out=H["b2d"][:, 0:8], in_=W["bqk2d"])
    ctx.sync.dma_start(out=H["b2d"][:, 8:16], in_=W["bout2d"])
    ctx.sync.dma_start(out=H["b2d"][:, 16:24], in_=W["g2d"])
    ctx.sync.dma_start(out=H["b2d"][:, 24:32], in_=W["be2d"])
    ctx.gpsimd.dma_start(out=H["bvrow"][0:1, :], in_=W["bv_row"])
    ctx.gpsimd.dma_start(out=H["onesk1"][0:1, :], in_=consts["ones128"][0:1, :])
    ld = ctx.snap("Y", "G")
    H["wqk"] = ctx.sbuf("Awqk", (128, NCH * 1024), BF16)
    H["wv"] = ctx.sbuf("Awv", (128, NCH * 512), BF16)
    H["wout"] = ctx.sbuf("Awout", (128, 4 * 1024), BF16)
    for c in range(NCH):
        ctx.gpsimd.dma_start(out=H["wqk"][:, c * 1024:(c + 1) * 1024],
                             in_=W["wqkT"][c])
        ctx.gpsimd.dma_start(out=H["wv"][:, c * 512:(c + 1) * 512],
                             in_=W["wv_rhs"][c])
    for m in range(NCH):
        ctx.gpsimd.dma_start(out=H["wout"][:, m * 512:(m + 1) * 512],
                             in_=W["woutT"][m])
    H["ready"] = ctx.snap("Y", "G")
    ctx.push_scope()
    ps = ctx.psum("Abvps", (128, 512), F32)
    mk = ctx.tensor.matmul(ps[:, :], H["onesk1"][0:1, :], H["bvrow"][0:1, :],
                           start=True, stop=True, wait=ld, tinc=True)
    H["bv_done"] = [ctx.scalar.activation(out=H["bvrep"][:, :], in_=ps[:, :],
                                          func=AF.Identity, wait=[mk])]
    ctx.pop_scope()
    return H


def emit_phase_A_seq(ctx, W, H, x_fm, stream_out, s):
    qk_sb = ctx.sbuf("Aqk", (128, NCH * T), BF16)
    v_sb = ctx.sbuf("Av", (128, NTT * 512), BF16)
    o_fm = ctx.sbuf("Aofm", (128, 4 * T), BF16)

    ps_s = [ctx.psum(f"Aps{i}", (128, 512), F32) for i in range(4)]
    ps_t = [ctx.psum(f"Apt{i}", (128, 512), BF16) for i in range(2)]
    ps_pv = ctx.psum("Apv", (128, 64), F32)
    ps_ot = ctx.psum("Aot", (128, 128), BF16)

    ctx.push_scope()
    x_sb = ctx.sbuf("Ax", (128, NCH * T), BF16)
    for c in range(NCH):
        ctx.gpsimd.dma_start(out=x_sb[:, c * T:(c + 1) * T],
                             in_=x_fm[s, c, :, 0:T],
                             wait=(H["ready"] if c == 0 else None))
    x_ready = ctx.snap("G")

    ev_pr = [None, None, None, None]
    for m in range(NCH):
        for ns in range(4):
            pslot = ns % 4
            pp = ps_s[pslot]
            for k in range(NCH):
                w = None
                if k == 0:
                    w = list(x_ready) + H["bv_done"]
                    if ev_pr[pslot]:
                        w = w + [ev_pr[pslot]]
                ctx.tensor.matmul(pp[:, :],
                                  H["wqk"][:, m * 1024 + k * 128: m * 1024 + (k + 1) * 128],
                                  x_sb[:, k * T + ns * 512: k * T + (ns + 1) * 512],
                                  start=(k == 0), stop=(k == NCH - 1),
                                  wait=w, tinc=(k == NCH - 1))
            mm_now = ctx.snap("T")
            ev_pr[pslot] = ctx.scalar.activation(
                out=qk_sb[:, m * T + ns * 512: m * T + (ns + 1) * 512],
                in_=pp[:, :], func=AF.Identity,
                bias=H["bqk2d"][:, m:m + 1], wait=mm_now)
    qk_done = ctx.snap("S")

    for mt in range(NTT):
        pslot = mt % 4
        pp = ps_s[pslot]
        for k in range(NCH):
            w = None
            if k == 0:
                w = [ev_pr[pslot]] if ev_pr[pslot] else list(x_ready)
            ctx.tensor.matmul(pp[:, :],
                              x_sb[:, k * T + mt * 128: k * T + (mt + 1) * 128],
                              H["wv"][:, k * 512:(k + 1) * 512],
                              start=(k == 0), stop=(k == NCH - 1),
                              wait=w, tinc=(k == NCH - 1))
        mm_now = ctx.snap("T")
        ev_pr[pslot] = ctx.vector.tensor_add(v_sb[:, mt * 512:(mt + 1) * 512],
                                             pp[:, :], H["bvrep"][:, :],
                                             wait=mm_now)
    v_done = [ev_pr[i] for i in range(4) if ev_pr[i]]
    ctx.pop_scope()

    ctx.push_scope()
    p_sb = [ctx.sbuf(f"Ap{i}", (128, T), BF16) for i in range(2)]
    pt_sb = [ctx.sbuf(f"Apt{i}", (128, T), BF16) for i in range(2)]
    otm_sb = [ctx.sbuf(f"Aotm{i}", (128, 64), BF16) for i in range(2)]
    sm_sb = ctx.sbuf("Asm", (128, 16), F32)
    mcols = sm_sb[:, 0:4]
    mm_ = sm_sb[:, 4:5]
    msc = sm_sb[:, 5:6]
    acc4 = sm_sb[:, 8:12]
    rs = sm_sb[:, 6:7]
    rr = sm_sb[:, 7:8]

    exp_marks = [None, None, None, None]
    tev_marks = [None, None]
    pv_mm_slot = [None, None]
    pv_mark = None
    ot_mark = None
    otm_mark = [None, None]
    pt_done = [None, None]

    units = [(h, it) for h in range(8) for it in range(NTT)]
    for ui, (h, it) in enumerate(units):
        par = h % 2
        qrows = slice(par * 64, par * 64 + 64)
        qcols = slice((h // 2) * T + it * 128, (h // 2) * T + (it + 1) * 128)
        uslot = ui % 2
        for j in range(4):
            w = []
            if exp_marks[j]:
                w.append(exp_marks[j])
            if ui == 0 and j == 0:
                w = w + qk_done + v_done
            ctx.tensor.matmul(
                ps_s[j][:, :], qk_sb[qrows, qcols],
                qk_sb[qrows, (4 + h // 2) * T + j * 512:(4 + h // 2) * T + (j + 1) * 512],
                start=True, stop=True, wait=(w or None), tinc=True,
                tile_position=(par * 64, 0))
        mm_s = ctx.snap("T")
        mk = None
        for j in range(4):
            mk = ctx.vector.tensor_reduce(mcols[:, j:j + 1], ps_s[j][:, :],
                                          AX.X, ALU.max,
                                          wait=(mm_s if j == 0 else [mk]))
        mk = ctx.vector.tensor_reduce(mm_, mcols, AX.X, ALU.max, wait=[mk])
        mk_msc = ctx.vector.tensor_single_scalar(msc, mm_, -A_SCALE, ALU.mult,
                                                 wait=[mk])
        emk = []
        for j in range(4):
            w = [mk_msc]
            if j == 0 and pt_done[uslot]:
                w.append(pt_done[uslot])
            emk.append(ctx.scalar.activation(
                out=p_sb[uslot][:, j * 512:(j + 1) * 512], in_=ps_s[j][:, :],
                func=AF.Exp, scale=A_SCALE, bias=msc,
                accum_out=acc4[:, j:j + 1],
                wait=(w if j == 0 else [mk_msc])))
            exp_marks[j] = emk[-1]
        mk = ctx.vector.tensor_reduce(rs, acc4, AX.X, ALU.add, wait=[emk[-1]])
        rmk = ctx.vector.reciprocal(rr, rs, wait=[mk])
        tmarks = []
        for jt in range(NTT):
            tb = (jt // 4) % 2
            w = [emk[-1]]
            if tev_marks[tb] and jt % 4 == 0:
                w.append(tev_marks[tb])
            tmarks.append(ctx.tensor.transpose(
                ps_t[tb][:, (jt % 4) * 128:(jt % 4 + 1) * 128],
                p_sb[uslot][:, jt * 128:(jt + 1) * 128],
                H["ident"][:, :], wait=w, tinc=True))
            if jt % 4 == 3:
                wv2 = [tmarks[-1]]
                if jt == 3 and pv_mm_slot[uslot]:
                    wv2 = wv2 + pv_mm_slot[uslot]
                tev_marks[tb] = ctx.vector.tensor_copy(
                    pt_sb[uslot][:, (jt - 3) * 128:(jt + 1) * 128],
                    ps_t[tb][:, :], wait=wv2)
        pt_done[uslot] = tev_marks[(NTT - 1) // 4 % 2]
        for jt in range(NTT):
            w = None
            if jt == 0:
                w = [tev_marks[0], tev_marks[1]]
                if pv_mark:
                    w.append(pv_mark)
            ctx.tensor.matmul(ps_pv[:, :],
                              pt_sb[uslot][:, jt * 128:(jt + 1) * 128],
                              v_sb[:, jt * 512 + h * 64: jt * 512 + (h + 1) * 64],
                              start=(jt == 0), stop=(jt == NTT - 1),
                              wait=w, tinc=(jt == NTT - 1))
        mm_pv = ctx.snap("T")
        pv_mm_slot[uslot] = mm_pv
        w = mm_pv + [rmk]
        if otm_mark[uslot]:
            w.append(otm_mark[uslot])
        pv_mark = ctx.vector.tensor_scalar_mul(otm_sb[uslot][:, :], ps_pv[:, :],
                                               rr, wait=w)
        w = [pv_mark]
        if ot_mark:
            w.append(ot_mark)
        tmk = ctx.tensor.transpose(ps_ot[par * 64:par * 64 + 64, :],
                                   otm_sb[uslot][:, :], H["ident"][:, :],
                                   wait=w, tinc=True,
                                   tile_position=(0, par * 64))
        ot_mark = ctx.scalar.activation(
            out=o_fm[par * 64 + 0:par * 64 + 64,
                     (h // 2) * T + it * 128:(h // 2) * T + (it + 1) * 128],
            in_=ps_ot[par * 64:par * 64 + 64, :], func=AF.Identity, wait=[tmk])
        otm_mark[uslot] = ot_mark
    attn_done = ctx.snap("V", "S", "T")
    ctx.pop_scope()

    ctx.push_scope()
    xt_sb = ctx.sbuf("Axt", (128, NCH * 512), BF16)
    y_sb = ctx.sbuf("Ay", (128, NCH * 512), F32)
    zt_sb = ctx.sbuf("Azt", (128, 512), F32)
    ubf_sb = ctx.sbuf("Aubf", (128, NCH * 512), BF16)
    usq_sb = ctx.sbuf("Ausq", (128, NCH * 512), BF16)
    stat_sb = ctx.sbuf("Astat", (128, 4 * 512), F32)
    out_sb = ctx.sbuf("Aout2", (128, NCH * 512), BF16)
    ps_a = ps_s[0]
    ps_b = ps_s[1]

    stats_mm_prev = None
    norm_prev = None
    st_prev = None
    for ns in range(4):
        xw = list(attn_done)
        if st_prev:
            xw.append(st_prev)
        for c in range(NCH):
            ctx.gpsimd.dma_start(out=xt_sb[:, c * 512:(c + 1) * 512],
                                 in_=x_fm[s, c, :, ns * 512:(ns + 1) * 512],
                                 wait=(xw if c == 0 else None))
        xt_ready = ctx.snap("G")
        u_marks = []
        for m in range(NCH):
            pp = ps_s[m % 2]
            for k in range(4):
                w = None
                if k == 0:
                    w = xt_ready + attn_done
                    if m >= 2:
                        w = w + [u_marks[m - 2]]
                    elif norm_prev:
                        w = w + norm_prev
                ctx.tensor.matmul(pp[:, :],
                                  H["wout"][:, m * 512 + k * 128: m * 512 + (k + 1) * 128],
                                  o_fm[:, k * T + ns * 512: k * T + (ns + 1) * 512],
                                  start=(k == 0), stop=(k == 3),
                                  wait=w, tinc=(k == 3))
            mm_now = ctx.snap("T")
            u_marks.append(ctx.scalar.activation(
                out=y_sb[:, m * 512:(m + 1) * 512], in_=pp[:, :],
                func=AF.Identity, bias=H["bout2d"][:, m:m + 1], wait=mm_now))
        dep_u = [u_marks[-1]]

        cm = []
        for c in range(NCH):
            ycols = y_sb[:, c * 512:(c + 1) * 512]
            w = list(dep_u)
            if c == 0 and stats_mm_prev:
                w += stats_mm_prev
            cm.append(ctx.vector.tensor_copy(ubf_sb[:, c * 512:(c + 1) * 512],
                                             ycols, wait=(w if c == 0 else dep_u)))
            cm.append(ctx.scalar.activation(out=usq_sb[:, c * 512:(c + 1) * 512],
                                            in_=ycols, func=AF.Square,
                                            wait=(w if c == 0 else dep_u)))
        dep = [cm[-2], cm[-1]] + (stats_mm_prev or [])
        mu, rstd = emit_ln_stats(ctx, 512, LN_EPS_A,
                                 lambda c: ubf_sb[:, c * 512:(c + 1) * 512],
                                 lambda c: usq_sb[:, c * 512:(c + 1) * 512],
                                 H["ones"], ps_a, ps_b, stat_sb, dep)
        stats_mm_prev = ctx.snap("T")
        rstd_done = ctx.snap("S")
        umk = []
        for c in range(NCH):
            ycols = y_sb[:, c * 512:(c + 1) * 512]
            w = list(rstd_done) if c == 0 else None
            mk = ctx.vector.tensor_sub(zt_sb[:, :], ycols, mu, wait=w)
            mk = ctx.vector.tensor_mul(zt_sb[:, :], zt_sb[:, :], rstd, wait=[mk])
            mk = ctx.vector.tensor_scalar(zt_sb[:, :], zt_sb[:, :],
                                          H["g2d"][:, c:c + 1],
                                          H["be2d"][:, c:c + 1],
                                          ALU.mult, ALU.add, wait=[mk])
            mk = ctx.vector.scalar_tensor_tensor(
                ycols, zt_sb[:, :], KAPPA, xt_sb[:, c * 512:(c + 1) * 512],
                ALU.mult, ALU.add, wait=[mk])
            umk.append(mk)
        dep_u2 = [umk[-1]]

        cm = []
        for c in range(NCH):
            ucols = y_sb[:, c * 512:(c + 1) * 512]
            w = list(dep_u2) + stats_mm_prev
            cm.append(ctx.vector.tensor_copy(ubf_sb[:, c * 512:(c + 1) * 512],
                                             ucols, wait=(w if c == 0 else dep_u2)))
            cm.append(ctx.scalar.activation(out=usq_sb[:, c * 512:(c + 1) * 512],
                                            in_=ucols, func=AF.Square,
                                            wait=(w if c == 0 else dep_u2)))
        dep = [cm[-2], cm[-1]] + stats_mm_prev
        mu, rstd = emit_ln_stats(ctx, 512, LN_EPS_OUTER,
                                 lambda c: ubf_sb[:, c * 512:(c + 1) * 512],
                                 lambda c: usq_sb[:, c * 512:(c + 1) * 512],
                                 H["ones"], ps_a, ps_b, stat_sb, dep)
        stats_mm_prev = ctx.snap("T")
        rstd_done = ctx.snap("S")
        last = None
        for c in range(NCH):
            ucols = y_sb[:, c * 512:(c + 1) * 512]
            w = list(rstd_done) if c == 0 else None
            mk = ctx.vector.tensor_sub(ucols, ucols, mu, wait=w)
            last = ctx.vector.tensor_mul(out_sb[:, c * 512:(c + 1) * 512],
                                         ucols, rstd, wait=[mk])
        norm_prev = [last] + stats_mm_prev
        for c in range(NCH):
            st_prev = ctx.sync.dma_start(
                out=stream_out[s, c, :, ns * 512:(ns + 1) * 512],
                in_=out_sb[:, c * 512:(c + 1) * 512],
                wait=([last] if c == 0 else None))
    ctx.pop_scope()
    ctx.sync.wait_ge("Y", ctx.c("Y"))


def prep_A(inputs):
    W = {}
    Wqk = inputs["L0_attn_W_qk"]
    perm = np.zeros(1024, dtype=np.int64)
    for j in range(8):
        for p in range(128):
            if j < 4:
                h = 2 * j + p // 64
                perm[j * 128 + p] = h * 128 + (p % 64) * 2
            else:
                h = 2 * (j - 4) + p // 64
                perm[j * 128 + p] = h * 128 + (p % 64) * 2 + 1
    Wqk_p = np.ascontiguousarray(Wqk[:, perm])
    W["wqkT"] = tiles_lhsT(Wqk_p)
    W["bqk2d"] = vec2d(inputs["L0_attn_b_qk"][perm])
    W["wv_rhs"] = np.ascontiguousarray(inputs["L0_attn_W_v"].reshape(8, 128, 512))
    W["bv_row"] = np.ascontiguousarray(inputs["L0_attn_b_v"].reshape(1, 512))
    W["woutT"] = tiles_lhsT(inputs["L0_attn_W_out"])
    W["bout2d"] = vec2d(inputs["L0_attn_b_out"])
    W["g2d"] = vec2d(inputs["L0_attn_ln_gamma"])
    W["be2d"] = vec2d(inputs["L0_attn_ln_beta"])
    return W


def declare_A_weights(nc):
    aps = {}
    aps["wqkT"] = nc.dram_tensor("AwqkT", [8, 128, 1024], F32, kind="ExternalInput").ap()
    aps["wv_rhs"] = nc.dram_tensor("Awv_rhs", [8, 128, 512], F32, kind="ExternalInput").ap()
    aps["woutT"] = nc.dram_tensor("AwoutT", [8, 128, 512], F32, kind="ExternalInput").ap()
    aps["bv_row"] = nc.dram_tensor("Abv_row", [1, 512], F32, kind="ExternalInput").ap()
    for n in ("bqk2d", "bout2d", "g2d", "be2d"):
        aps[n] = nc.dram_tensor("A" + n, [128, 8], F32, kind="ExternalInput").ap()
    return aps


def build_A_standalone():
    nc = bass.Bass("TRN2", target_bir_lowering=False)
    x_fm = nc.dram_tensor("x_fm", [SEQ, NCH, 128, T], F32, kind="ExternalInput").ap()
    stream_out = nc.dram_tensor("stream_out", [SEQ, NCH, 128, T], BF16,
                                kind="ExternalOutput").ap()
    with contextlib.ExitStack() as es:
        ctx = Ctx(nc, es)
        ctx.init_epscols()
        W = declare_A_weights(nc)
        consts = declare_consts(nc)
        ctx.push_scope()
        H = emit_phase_A_consts(ctx, W, consts)
        for s in range(SEQ):
            ctx.push_scope()
            emit_phase_A_seq(ctx, W, H, x_fm, stream_out, s)
            ctx.flush(f"A_s{s}")
            ctx.pop_scope()
        ctx.pop_scope()
    return nc


def run_A(inputs, x_fm_percore):
    nc = build_A_standalone()
    W = dmap_prefix(prep_A(inputs), "A")
    in_maps = []
    for c in range(NCORES):
        m = dict(W)
        m.update(consts_np())
        m["x_fm"] = x_fm_percore[c]
        in_maps.append(m)
    res = run_bass_kernel_spmd(nc, in_maps, core_ids=list(range(NCORES)))
    return [r["stream_out"] for r in res.results]


def prep_D(inputs):
    W = {}
    W["w1_T"] = tiles_lhsT(np.ascontiguousarray(inputs["L1_ffn_w1_w"].T))
    W["w2_T"] = tiles_lhsT(np.ascontiguousarray(inputs["L1_ffn_w2_w"].T))
    W["h1_T"] = tiles_lhsT(np.ascontiguousarray(inputs["L1_ffn_h1_w"].T))
    W["h2_T"] = tiles_lhsT(np.ascontiguousarray(inputs["L1_ffn_h2_w"].T))
    W["w3_T"] = tiles_lhsT(np.ascontiguousarray(inputs["L1_ffn_w3_w"].T))
    for n in ("w1", "w2", "h1", "h2", "w3"):
        W[n + "_s2d"] = vec2d(inputs[f"L1_ffn_{n}_s"])
    return W


def consts_np():
    sel8 = np.zeros((8, 1024), dtype=BF16NP)
    for e in range(8):
        sel8[e, e * 128:(e + 1) * 128] = 1
    return {"ones128": np.ones((128, 128), dtype=BF16NP),
            "ident128b": np.eye(128, dtype=BF16NP),
            "ident128f": np.eye(128, dtype=np.float32),
            "sel8": sel8}


def declare_consts(nc):
    return {"ones128": nc.dram_tensor("ones128", [128, 128], BF16, kind="ExternalInput").ap(),
            "ident128b": nc.dram_tensor("ident128b", [128, 128], BF16, kind="ExternalInput").ap(),
            "ident128f": nc.dram_tensor("ident128f", [128, 128], F32, kind="ExternalInput").ap(),
            "sel8": nc.dram_tensor("sel8", [8, 1024], BF16, kind="ExternalInput").ap()}


def declare_D_weights(nc):
    aps = {}
    for name, shape in [
        ("w1_T", [HCH, 128, NCH * 128]), ("w2_T", [HCH, 128, NCH * 128]),
        ("h1_T", [HCH, 128, HCH * 128]), ("h2_T", [HCH, 128, HCH * 128]),
        ("w3_T", [NCH, 128, HCH * 128]),
    ]:
        aps[name] = nc.dram_tensor("D" + name, shape, F32, kind="ExternalInput").ap()
    for name in ("w1_s2d", "w2_s2d", "h1_s2d", "h2_s2d"):
        aps[name] = nc.dram_tensor("D" + name, [128, 32], F32, kind="ExternalInput").ap()
    aps["w3_s2d"] = nc.dram_tensor("Dw3_s2d", [128, 8], F32, kind="ExternalInput").ap()
    return aps


def dmap_prefix(W, prefix):
    return {prefix + k: v for k, v in W.items()}


def build_D_standalone():
    nc = bass.Bass("TRN2", target_bir_lowering=False)
    stream_in = nc.dram_tensor("stream_in", [SEQ, NCH, 128, T], BF16,
                               kind="ExternalInput").ap()
    y_fm = nc.dram_tensor("y_fm", [SEQ, NCH, 128, T], F32,
                          kind="ExternalOutput").ap()
    actA = nc.dram_tensor("actA", [HCH, 128, NTOK], BF16, kind="Internal").ap()
    actB = nc.dram_tensor("actB", [HCH, 128, NTOK], BF16, kind="Internal").ap()
    with contextlib.ExitStack() as es:
        ctx = Ctx(nc, es)
        ctx.init_epscols()
        W = declare_D_weights(nc)
        consts = declare_consts(nc)
        emit_phase_D(ctx, W, stream_in, actA, actB, y_fm, consts)
    return nc


def run_D(inputs, stream_bf16_percore):
    nc = build_D_standalone()
    W = dmap_prefix(prep_D(inputs), "D")
    in_maps = []
    for c in range(NCORES):
        m = dict(W)
        m.update(consts_np())
        m["stream_in"] = stream_bf16_percore[c]
        in_maps.append(m)
    res = run_bass_kernel_spmd(nc, in_maps, core_ids=list(range(NCORES)))
    return [r["y_fm"] for r in res.results]


def kernel(**inputs):
    inputs = {k: np.asarray(v) for k, v in inputs.items()}
    x = inputs["L0_attn_x"]
    x_fm = [np.ascontiguousarray(
        x[4 * c:4 * c + 4].transpose(0, 2, 1).reshape(SEQ, NCH, 128, T)
        .astype(np.float32)) for c in range(NCORES)]

    s1 = run_A(inputs, x_fm)
    s1 = [np.asarray(p) for p in s1]
    s2 = run_B(inputs, s1)
    s2 = [np.asarray(p) for p in s2]
    s3 = run_C(inputs, s2)
    s3 = [np.asarray(p) for p in s3]
    y = run_D(inputs, s3)
    full = np.concatenate([np.asarray(p) for p in y], axis=0)
    return np.ascontiguousarray(
        full.reshape(32, 1024, 2048).transpose(0, 2, 1)).astype(np.float32)


# version 02
# speedup vs baseline: 1.0429x; 1.0429x over previous
import contextlib
import numpy as np
import ml_dtypes
import concourse.bass as bass
import concourse.mybir as mybir
from concourse.bass_utils import run_bass_kernel_spmd

F32 = mybir.dt.float32
BF16 = mybir.dt.bfloat16
AF = mybir.ActivationFunctionType
ALU = mybir.AluOpType
AX = mybir.AxisListType
BF16NP = ml_dtypes.bfloat16

NCORES = 8
SEQ = 4
T = 2048
DM = 1024
NCH = DM // 128
NTOK = SEQ * T
DH = 4096
HCH = DH // 128

LN_EPS_OUTER = 1e-5
LN_EPS_A = 1e-5
LN_EPS_B = 1e-6
CTX_EPS = 1e-6
RSQRT2 = 2.0 ** (-0.5)
KAPPA = RSQRT2 / (1.0 + RSQRT2)

ENGINES = ("tensor", "vector", "scalar", "gpsimd", "sync")
AUTOSEM = {"tensor": ("T", 1), "vector": ("V", 1), "scalar": ("S", 1),
           "gpsimd": ("G", 16), "sync": ("Y", 16)}
NOINC_OPS = {"wait_ge", "nop", "engine_nop"}


class EQ:
    def __init__(self, ctx, name):
        object.__setattr__(self, "_ctx", ctx)
        object.__setattr__(self, "_name", name)

    def __getattr__(self, opname):
        ctx = self._ctx
        name = self._name

        def call(*args, wait=None, tinc=True, **kwargs):
            if opname == "wait_ge":
                s, v = args

                def run_wait(eng, s=s, v=v):
                    eng.wait_ge(ctx.sems[s], v)
                ctx.q[name].append(run_wait)
                return None
            sem, step = AUTOSEM[name]
            do_inc = opname not in NOINC_OPS and (name != "tensor" or tinc)
            if do_inc:
                ctx.cnt[sem] = ctx.cnt.get(sem, 0) + step
            mark = (sem, ctx.cnt.get(sem, 0)) if do_inc else None

            def run(eng, args=args, kwargs=kwargs, wait=wait, mark=mark):
                if wait:
                    seen = {}
                    for s, v in wait:
                        if v > 0 and seen.get(s, -1) < v:
                            seen[s] = v
                    for s, v in seen.items():
                        eng.wait_ge(ctx.sems[s], v)
                inst = getattr(eng, opname)(*args, **kwargs)
                if mark is not None:
                    inst.then_inc(ctx.sems[mark[0]], AUTOSEM[name][1])
                return inst

            ctx.q[name].append(run)
            return mark

        return call


class Ctx:
    def __init__(self, nc, es):
        self.nc = nc
        self.es = es
        self.scopes = [es]
        self.q = {e: [] for e in ENGINES}
        self.sems = {}
        self.cnt = {}
        for s in ("T", "V", "S", "G", "Y"):
            self.sems[s] = es.enter_context(nc.semaphore(s))
            self.cnt[s] = 0
        self.tensor = EQ(self, "tensor")
        self.vector = EQ(self, "vector")
        self.scalar = EQ(self, "scalar")
        self.gpsimd = EQ(self, "gpsimd")
        self.sync = EQ(self, "sync")

    def c(self, sem):
        return self.cnt.get(sem, 0)

    def snap(self, *sems):
        return [(s, self.cnt.get(s, 0)) for s in sems]

    def flush(self, name=None):
        with self.nc.Block(name) as block:
            for e in ENGINES:
                ops = self.q[e]
                if ops:
                    getattr(block, e)(lambda eng, ops=tuple(ops): [f(eng) for f in ops])
        for e in ENGINES:
            self.q[e] = []

    def push_scope(self):
        self.scopes.append(contextlib.ExitStack())

    def pop_scope(self):
        self.scopes.pop().close()

    def _uname(self, name):
        self._uid = getattr(self, "_uid", 0) + 1
        return f"{name}_{self._uid}"

    def sbuf(self, name, shape, dtype):
        return self.scopes[-1].enter_context(
            self.nc.sbuf_tensor(self._uname(name), list(shape), dtype))

    def psum(self, name, shape, dtype=F32):
        return self.scopes[-1].enter_context(
            self.nc.psum_tensor(self._uname(name), list(shape), dtype))

    def init_epscols(self):
        self.eps_sb = self.sbuf("epsconsts", (128, 4), F32)
        self.eps_cols = {}
        for i, v in enumerate((1e-5, 1e-6)):
            self.vector.memset(self.eps_sb[:, i:i + 1], v)
            self.eps_cols[v] = i

    def epscol(self, v):
        return self.eps_sb[:, self.eps_cols[v]:self.eps_cols[v] + 1]


def tiles_lhsT(wT):
    I_, O_ = wT.shape
    KC, Q = I_ // 128, O_ // 128
    t = wT.reshape(KC, 128, Q, 128).transpose(2, 1, 0, 3).reshape(Q, 128, KC * 128)
    return np.ascontiguousarray(t)


def vec2d(v):
    return np.ascontiguousarray(np.asarray(v).reshape(-1, 128).T)


def emit_ln_stats(ctx, NTt, eps, val_chunks, sq_chunks, ones_sb, ps_s1, ps_s2,
                  stat_sb, dep):
    for c in range(NCH):
        ctx.tensor.matmul(ps_s1[:, 0:NTt], ones_sb[:, :], val_chunks(c),
                          start=(c == 0), stop=(c == NCH - 1),
                          wait=(dep if c == 0 else None), tinc=False)
        ctx.tensor.matmul(ps_s2[:, 0:NTt], ones_sb[:, :], sq_chunks(c),
                          start=(c == 0), stop=(c == NCH - 1),
                          tinc=(c == NCH - 1))
    stats_mm = ctx.snap("T")
    mu = stat_sb[:, 0 * NTt:1 * NTt]
    musq = stat_sb[:, 1 * NTt:2 * NTt]
    var = stat_sb[:, 2 * NTt:3 * NTt]
    rstd = stat_sb[:, 3 * NTt:4 * NTt]
    m_mu = ctx.scalar.activation(out=mu, in_=ps_s1[:, 0:NTt], func=AF.Copy,
                                 scale=1.0 / DM, wait=stats_mm + dep)
    m_musq = ctx.scalar.activation(out=musq, in_=mu, func=AF.Square,
                                   wait=[m_mu])
    m_var = ctx.vector.scalar_tensor_tensor(var, ps_s2[:, 0:NTt], 1.0 / DM,
                                            musq, ALU.mult, ALU.subtract,
                                            wait=[m_musq] + stats_mm)
    m_ln = ctx.scalar.activation(out=var, in_=var, func=AF.Ln,
                                 bias=ctx.epscol(eps), wait=[m_var])
    ctx.scalar.activation(out=rstd, in_=var, func=AF.Exp, scale=-0.5,
                          wait=[m_ln])
    return mu, rstd


def emit_mm_stage(ctx, tag, w_ap, scale_sb, scale_col0, src_slab, dst_slab,
                  KC, Q, act, extra=None):
    NT = 512
    NGRP = 2048
    ngroups = NTOK // NGRP
    ntiles = NGRP // NT

    w_sb = [ctx.sbuf(f"{tag}w{i}", (128, KC * 128), BF16) for i in range(2)]
    src_sb = ctx.sbuf(f"{tag}src", (128, KC * NGRP), BF16)
    t1_sb = [ctx.sbuf(f"{tag}t1_{i}", (128, NGRP), BF16) for i in range(2)] \
        if extra is not None else None
    dst_sb = [ctx.sbuf(f"{tag}dst{i}", (128, NGRP), BF16) for i in range(2)]
    ps = [ctx.psum(f"{tag}ps{i}", (128, NT), F32) for i in range(2)]

    mm_after_user = []
    ev_after_user = []
    st_after_user = []
    ev_after_tile = []
    mm_after_group = []

    for g in range(ngroups):
        n0 = g * NGRP
        w = mm_after_group[g - 1] if g >= 1 else None
        for k in range(KC):
            ctx.sync.dma_start(out=src_sb[:, k * NGRP:(k + 1) * NGRP],
                               in_=src_slab(k, n0, NGRP),
                               wait=(w if k == 0 else None))
        sl_ready = ctx.snap("Y")

        for m in range(Q):
            u = g * Q + m
            wslot = m % 2
            w = mm_after_user[u - 2] if u >= 2 else None
            ctx.gpsimd.dma_start(out=w_sb[wslot][:, :], in_=w_ap[m], wait=w)
            wl_ready = ctx.snap("G")

            if extra is not None:
                w = ev_after_user[u - 2] if u >= 2 else None
                ctx.sync.dma_start(out=t1_sb[m % 2][:, :],
                                   in_=extra["t1"](m, n0, NGRP), wait=w)
            sl_ready2 = ctx.snap("Y")

            for ni in range(ntiles):
                pslot = (u * ntiles + ni) % 2
                tile_idx = u * ntiles + ni
                for k in range(KC):
                    w = None
                    if k == 0:
                        w = wl_ready + sl_ready2
                        if tile_idx >= 2:
                            w = w + [ev_after_tile[tile_idx - 2]]
                    ctx.tensor.matmul(
                        ps[pslot][:, :],
                        w_sb[wslot][:, k * 128:(k + 1) * 128],
                        src_sb[:, k * NGRP + ni * NT: k * NGRP + (ni + 1) * NT],
                        start=(k == 0), stop=(k == KC - 1),
                        wait=w, tinc=(k == KC - 1))
                mm_now = ctx.snap("T")
                dcols = dst_sb[m % 2][:, ni * NT:(ni + 1) * NT]
                wlist = list(mm_now)
                if ni == 0 and u >= 2:
                    wlist.append(st_after_user[u - 2])
                if extra is None:
                    mk = ctx.scalar.activation(
                        out=dcols, in_=ps[pslot][:, :], func=act,
                        scale=scale_sb[:, scale_col0 + m:scale_col0 + m + 1],
                        wait=wlist)
                else:
                    mk = ctx.vector.scalar_tensor_tensor(
                        dcols, ps[pslot][:, :],
                        scale_sb[:, scale_col0 + m:scale_col0 + m + 1],
                        t1_sb[m % 2][:, ni * NT:(ni + 1) * NT],
                        ALU.mult, ALU.mult, wait=wlist)
                ev_after_tile.append(mk)
            mm_after_user.append(ctx.snap("T"))
            ev_after_user.append([ev_after_tile[-1]])
            mk = ctx.sync.dma_start(out=dst_slab(m, n0, NGRP),
                                    in_=dst_sb[m % 2][:, :],
                                    wait=[ev_after_tile[-1]])
            st_after_user.append(mk)
        mm_after_group.append(ctx.snap("T"))
    ctx.sync.wait_ge("Y", ctx.c("Y"))


def emit_phase_D(ctx, W, stream_in, actA, actB, y_fm, consts):
    sc = ctx.sbuf("Dscales", (128, 160), F32)
    ctx.sync.dma_start(out=sc[:, 0:32], in_=W["w1_s2d"])
    ctx.sync.dma_start(out=sc[:, 32:64], in_=W["w2_s2d"])
    ctx.sync.dma_start(out=sc[:, 64:96], in_=W["h1_s2d"])
    ctx.sync.dma_start(out=sc[:, 96:128], in_=W["h2_s2d"])
    ctx.sync.dma_start(out=sc[:, 128:136], in_=W["w3_s2d"])
    ctx.sync.wait_ge("Y", ctx.c("Y"))
    ctx.flush("D_scales")

    def stream_slab(k, n0, nn):
        s, t0 = n0 // T, n0 % T
        return stream_in[s, k, :, t0:t0 + nn]

    def mk_slab(dram):
        return lambda m, n0, nn: dram[m, :, n0:n0 + nn]

    stages = [
        ("D_w1", lambda: emit_mm_stage(ctx, "Dw1", W["w1_T"], sc, 0, stream_slab,
                                       mk_slab(actA), KC=NCH, Q=HCH, act=AF.Silu)),
        ("D_w2", lambda: emit_mm_stage(ctx, "Dw2", W["w2_T"], sc, 32, stream_slab,
                                       mk_slab(actB), KC=NCH, Q=HCH, act=None,
                                       extra={"t1": mk_slab(actA)})),
        ("D_h1", lambda: emit_mm_stage(ctx, "Dh1", W["h1_T"], sc, 64, mk_slab(actB),
                                       mk_slab(actA), KC=HCH, Q=HCH, act=AF.Gelu)),
        ("D_h2", lambda: emit_mm_stage(ctx, "Dh2", W["h2_T"], sc, 96, mk_slab(actA),
                                       mk_slab(actB), KC=HCH, Q=HCH, act=AF.Gelu)),
        ("D_tail", lambda: emit_phase_D_tail(ctx, W, stream_in, actB, y_fm, sc, consts)),
    ]
    for name, fn in stages:
        ctx.push_scope()
        fn()
        ctx.flush(name)
        ctx.pop_scope()


def emit_phase_D_tail(ctx, W, stream_in, actB, y_fm, sc, consts):
    NT = 256
    KC, Q = HCH, NCH
    w_ap = W["w3_T"]

    w_sb = ctx.sbuf("Dt_w", (128, Q * KC * 128), BF16)
    for m in range(Q):
        ctx.gpsimd.dma_start(out=w_sb[:, m * KC * 128:(m + 1) * KC * 128],
                             in_=w_ap[m])
    wl_ready = ctx.snap("G")

    src_sb = [ctx.sbuf(f"Dt_src{i}", (128, KC * NT), BF16) for i in range(2)]
    x_sb = [ctx.sbuf(f"Dt_x{i}", (128, NCH * NT), BF16) for i in range(2)]
    u_sb = ctx.sbuf("Dt_u", (128, NCH * NT), F32)
    ubf_sb = ctx.sbuf("Dt_ubf", (128, NCH * NT), BF16)
    usq_sb = ctx.sbuf("Dt_usq", (128, NCH * NT), BF16)
    stat_sb = ctx.sbuf("Dt_stat", (128, 4 * NT), F32)
    out_sb = [ctx.sbuf(f"Dt_out{i}", (128, NCH * NT), F32) for i in range(2)]
    ones_sb = ctx.sbuf("Dt_ones", (128, 128), BF16)
    ctx.sync.dma_start(out=ones_sb[:, :], in_=consts["ones128"])

    ps = [ctx.psum(f"Dt_ps{i}", (128, NT), F32) for i in range(2)]
    ps_s1 = ctx.psum("Dt_s1", (128, NT), F32)
    ps_s2 = ctx.psum("Dt_s2", (128, NT), F32)

    ntiles = NTOK // NT
    mm_after_tile = []
    norm_done_tile = []
    st_after_tile = []
    stats_mm_prev = None

    for ni in range(ntiles):
        n0 = ni * NT
        s, t0 = n0 // T, n0 % T
        slot = ni % 2
        w = None
        if ni >= 2:
            w = mm_after_tile[ni - 2] + norm_done_tile[ni - 2]
        for k in range(KC):
            ctx.sync.dma_start(out=src_sb[slot][:, k * NT:(k + 1) * NT],
                               in_=actB[k, :, n0:n0 + NT],
                               wait=(w if k == 0 else None))
        for c in range(NCH):
            ctx.sync.dma_start(out=x_sb[slot][:, c * NT:(c + 1) * NT],
                               in_=stream_in[s, c, :, t0:t0 + NT])
        sl_ready = ctx.snap("Y")

        tile_ev_u = []
        for m in range(Q):
            pslot = m % 2
            for k in range(KC):
                w = None
                if k == 0:
                    w = sl_ready + wl_ready
                    if ni >= 1 and m == 0:
                        w = w + norm_done_tile[ni - 1]
                    if m >= 2:
                        w = w + [tile_ev_u[m - 2]]
                ctx.tensor.matmul(ps[pslot][:, :],
                                  w_sb[:, (m * KC + k) * 128:(m * KC + k + 1) * 128],
                                  src_sb[slot][:, k * NT:(k + 1) * NT],
                                  start=(k == 0), stop=(k == KC - 1),
                                  wait=w, tinc=(k == KC - 1))
            mm_now = ctx.snap("T")
            mk = ctx.vector.scalar_tensor_tensor(
                u_sb[:, m * NT:(m + 1) * NT], ps[pslot][:, :],
                sc[:, 128 + m:128 + m + 1], x_sb[slot][:, m * NT:(m + 1) * NT],
                ALU.mult, ALU.add, wait=mm_now)
            tile_ev_u.append(mk)
        mm_after_tile.append(ctx.snap("T"))

        dep_u = [tile_ev_u[-1]]
        cm = []
        for c in range(NCH):
            w = list(dep_u)
            if stats_mm_prev and c == 0:
                w += stats_mm_prev
            cm.append(ctx.vector.tensor_copy(
                ubf_sb[:, c * NT:(c + 1) * NT], u_sb[:, c * NT:(c + 1) * NT],
                wait=(w if c == 0 else dep_u)))
            cm.append(ctx.scalar.activation(
                out=usq_sb[:, c * NT:(c + 1) * NT],
                in_=u_sb[:, c * NT:(c + 1) * NT], func=AF.Square,
                wait=(w if c == 0 else dep_u)))
        dep = [cm[-2], cm[-1]]
        mu, rstd = emit_ln_stats(ctx, NT, LN_EPS_OUTER,
                                 lambda c: ubf_sb[:, c * NT:(c + 1) * NT],
                                 lambda c: usq_sb[:, c * NT:(c + 1) * NT],
                                 ones_sb, ps_s1, ps_s2, stat_sb, dep)
        stats_mm_prev = ctx.snap("T")
        rstd_done = ctx.snap("S")
        oslot = ni % 2
        last = None
        for c in range(NCH):
            ucols = u_sb[:, c * NT:(c + 1) * NT]
            w = list(rstd_done) if c == 0 else []
            if ni >= 2 and c == 0:
                w.append(st_after_tile[ni - 2])
            mk = ctx.vector.tensor_sub(ucols, ucols, mu, wait=(w or None))
            last = ctx.vector.tensor_mul(out_sb[oslot][:, c * NT:(c + 1) * NT],
                                         ucols, rstd, wait=[mk])
        norm_done_tile.append([last])
        stmk = None
        for c in range(NCH):
            stmk = ctx.sync.dma_start(out=y_fm[s, c, :, t0:t0 + NT],
                                      in_=out_sb[oslot][:, c * NT:(c + 1) * NT],
                                      wait=([last] if c == 0 else None))
        st_after_tile.append(stmk)
    ctx.sync.wait_ge("Y", ctx.c("Y"))


B_BIG = 30000.0


def emit_phase_B(ctx, W, stream_in, stream_out, consts, dbg=None):
    TM = 1024
    NS = TM // 512
    npass = NTOK // TM
    NE = 8

    gate_w = ctx.sbuf("Bgatew", (128, 64), BF16)
    identf = ctx.sbuf("Bidf", (128, 128), F32)
    ones_sb = ctx.sbuf("Bones", (128, 128), BF16)
    sel8 = ctx.sbuf("Bsel8", (8, 1024), BF16)
    gb2d = ctx.sbuf("Bgb", (128, 448), F32)
    gamma2d = gb2d[:, 0:64]
    beta2d = gb2d[:, 64:128]
    b1_2d = gb2d[:, 128:384]
    b2_2d = gb2d[:, 384:448]
    ctx.gpsimd.dma_start(out=gate_w[:, :], in_=W["gate_w"])
    ctx.sync.dma_start(out=identf[:, :], in_=consts["ident128f"])
    ctx.sync.dma_start(out=ones_sb[:, :], in_=consts["ones128"])
    ctx.sync.dma_start(out=sel8[:, :], in_=consts["sel8"])
    ctx.sync.dma_start(out=gb2d[:, 0:64], in_=W["gamma2d"])
    ctx.sync.dma_start(out=gb2d[:, 64:128], in_=W["beta2d"])
    ctx.sync.dma_start(out=gb2d[:, 128:384], in_=W["b1_2d"])
    ctx.sync.dma_start(out=gb2d[:, 384:448], in_=W["b2_2d"])
    consts_ready = ctx.snap("Y", "G")

    x_sb = ctx.sbuf("Bx", (128, NCH * TM), BF16)
    sq_sb = ctx.sbuf("Bsq", (128, NCH * 512), BF16)
    xhat_sb = ctx.sbuf("Bxhat", (128, NCH * TM), BF16)
    xh_scr = ctx.sbuf("Bxhscr", (128, 512), F32)
    stat_sb = ctx.sbuf("Bstat", (128, 4 * 512), F32)
    nrm_sb = ctx.sbuf("Bnrm", (128, NCH * TM), BF16)
    h_sb = ctx.sbuf("Bh", (128, HCH * TM), BF16)
    u_sb = h_sb.bitcast(F32)
    y_sb = ctx.sbuf("By", (128, NCH * TM), F32)
    tmp_sb = ctx.sbuf("Btmp", (128, 512), F32)
    cwb_sb = ctx.sbuf("Bcwb", (128, TM), BF16)
    scores_sb = ctx.sbuf("Bscores", (8, TM), F32)
    s_tm = ctx.sbuf("Bstm", (128, 64), F32)
    g_scr = ctx.sbuf("Bgscr", (128, 48), F32)
    cw_tm = ctx.sbuf("Bcwtm", (128, 64), F32)
    cw_fm = ctx.sbuf("Bcwfm", (8, TM), BF16)
    w1slab = [ctx.sbuf(f"Bw1s{i}", (128, 1024), BF16) for i in range(2)]
    w2slab = [ctx.sbuf(f"Bw2s{i}", (128, 4096), BF16) for i in range(2)]

    ps_h = [ctx.psum(f"Bpsh{i}", (128, 512), F32) for i in range(2)]
    ps_y = [ctx.psum(f"Bpsy{i}", (128, 512), F32) for i in range(2)]
    ps_cw = ctx.psum("Bpscw", (128, 1024), F32)
    ps_s1 = ctx.psum("Bps1", (128, 512), F32)
    ps_s2 = ctx.psum("Bps2", (128, 512), F32)

    free_h = [None, None]
    free_y = [None, None]
    free_cw = None
    w1_users = []
    w2_users = []
    u_read_x = None
    st_prev = None
    stats_mm_prev = None
    evac_prev_e = None

    for p in range(npass):
        s, t0 = p // (T // TM), (p % (T // TM)) * TM

        for c in range(NCH):
            w = list(consts_ready)
            if p >= 1 and c == 0:
                w.append(u_read_x)
            ctx.sync.dma_start(out=x_sb[:, c * TM:(c + 1) * TM],
                               in_=stream_in[s, c, :, t0:t0 + TM],
                               wait=(w if c == 0 else None))
        x_ready = ctx.snap("Y")

        for ns in range(NS):
            for k in range(NCH):
                w = None
                if k == 0:
                    w = x_ready + consts_ready
                    if free_y[ns]:
                        w = w + [free_y[ns]]
                ctx.tensor.matmul(ps_y[ns][0:8, :], gate_w[:, k * 8:(k + 1) * 8],
                                  x_sb[:, k * TM + ns * 512: k * TM + (ns + 1) * 512],
                                  start=(k == 0), stop=(k == NCH - 1),
                                  wait=w, tinc=(k == NCH - 1))
            mm_now = ctx.snap("T")
            free_y[ns] = ctx.scalar.activation(
                out=scores_sb[0:8, ns * 512:(ns + 1) * 512],
                in_=ps_y[ns][0:8, :], func=AF.Copy, wait=mm_now)

        for tau in range(8):
            w = None
            if tau == 0:
                w = [free_y[0], free_y[1]]
                if free_h[0]:
                    w.append(free_h[0])
            ctx.tensor.transpose(ps_h[0][:, tau * 8:(tau + 1) * 8],
                                 scores_sb[0:8, tau * 128:(tau + 1) * 128],
                                 identf[0:8, 0:8], wait=w, tinc=(tau == 7))
        mm_now = ctx.snap("T")
        stm_mk = ctx.vector.tensor_copy(s_tm[:, :], ps_h[0][:, 0:64], wait=mm_now)
        free_h[0] = stm_mk

        m1n = g_scr[:, 0:1]
        m2n = g_scr[:, 1:2]
        den = g_scr[:, 2:3]
        den2 = g_scr[:, 4:5]
        rden = g_scr[:, 3:4]
        sh = g_scr[:, 8:16]
        gmask = g_scr[:, 16:24]
        gm = g_scr[:, 24:32]
        gem = g_scr[:, 32:40]
        gex = g_scr[:, 40:48]
        mk = stm_mk
        for tau in range(8):
            S = s_tm[:, tau * 8:(tau + 1) * 8]
            mk1 = ctx.vector.tensor_reduce(m1n, S, AX.X, ALU.max, negate=True,
                                           wait=[mk])
            mk_gex = ctx.scalar.activation(out=gex, in_=S, func=AF.Exp,
                                           bias=m1n, wait=[mk1])
            mk_sh = ctx.scalar.activation(out=sh, in_=S, func=AF.Identity,
                                          bias=m1n, wait=[mk1])
            mk = ctx.vector.tensor_single_scalar(gmask, sh, 0.0, ALU.is_ge,
                                                 wait=[mk_sh])
            mk = ctx.vector.scalar_tensor_tensor(gm, gmask, -B_BIG, sh,
                                                 ALU.mult, ALU.add, wait=[mk])
            mk2 = ctx.vector.tensor_reduce(m2n, gm, AX.X, ALU.max, negate=True,
                                           wait=[mk])
            mk_gd = ctx.scalar.activation(out=gm, in_=sh, func=AF.Identity,
                                          bias=m2n, wait=[mk2, mk_gex])
            mk = ctx.vector.tensor_single_scalar(gmask, gm, 0.0, ALU.is_ge,
                                                 wait=[mk_gd])
            mk = ctx.vector.tensor_mul(gem, gex, gmask, wait=[mk])
            mkd = ctx.vector.tensor_reduce(den, gem, AX.X, ALU.add, wait=[mk])
            mk_d2 = ctx.scalar.activation(out=den2, in_=den, func=AF.Identity,
                                          wait=[mkd])
            mk_rd = ctx.vector.reciprocal(rden, den2, wait=[mk_d2])
            mk = ctx.scalar.activation(out=cw_tm[:, tau * 8:(tau + 1) * 8],
                                       in_=gem, func=AF.Identity, scale=rden,
                                       wait=[mk_rd, mkd])
        cw_done = mk
        if dbg is not None:
            ctx.sync.dma_start(out=dbg["cwtm"][p], in_=cw_tm[:, :], wait=[cw_done])
            ctx.sync.dma_start(out=dbg["stm"][p], in_=s_tm[:, :], wait=[cw_done])

        for tau in range(8):
            w = None
            if tau == 0:
                w = [cw_done]
                if free_cw:
                    w.append(free_cw)
            ctx.tensor.transpose(ps_cw[0:8, tau * 128:(tau + 1) * 128],
                                 cw_tm[:, tau * 8:(tau + 1) * 8],
                                 identf[:, :], wait=w, tinc=(tau == 7))
        mm_now = ctx.snap("T")
        free_cw = ctx.vector.tensor_copy(cw_fm[0:8, :], ps_cw[0:8, :], wait=mm_now)
        if dbg is not None:
            ctx.sync.dma_start(out=dbg["cw"][p], in_=cw_fm[0:8, :], wait=[free_cw])

        xhat_last = None
        for half in range(2):
            sq_last = None
            for c in range(NCH):
                xc = x_sb[:, c * TM + half * 512: c * TM + (half + 1) * 512]
                w = None
                if c == 0:
                    w = list(x_ready) + (stats_mm_prev or [])
                    if half == 0 and p >= 1 and st_prev:
                        w.append(st_prev)
                sq_last = ctx.scalar.activation(
                    out=sq_sb[:, c * 512:(c + 1) * 512], in_=xc, func=AF.Square,
                    wait=w)
            dep = [sq_last] + x_ready + (stats_mm_prev or [])
            mu, rstd = emit_ln_stats(
                ctx, 512, LN_EPS_B,
                lambda c: x_sb[:, c * TM + half * 512: c * TM + (half + 1) * 512],
                lambda c: sq_sb[:, c * 512:(c + 1) * 512],
                ones_sb, ps_s1, ps_s2, stat_sb, dep)
            stats_mm_prev = ctx.snap("T")
            rstd_done = ctx.snap("S")
            for c in range(NCH):
                xc = x_sb[:, c * TM + half * 512: c * TM + (half + 1) * 512]
                w = list(rstd_done) if c == 0 else None
                if c == 0 and p >= 1 and st_prev:
                    w.append(st_prev)
                mk = ctx.vector.tensor_sub(xh_scr[:, :], xc, mu, wait=w)
                xhat_last = ctx.vector.tensor_mul(
                    xhat_sb[:, c * TM + half * 512: c * TM + (half + 1) * 512],
                    xh_scr[:, :], rstd, wait=[mk])
        xhat_done = [xhat_last]
        if dbg is not None:
            for c in range(NCH):
                ctx.sync.dma_start(out=dbg["xhat"][p, c],
                                   in_=xhat_sb[:, c * TM:(c + 1) * TM],
                                   wait=(xhat_done if c == 0 else None))

        mm_w1_done = None
        y_marks = [None] * (NCH * NS)
        for e in range(NE):
            for c in range(NCH):
                w = None
                if c == 0:
                    w = list(xhat_done)
                    if mm_w1_done:
                        w += mm_w1_done
                    if stats_mm_prev:
                        w += stats_mm_prev
                ctx.vector.tensor_scalar(nrm_sb[:, c * TM:(c + 1) * TM],
                                         xhat_sb[:, c * TM:(c + 1) * TM],
                                         gamma2d[:, e * 8 + c:e * 8 + c + 1],
                                         beta2d[:, e * 8 + c:e * 8 + c + 1],
                                         ALU.mult, ALU.add, wait=w)
            nrm_done = ctx.snap("V")

            for ns in range(NS):
                w = None
                if ns == 0:
                    w = [free_cw]
                    if evac_prev_e:
                        w.append(evac_prev_e)
                ctx.tensor.matmul(ps_cw[:, ns * 512:(ns + 1) * 512],
                                  sel8[0:8, e * 128:(e + 1) * 128],
                                  cw_fm[0:8, ns * 512:(ns + 1) * 512],
                                  start=True, stop=True, wait=w, tinc=(ns == NS - 1))
            mm_now = ctx.snap("T")
            free_cw = ctx.scalar.activation(out=cwb_sb[:, :], in_=ps_cw[:, :],
                                            func=AF.Copy, wait=mm_now)

            for m in range(HCH):
                slot = m % 2
                iu = len(w1_users)
                w = w1_users[iu - 2] if iu >= 2 else None
                ctx.gpsimd.dma_start(out=w1slab[slot][:, :], in_=W["w1T"][e, m],
                                     wait=w)
                wl_ready = ctx.snap("G")
                for ns in range(NS):
                    for k in range(NCH):
                        w = None
                        if k == 0:
                            w = wl_ready + nrm_done
                            if free_h[ns]:
                                w = w + [free_h[ns]]
                        ctx.tensor.matmul(
                            ps_h[ns][:, :], w1slab[slot][:, k * 128:(k + 1) * 128],
                            nrm_sb[:, k * TM + ns * 512: k * TM + (ns + 1) * 512],
                            start=(k == 0), stop=(k == NCH - 1),
                            wait=w, tinc=(k == NCH - 1))
                    mm_now = ctx.snap("T")
                    free_h[ns] = ctx.scalar.activation(
                        out=h_sb[:, m * TM + ns * 512: m * TM + (ns + 1) * 512],
                        in_=ps_h[ns][:, :], func=AF.Relu,
                        bias=b1_2d[:, e * HCH + m:e * HCH + m + 1], wait=mm_now)
                w1_users.append(ctx.snap("T"))
            mm_w1_done = ctx.snap("T")
            h_done = ctx.snap("S")

            for j in range(NCH):
                slot = j % 2
                iu = len(w2_users)
                w = w2_users[iu - 2] if iu >= 2 else None
                ctx.gpsimd.dma_start(out=w2slab[slot][:, :], in_=W["w2T"][e, j],
                                     wait=w)
                wl_ready = ctx.snap("G")
                for ns in range(NS):
                    for k in range(HCH):
                        w = None
                        if k == 0:
                            w = wl_ready + h_done
                            if free_y[ns]:
                                w = w + [free_y[ns]]
                        ctx.tensor.matmul(
                            ps_y[ns][:, :], w2slab[slot][:, k * 128:(k + 1) * 128],
                            h_sb[:, k * TM + ns * 512: k * TM + (ns + 1) * 512],
                            start=(k == 0), stop=(k == HCH - 1),
                            wait=w, tinc=(k == HCH - 1))
                    mm_now = ctx.snap("T")
                    ycols = y_sb[:, j * TM + ns * 512: j * TM + (ns + 1) * 512]
                    b2col = b2_2d[:, e * NCH + j:e * NCH + j + 1]
                    cwcols = cwb_sb[:, ns * 512:(ns + 1) * 512]
                    if e == 0:
                        mk = ctx.vector.scalar_tensor_tensor(
                            ycols, ps_y[ns][:, :], b2col, cwcols,
                            ALU.add, ALU.mult, wait=mm_now + [free_cw])
                    else:
                        mk = ctx.vector.scalar_tensor_tensor(
                            tmp_sb[:, :], ps_y[ns][:, :], b2col, cwcols,
                            ALU.add, ALU.mult,
                            wait=mm_now + [free_cw, y_marks[j * NS + ns]])
                        mk = ctx.vector.tensor_add(ycols, ycols, tmp_sb[:, :],
                                                   wait=[mk])
                    free_y[ns] = mk
                    y_marks[j * NS + ns] = mk
                w2_users.append(ctx.snap("T"))
            evac_prev_e = free_y[1]
        mm_all = ctx.snap("T")
        if dbg is not None:
            for c in range(NCH):
                ctx.sync.dma_start(out=dbg["y"][p, c],
                                   in_=y_sb[:, c * TM:(c + 1) * TM],
                                   wait=([free_y[0], free_y[1]] if c == 0 else None))

        u_last = None
        for c in range(NCH):
            w = [y_marks[c * NS], y_marks[c * NS + 1]]
            if c == 0:
                w += mm_all
            u_last = ctx.vector.scalar_tensor_tensor(
                u_sb[:, c * TM:(c + 1) * TM], y_sb[:, c * TM:(c + 1) * TM],
                0.5, x_sb[:, c * TM:(c + 1) * TM], ALU.mult, ALU.add, wait=w)
        u_read_x = u_last

        for half in range(2):
            cm = []
            for c in range(NCH):
                ucols = u_sb[:, c * TM + half * 512: c * TM + (half + 1) * 512]
                w = [u_read_x]
                if c == 0:
                    w = w + stats_mm_prev
                cm.append(ctx.vector.tensor_copy(
                    nrm_sb[:, c * 512:(c + 1) * 512], ucols,
                    wait=(w if c == 0 else [u_read_x])))
                cm.append(ctx.scalar.activation(
                    out=sq_sb[:, c * 512:(c + 1) * 512], in_=ucols,
                    func=AF.Square, wait=(w if c == 0 else [u_read_x])))
            dep = [cm[-2], cm[-1]] + stats_mm_prev
            mu, rstd = emit_ln_stats(
                ctx, 512, LN_EPS_OUTER,
                lambda c: nrm_sb[:, c * 512:(c + 1) * 512],
                lambda c: sq_sb[:, c * 512:(c + 1) * 512],
                ones_sb, ps_s1, ps_s2, stat_sb, dep)
            stats_mm_prev = ctx.snap("T")
            rstd_done = ctx.snap("S")
            for c in range(NCH):
                ucols = u_sb[:, c * TM + half * 512: c * TM + (half + 1) * 512]
                ocols = xhat_sb[:, c * TM + half * 512: c * TM + (half + 1) * 512]
                w = list(rstd_done) if c == 0 else None
                mk = ctx.vector.tensor_sub(ucols, ucols, mu, wait=w)
                mk = ctx.vector.tensor_mul(ocols, ucols, rstd, wait=[mk])
                st_prev = ctx.sync.dma_start(
                    out=stream_out[s, c, :, t0 + half * 512: t0 + (half + 1) * 512],
                    in_=ocols, wait=[mk])
    ctx.sync.wait_ge("Y", ctx.c("Y"))


def prep_B(inputs):
    W = {}
    W1 = inputs["L0_ffn_W1"]
    W2 = inputs["L0_ffn_W2"]
    W["w1T"] = np.stack([tiles_lhsT(W1[e]) for e in range(8)])
    W["w2T"] = np.stack([tiles_lhsT(W2[e]) for e in range(8)])
    Wg = inputs["L0_ffn_Wg"]
    W["gate_w"] = np.ascontiguousarray(
        Wg.reshape(8, 128, 8).transpose(1, 0, 2).reshape(128, 64))
    W["gamma2d"] = np.ascontiguousarray(
        inputs["L0_ffn_gamma"].reshape(8, 8, 128).transpose(2, 0, 1).reshape(128, 64))
    W["beta2d"] = np.ascontiguousarray(
        inputs["L0_ffn_beta"].reshape(8, 8, 128).transpose(2, 0, 1).reshape(128, 64))
    W["b1_2d"] = np.ascontiguousarray(
        inputs["L0_ffn_b1"].reshape(8, 32, 128).transpose(2, 0, 1).reshape(128, 256))
    W["b2_2d"] = np.ascontiguousarray(
        inputs["L0_ffn_b2"].reshape(8, 8, 128).transpose(2, 0, 1).reshape(128, 64))
    return W


def declare_B_weights(nc):
    aps = {}
    aps["w1T"] = nc.dram_tensor("Bw1T", [8, 32, 128, 1024], F32, kind="ExternalInput").ap()
    aps["w2T"] = nc.dram_tensor("Bw2T", [8, 8, 128, 4096], F32, kind="ExternalInput").ap()
    aps["gate_w"] = nc.dram_tensor("Bgate_w", [128, 64], F32, kind="ExternalInput").ap()
    for n, cols in (("gamma2d", 64), ("beta2d", 64), ("b1_2d", 256), ("b2_2d", 64)):
        aps[n] = nc.dram_tensor("B" + n, [128, cols], F32, kind="ExternalInput").ap()
    return aps


def build_B_standalone(debug=False):
    nc = bass.Bass("TRN2", target_bir_lowering=False)
    stream_in = nc.dram_tensor("stream_in", [SEQ, NCH, 128, T], BF16,
                               kind="ExternalInput").ap()
    stream_out = nc.dram_tensor("stream_out", [SEQ, NCH, 128, T], BF16,
                                kind="ExternalOutput").ap()
    dbg = None
    if debug:
        dbg = {
            "cwtm": nc.dram_tensor("dbg_cwtm", [8, 128, 64], F32, kind="ExternalOutput").ap(),
            "stm": nc.dram_tensor("dbg_stm", [8, 128, 64], F32, kind="ExternalOutput").ap(),
            "cw": nc.dram_tensor("dbg_cw", [8, 8, 1024], BF16, kind="ExternalOutput").ap(),
            "xhat": nc.dram_tensor("dbg_xhat", [8, NCH, 128, 1024], BF16, kind="ExternalOutput").ap(),
            "y": nc.dram_tensor("dbg_y", [8, NCH, 128, 1024], F32, kind="ExternalOutput").ap(),
        }
    with contextlib.ExitStack() as es:
        ctx = Ctx(nc, es)
        ctx.init_epscols()
        W = declare_B_weights(nc)
        consts = declare_consts(nc)
        ctx.push_scope()
        emit_phase_B(ctx, W, stream_in, stream_out, consts, dbg=dbg)
        ctx.flush("B")
        ctx.pop_scope()
    return nc


def run_B(inputs, stream_bf16_percore, debug=False):
    nc = build_B_standalone(debug)
    W = dmap_prefix(prep_B(inputs), "B")
    in_maps = []
    for c in range(NCORES):
        m = dict(W)
        m.update(consts_np())
        m["stream_in"] = stream_bf16_percore[c]
        in_maps.append(m)
    res = run_bass_kernel_spmd(nc, in_maps, core_ids=list(range(NCORES)))
    if debug:
        return res.results
    return [r["stream_out"] for r in res.results]



C_ES = 64.0 ** (-0.25)
C_SECTIONS = {"p1", "p2", "p3"}
C_WAVES = (0, 1, 2, 3)
C_INJECT = False
NBUK = T // 64
NTT = T // 128


def emit_phase_C_consts(ctx, W, consts):
    H = {}
    H["selq16"] = ctx.sbuf("Cselq", (128, 128), BF16)
    H["selb16"] = ctx.sbuf("Cselb", (16, 1024), BF16)
    H["b2ind"] = ctx.sbuf("Cb2i", (128, 2), BF16)
    H["ones"] = ctx.sbuf("Cones", (128, 128), BF16)
    H["ones32"] = ctx.sbuf("Cones32", (128, 32), F32)
    H["bo2d"] = ctx.sbuf("Cbo", (128, 8), F32)
    ctx.sync.dma_start(out=H["selq16"][:, :], in_=consts["selq16"])
    ctx.sync.dma_start(out=H["selb16"][:, :], in_=consts["selb16"])
    ctx.sync.dma_start(out=H["b2ind"][:, :], in_=consts["b2ind"])
    ctx.sync.dma_start(out=H["ones"][:, :], in_=consts["ones128"])
    ctx.sync.dma_start(out=H["bo2d"][:, :], in_=W["bo2d"])
    ctx.vector.memset(H["ones32"][:, :], 1.0)
    H["ready"] = ctx.snap("Y", "V")
    return H


def emit_phase_C_seq(ctx, W, H, stream_in, stream_out, s, dbg=None):
    attn_sb = ctx.sbuf("Cattn", (128, NCH * T), BF16)
    ksum_sb = ctx.sbuf("Cks", (128, 256), F32)
    kcum_sb = ctx.sbuf("Ckc", (128, 256), F32)
    rkc_sb = ctx.sbuf("Crkc", (128, 256), F32)

    ps_a = ctx.psum("Cpsa", (128, 512), F32)
    ps_b = ctx.psum("Cpsb", (128, 512), F32)

    ctx.push_scope()
    qe_sb = ctx.sbuf("Cqe", (128, NCH * T), BF16)
    fk_sb = ctx.sbuf("Cfk", (128, NTT * 1024), BF16)
    v_sb = ctx.sbuf("Cv", (128, NTT * 1024), BF16)

    ctx.push_scope()
    x_sb = ctx.sbuf("Cx", (128, NCH * T), BF16)
    wq_sb = [ctx.sbuf(f"Cwq{i}", (128, 1024), BF16) for i in range(2)]
    wkv_sb = ctx.sbuf("Cwkv", (128, NCH * 1024), BF16)
    rq_sb = ctx.sbuf("Crq", (16, T), BF16)
    rq32_sb = ctx.sbuf("Crq32", (16, 512), F32)
    ks_scr = ctx.sbuf("Ckss", (128, 6 * 512), F32)

    for c in range(NCH):
        ctx.sync.dma_start(out=x_sb[:, c * T:(c + 1) * T],
                           in_=stream_in[s, c, :, 0:T],
                           wait=(H["ready"] if c == 0 else None))
    x_ready = ctx.snap("Y")

    if "p1" in C_SECTIONS:
        evac_prev = [None, None]
        for m in range(NCH):
            slot = m % 2
            w = evac_prev[slot]
            ctx.gpsimd.dma_start(out=wq_sb[slot][:, :], in_=W["wqT"][m],
                                 wait=([w] if w else None))
            wl = ctx.snap("G")
            for ns in range(4):
                pslot = ns % 2
                pp = [ps_a, ps_b][pslot]
                for k in range(NCH):
                    w = None
                    if k == 0:
                        w = wl + x_ready
                        if evac_prev[pslot]:
                            w = w + [evac_prev[pslot]]
                    ctx.tensor.matmul(pp[:, :], wq_sb[slot][:, k * 128:(k + 1) * 128],
                                      x_sb[:, k * T + ns * 512: k * T + (ns + 1) * 512],
                                      start=(k == 0), stop=(k == NCH - 1),
                                      wait=w, tinc=(k == NCH - 1))
                mm_now = ctx.snap("T")
                evac_prev[pslot] = ctx.scalar.activation(
                    out=qe_sb[:, m * T + ns * 512: m * T + (ns + 1) * 512],
                    in_=pp[:, :], func=AF.Exp, scale=C_ES, wait=mm_now)
        qe_done = ctx.snap("S")

        rq_marks = []
        for ns in range(4):
            pslot = ns % 2
            pp = [ps_a, ps_b][pslot]
            for c in range(NCH):
                w = None
                if c == 0:
                    w = list(qe_done)
                    if ns < 2:
                        w += [evac_prev[0], evac_prev[1]]
                    if ns >= 2:
                        w += [rq_marks[ns - 2]]
                ctx.tensor.matmul(pp[0:16, :], H["selq16"][:, c * 16:(c + 1) * 16],
                                  qe_sb[:, c * T + ns * 512: c * T + (ns + 1) * 512],
                                  start=(c == 0), stop=(c == NCH - 1),
                                  wait=w, tinc=(c == NCH - 1))
            mm_now = ctx.snap("T")
            w = list(mm_now)
            if rq_marks:
                w.append(rq_marks[-1])
            mkr = ctx.vector.reciprocal(rq32_sb[0:16, :], pp[0:16, :], wait=w)
            rq_marks.append(ctx.scalar.activation(
                out=rq_sb[0:16, ns * 512:(ns + 1) * 512], in_=rq32_sb[0:16, :],
                func=AF.Identity, wait=[mkr]))

        rb_sb = ctx.sbuf("Crb", (128, 2 * 512), BF16)
        sq_marks = []
        cp_marks = []
        for c in range(NCH):
            for ns in range(4):
                pslot = ns % 2
                pp = [ps_a, ps_b][pslot]
                idx = c * 4 + ns
                w = [rq_marks[ns]]
                if idx >= 2:
                    w.append(cp_marks[idx - 2])
                else:
                    w.append(rq_marks[ns + 2])
                ctx.tensor.matmul(pp[:, :], H["selb16"][0:16, c * 128:(c + 1) * 128],
                                  rq_sb[0:16, ns * 512:(ns + 1) * 512],
                                  start=True, stop=True, wait=w, tinc=True)
                mm_now = ctx.snap("T")
                rbc = rb_sb[:, pslot * 512:(pslot + 1) * 512]
                w2 = list(mm_now)
                if idx >= 2:
                    w2.append(sq_marks[idx - 2])
                cp = ctx.scalar.activation(out=rbc, in_=pp[:, :], func=AF.Identity,
                                           wait=w2)
                cp_marks.append(cp)
                cols = qe_sb[:, c * T + ns * 512: c * T + (ns + 1) * 512]
                sq_marks.append(ctx.vector.tensor_mul(cols, cols, rbc, wait=[cp]))
        sq_done = [sq_marks[-1], sq_marks[-2]]

        if dbg is not None and s == 0:
            ctx.sync.dma_start(out=dbg["rq"], in_=rq_sb[0:16, :], wait=sq_done)
            for c in range(NCH):
                ctx.sync.dma_start(out=dbg["sq"][c], in_=qe_sb[:, c * T:(c + 1) * T],
                                   wait=(sq_done if c == 0 else None))

    else:
        sq_done = list(x_ready)
    if "p2" in C_SECTIONS:
        for which, wsrc, dst in (("k", W["wkT_rhs"], fk_sb), ("v", W["wvT_rhs"], v_sb)):
            wv_gate = ctx.snap("T") if which == "v" else None
            for c in range(NCH):
                ctx.gpsimd.dma_start(out=wkv_sb[:, c * 1024:(c + 1) * 1024],
                                     in_=wsrc[c],
                                     wait=(wv_gate if c == 0 else None))
            wl = ctx.snap("G")
            ev_pr = [None, None]
            for m in range(NTT):
                for osl in range(2):
                    pslot = (m * 2 + osl) % 2
                    pp = [ps_a, ps_b][pslot]
                    for k in range(NCH):
                        w = None
                        if k == 0:
                            w = wl + x_ready
                            if which == "k" and m == 0 and osl == 0:
                                w = w + sq_done
                            if ev_pr[pslot]:
                                w = w + [ev_pr[pslot]]
                        ctx.tensor.matmul(
                            pp[:, :],
                            x_sb[:, k * T + m * 128: k * T + (m + 1) * 128],
                            wkv_sb[:, k * 1024 + osl * 512: k * 1024 + (osl + 1) * 512],
                            start=(k == 0), stop=(k == NCH - 1),
                            wait=w, tinc=(k == NCH - 1))
                    mm_now = ctx.snap("T")
                    dcols = dst[:, m * 1024 + osl * 512: m * 1024 + (osl + 1) * 512]
                    if which == "v":
                        ev_pr[pslot] = ctx.scalar.activation(
                            out=dcols, in_=pp[:, :], func=AF.Identity, wait=mm_now)
                    else:
                        off = pslot * 1536
                        ks = ks_scr[:, off:off + 512]
                        t2 = ks_scr[:, off + 512:off + 1024]
                        t3 = ks_scr[:, off + 1024:off + 1536]
                        a1 = ctx.scalar.activation(out=ks, in_=pp[:, :],
                                                   func=AF.Identity, scale=C_ES,
                                                   wait=mm_now + ([ev_pr[pslot]] if ev_pr[pslot] else []))
                        a2 = ctx.vector.tensor_single_scalar(t2, ks, 0.0, ALU.min,
                                                             wait=[a1])
                        a3 = ctx.scalar.activation(out=t3, in_=t2, func=AF.Exp,
                                                   wait=[a2])
                        a4 = ctx.vector.tensor_single_scalar(t2, ks, 0.0, ALU.max,
                                                             wait=[a1, a3])
                        ev_pr[pslot] = ctx.vector.tensor_add(dcols, t3, t2,
                                                             wait=[a4, a3])
            if which == "k":
                fk_done = [ev_pr[0], ev_pr[1]]
            else:
                v_done = [ev_pr[0], ev_pr[1]]

    else:
        fk_done = list(x_ready)
        v_done = list(x_ready)
    if dbg is not None and s == 0:
        for m in range(NTT):
            ctx.sync.dma_start(out=dbg["fk"][m], in_=fk_sb[:, m * 1024:(m + 1) * 1024],
                               wait=(fk_done + v_done if m == 0 else None))
            ctx.sync.dma_start(out=dbg["v"][m], in_=v_sb[:, m * 1024:(m + 1) * 1024])
    ctx.pop_scope()

    if "p3" in C_SECTIONS:
        for c in range(NCH):
            for m in range(NTT):
                w = None
                if c == 0 and m == 0:
                    w = fk_done + v_done
                ctx.tensor.matmul(ps_a[:, c * 32 + 2 * m: c * 32 + 2 * m + 2],
                                  fk_sb[:, m * 1024 + c * 128: m * 1024 + (c + 1) * 128],
                                  H["b2ind"][:, :], start=True, stop=True,
                                  wait=w, tinc=(c == NCH - 1 and m == NTT - 1))
        mm_now = ctx.snap("T")
        mkk = ctx.scalar.activation(out=ksum_sb[:, :], in_=ps_a[:, 0:256],
                                    func=AF.Identity, wait=mm_now)
        scan_mk = mkk
        for c in range(NCH):
            scan_mk = ctx.vector.tensor_tensor_scan(
                kcum_sb[:, c * 32:(c + 1) * 32], ksum_sb[:, c * 32:(c + 1) * 32],
                H["ones32"][:, 0:32], 0.0, ALU.add, ALU.mult,
                wait=([mkk] if c == 0 else None))
        mke = ctx.vector.tensor_single_scalar(kcum_sb[:, :], kcum_sb[:, :], CTX_EPS,
                                              ALU.add, wait=[scan_mk])
        mkr = ctx.vector.reciprocal(rkc_sb[:, :], kcum_sb[:, :], wait=[mke])
        rkc_done = [mkr]
        if dbg is not None and s == 0:
            ctx.sync.dma_start(out=dbg["rkc"], in_=rkc_sb[:, :], wait=[mkr])


    else:
        rkc_done = list(x_ready)
    if dbg is not None and C_INJECT and s == 0:
        mk = None
        w0 = sq_done + fk_done + v_done + rkc_done
        for c in range(NCH):
            mk = ctx.gpsimd.dma_start(out=qe_sb[:, c * T:(c + 1) * T],
                                      in_=dbg["inj_sq"][c],
                                      wait=(w0 if c == 0 else None))
        for m in range(NTT):
            mk = ctx.gpsimd.dma_start(out=fk_sb[:, m * 1024:(m + 1) * 1024],
                                      in_=dbg["inj_fk"][m])
            mk = ctx.gpsimd.dma_start(out=v_sb[:, m * 1024:(m + 1) * 1024],
                                      in_=dbg["inj_v"][m])
        mk2 = ctx.sync.dma_start(out=rkc_sb[:, :], in_=dbg["inj_rkc"])
        sq_done = [mk, mk2]
        fk_done = [mk]
        v_done = [mk]
        rkc_done = [mk2]
    z_marks = []
    for c in range(NCH):
        z_marks.append(ctx.vector.memset(attn_sb[:, c * T:c * T + 64], 0.0))
    zero_done = [z_marks[-1]]

    ctx.push_scope()
    ctx_ps = [ctx.psum(f"Cctx{i}", (128, 64), F32) for i in range(4)]
    ps_c = ctx.psum("Cpsc", (128, 512), F32)
    ps_d = ctx.psum("Cpsd", (128, 512), F32)
    aps = [ps_a, ps_b, ps_c, ps_d]
    ctxt_sb = ctx.sbuf("Cctxt", (128, 4 * 64), BF16)
    free_attn = [None, None, None, None]
    wave_end = []

    for wave in C_WAVES:
        chunks = [2 * wave, 2 * wave + 1]
        heads = [2 * c + par for c in chunks for par in range(2)]
        ev_ctx = {h: None for h in heads}
        mm_ctx = {h: None for h in heads}
        mm_attn_prev = {h: None for h in heads}
        for u in range(NBUK):
            for hi, h in enumerate(heads):
                c = h // 2
                ci = chunks.index(c)
                par = h % 2
                bank = ctx_ps[ci * 2 + par]
                rowsl = slice(par * 64, par * 64 + 64)
                tcols = slice((u // 2) * 1024 + h * 64, (u // 2) * 1024 + h * 64 + 64)
                trows = slice((u % 2) * 64, (u % 2) * 64 + 64)
                w = []
                if ev_ctx[h]:
                    w.append(ev_ctx[h])
                if u == 0:
                    w = w + fk_done + v_done + zero_done
                    if wave_end:
                        w = w + wave_end
                mm_ctx[h] = ctx.tensor.matmul(
                    bank[rowsl, :],
                    fk_sb[trows, tcols], v_sb[trows, tcols],
                    start=(u == 0), stop=(u == NBUK - 1),
                    wait=(w or None), tinc=True,
                    tile_position=((u % 2) * 64, par * 64))
                if u < NBUK - 1:
                    ev_ctx[h] = ctx.scalar.activation(
                        out=ctxt_sb[rowsl, (ci * 2 + par) * 64:
                                    (ci * 2 + par) * 64 + 64],
                        in_=bank[rowsl, :], func=AF.Identity,
                        scale=rkc_sb[rowsl.start:rowsl.start + 64,
                                     c * 32 + u: c * 32 + u + 1],
                        wait=[mm_ctx[h]] + (rkc_done if u == 0 else []))
                    if dbg is not None and s == 0 and wave == 0 and h == 0 and u == 0:
                        dmk = ctx.sync.dma_start(out=dbg["ctxt0"],
                                                 in_=ctxt_sb[0:64, 0:64],
                                                 wait=[ev_ctx[h]])
                        ev_ctx[h] = dmk
                    au = u + 1
                    aslot = ((au - 1) // 8) % 2
                    pa = aps[ci * 2 + aslot]
                    w = [ev_ctx[h]]
                    if (au - 1) % 8 == 0 and par == 0 and free_attn[ci * 2 + aslot]:
                        w.append(free_attn[ci * 2 + aslot])
                    if hi == 0 and u == 0 and not wave_end:
                        w = w + sq_done + rkc_done
                    mm_attn_prev[h] = ctx.tensor.matmul(
                        pa[rowsl, ((au - 1) % 8) * 64: ((au - 1) % 8) * 64 + 64],
                        ctxt_sb[rowsl, (ci * 2 + par) * 64:
                                (ci * 2 + par) * 64 + 64],
                        qe_sb[rowsl, c * T + au * 64: c * T + (au + 1) * 64],
                        start=True, stop=True, wait=w, tinc=True,
                        tile_position=(par * 64, par * 64))
            if u >= 8 and (u % 8 == 0):
                g = u // 8 - 1
                aslot = g % 2
                for ci, c in enumerate(chunks):
                    pa = aps[ci * 2 + aslot]
                    wv = [mm_attn_prev[2 * c], mm_attn_prev[2 * c + 1]]
                    free_attn[ci * 2 + aslot] = ctx.vector.tensor_copy(
                        attn_sb[:, c * T + (8 * g + 1) * 64: c * T + (8 * g + 9) * 64],
                        pa[:, :], wait=wv)
        g = 3
        aslot = g % 2
        wave_end = []
        for ci, c in enumerate(chunks):
            pa = aps[ci * 2 + aslot]
            wv = [mm_attn_prev[2 * c], mm_attn_prev[2 * c + 1]]
            last = ctx.vector.tensor_copy(
                attn_sb[:, c * T + (8 * g + 1) * 64: c * T + (8 * g + 8) * 64],
                pa[:, 0:448], wait=wv)
            free_attn[ci * 2 + aslot] = last
            wave_end.append(last)
        wave_end = wave_end + [mm_ctx[h] for h in heads]
    ctx.pop_scope()
    if dbg is not None and s == 0:
        for c in range(NCH):
            ctx.sync.dma_start(out=dbg["attn"][c], in_=attn_sb[:, c * T:(c + 1) * T],
                               wait=(ctx.snap("V") if c == 0 else None))
    ctx.pop_scope()
    attn_done = ctx.snap("V", "T")

    ctx.push_scope()
    wo_sb = ctx.sbuf("Cwo", (128, NCH * 1024), BF16)
    u_sb = ctx.sbuf("Cu", (128, NCH * 512), F32)
    xt_sb = ctx.sbuf("Cxt", (128, NCH * 512), BF16)
    ubf_sb = ctx.sbuf("Cubf", (128, NCH * 512), BF16)
    usq_sb = ctx.sbuf("Cusq", (128, NCH * 512), BF16)
    stat_sb = ctx.sbuf("Cstat", (128, 4 * 512), F32)
    out_sb = ctx.sbuf("Cout", (128, NCH * 512), BF16)
    for m in range(NCH):
        ctx.gpsimd.dma_start(out=wo_sb[:, m * 1024:(m + 1) * 1024],
                             in_=W["woT"][m],
                             wait=(attn_done if m == 0 else None))
    wl = ctx.snap("G")
    stats_mm_prev = None
    norm_prev = None
    st_prev = None
    for ns in range(4):
        xw = list(attn_done)
        if st_prev:
            xw = xw + [st_prev]
        for c in range(NCH):
            ctx.sync.dma_start(out=xt_sb[:, c * 512:(c + 1) * 512],
                               in_=stream_in[s, c, :, ns * 512:(ns + 1) * 512],
                               wait=(xw if c == 0 else None))
        xt_ready = ctx.snap("Y")
        u_marks = []
        for m in range(NCH):
            pslot = m % 2
            pp = [ps_a, ps_b][pslot]
            for k in range(NCH):
                w = None
                if k == 0:
                    w = wl + xt_ready + attn_done
                    if m >= 2:
                        w = w + [u_marks[m - 2]]
                    elif norm_prev:
                        w = w + norm_prev
                ctx.tensor.matmul(pp[:, :],
                                  wo_sb[:, m * 1024 + k * 128: m * 1024 + (k + 1) * 128],
                                  attn_sb[:, k * T + ns * 512: k * T + (ns + 1) * 512],
                                  start=(k == 0), stop=(k == NCH - 1),
                                  wait=w, tinc=(k == NCH - 1))
            mm_now = ctx.snap("T")
            u_marks.append(ctx.vector.scalar_tensor_tensor(
                u_sb[:, m * 512:(m + 1) * 512], pp[:, :],
                H["bo2d"][:, m:m + 1], xt_sb[:, m * 512:(m + 1) * 512],
                ALU.add, ALU.add, wait=mm_now))
        dep_u = [u_marks[-1]]
        cm = []
        for c in range(NCH):
            ucols = u_sb[:, c * 512:(c + 1) * 512]
            w = list(dep_u)
            if c == 0 and stats_mm_prev:
                w += stats_mm_prev
            cm.append(ctx.vector.tensor_copy(ubf_sb[:, c * 512:(c + 1) * 512],
                                             ucols, wait=(w if c == 0 else dep_u)))
            cm.append(ctx.scalar.activation(out=usq_sb[:, c * 512:(c + 1) * 512],
                                            in_=ucols, func=AF.Square,
                                            wait=(w if c == 0 else dep_u)))
        dep = [cm[-2], cm[-1]]
        mu, rstd = emit_ln_stats(ctx, 512, LN_EPS_OUTER,
                                 lambda c: ubf_sb[:, c * 512:(c + 1) * 512],
                                 lambda c: usq_sb[:, c * 512:(c + 1) * 512],
                                 H["ones"], ps_a, ps_b, stat_sb, dep)
        stats_mm_prev = ctx.snap("T")
        rstd_done = ctx.snap("S")
        last = None
        for c in range(NCH):
            ucols = u_sb[:, c * 512:(c + 1) * 512]
            w = list(rstd_done) if c == 0 else None
            mk = ctx.vector.tensor_sub(ucols, ucols, mu, wait=w)
            last = ctx.vector.tensor_mul(out_sb[:, c * 512:(c + 1) * 512],
                                         ucols, rstd, wait=[mk])
        norm_prev = [last] + stats_mm_prev
        for c in range(NCH):
            st_prev = ctx.sync.dma_start(
                out=stream_out[s, c, :, ns * 512:(ns + 1) * 512],
                in_=out_sb[:, c * 512:(c + 1) * 512],
                wait=([last] if c == 0 else None))
    ctx.pop_scope()
    ctx.sync.wait_ge("Y", ctx.c("Y"))


def prep_C(inputs):
    W = {}
    W["wqT"] = tiles_lhsT(np.ascontiguousarray(inputs["L1_attn_Wq"].T))
    W["woT"] = tiles_lhsT(np.ascontiguousarray(inputs["L1_attn_Wo"].T))
    W["wkT_rhs"] = np.ascontiguousarray(inputs["L1_attn_Wk"].T).reshape(8, 128, 1024)
    W["wvT_rhs"] = np.ascontiguousarray(inputs["L1_attn_Wv"].T).reshape(8, 128, 1024)
    W["bo2d"] = vec2d(inputs["L1_attn_bo"])
    return W


def consts_C_np():
    selq = np.zeros((128, 128), dtype=BF16NP)
    selb = np.zeros((16, 1024), dtype=BF16NP)
    for c in range(8):
        for p in range(128):
            h = 2 * c + (p // 64)
            selq[p, c * 16 + h] = 1
            selb[h, c * 128 + p] = 1
    b2 = np.zeros((128, 2), dtype=BF16NP)
    b2[0:64, 0] = 1
    b2[64:128, 1] = 1
    return {"selq16": selq, "selb16": selb, "b2ind": b2}


def declare_C_consts(nc):
    return {"selq16": nc.dram_tensor("selq16", [128, 128], BF16, kind="ExternalInput").ap(),
            "selb16": nc.dram_tensor("selb16", [16, 1024], BF16, kind="ExternalInput").ap(),
            "b2ind": nc.dram_tensor("b2ind", [128, 2], BF16, kind="ExternalInput").ap()}


def declare_C_weights(nc):
    aps = {}
    aps["wqT"] = nc.dram_tensor("CwqT", [8, 128, 1024], F32, kind="ExternalInput").ap()
    aps["woT"] = nc.dram_tensor("CwoT", [8, 128, 1024], F32, kind="ExternalInput").ap()
    aps["wkT_rhs"] = nc.dram_tensor("CwkT_rhs", [8, 128, 1024], F32, kind="ExternalInput").ap()
    aps["wvT_rhs"] = nc.dram_tensor("CwvT_rhs", [8, 128, 1024], F32, kind="ExternalInput").ap()
    aps["bo2d"] = nc.dram_tensor("Cbo2d", [128, 8], F32, kind="ExternalInput").ap()
    return aps


def build_C_standalone(debug=False):
    nc = bass.Bass("TRN2", target_bir_lowering=False)
    stream_in = nc.dram_tensor("stream_in", [SEQ, NCH, 128, T], BF16,
                               kind="ExternalInput").ap()
    stream_out = nc.dram_tensor("stream_out", [SEQ, NCH, 128, T], BF16,
                                kind="ExternalOutput").ap()
    dbg = None
    if debug:
        dbg = {
            "sq": nc.dram_tensor("dbg_sq", [NCH, 128, T], BF16, kind="ExternalOutput").ap(),
            "fk": nc.dram_tensor("dbg_fk", [NTT, 128, 1024], BF16, kind="ExternalOutput").ap(),
            "v": nc.dram_tensor("dbg_v", [NTT, 128, 1024], BF16, kind="ExternalOutput").ap(),
            "rkc": nc.dram_tensor("dbg_rkc", [128, 256], F32, kind="ExternalOutput").ap(),
            "rq": nc.dram_tensor("dbg_rq", [16, T], BF16, kind="ExternalOutput").ap(),
            "ctxt0": nc.dram_tensor("dbg_ctxt0", [64, 64], BF16, kind="ExternalOutput").ap(),
            "inj_sq": nc.dram_tensor("inj_sq", [NCH, 128, T], BF16, kind="ExternalInput").ap(),
            "inj_fk": nc.dram_tensor("inj_fk", [NTT, 128, 1024], BF16, kind="ExternalInput").ap(),
            "inj_v": nc.dram_tensor("inj_v", [NTT, 128, 1024], BF16, kind="ExternalInput").ap(),
            "inj_rkc": nc.dram_tensor("inj_rkc", [128, 256], F32, kind="ExternalInput").ap(),

            "attn": nc.dram_tensor("dbg_attn", [NCH, 128, T], BF16, kind="ExternalOutput").ap(),
        }
    with contextlib.ExitStack() as es:
        ctx = Ctx(nc, es)
        ctx.init_epscols()
        W = declare_C_weights(nc)
        consts = declare_consts(nc)
        consts.update(declare_C_consts(nc))
        ctx.push_scope()
        H = emit_phase_C_consts(ctx, W, consts)
        for s in range(SEQ):
            ctx.push_scope()
            emit_phase_C_seq(ctx, W, H, stream_in, stream_out, s, dbg=dbg)
            ctx.flush(f"C_s{s}")
            ctx.pop_scope()
        ctx.pop_scope()
    return nc


def run_C(inputs, stream_bf16_percore, debug=False):
    nc = build_C_standalone(debug)
    W = dmap_prefix(prep_C(inputs), "C")
    in_maps = []
    for c in range(NCORES):
        m = dict(W)
        m.update(consts_np())
        m.update(consts_C_np())
        m["stream_in"] = stream_bf16_percore[c]
        if debug:
            m["inj_sq"] = np.zeros((NCH, 128, T), BF16NP)
            m["inj_fk"] = np.zeros((NTT, 128, 1024), BF16NP)
            m["inj_v"] = np.zeros((NTT, 128, 1024), BF16NP)
            m["inj_rkc"] = np.zeros((128, 256), np.float32)
        in_maps.append(m)
    res = run_bass_kernel_spmd(nc, in_maps, core_ids=list(range(NCORES)))
    if debug:
        return res.results
    return [r["stream_out"] for r in res.results]



A_SCALE = 64.0 ** (-0.5)


def emit_phase_A_consts(ctx, W, consts):
    H = {}
    H["ident"] = ctx.sbuf("Aid", (128, 128), BF16)
    H["ones"] = ctx.sbuf("Aones", (128, 128), BF16)
    H["b2d"] = ctx.sbuf("Ab2d", (128, 8 + 8 + 8 + 8), F32)
    H["bqk2d"] = H["b2d"][:, 0:8]
    H["bout2d"] = H["b2d"][:, 8:16]
    H["g2d"] = H["b2d"][:, 16:24]
    H["be2d"] = H["b2d"][:, 24:32]
    H["bvrow"] = ctx.sbuf("Abv", (1, 512), BF16)
    H["onesk1"] = ctx.sbuf("Ak1", (1, 128), BF16)
    H["bvrep"] = ctx.sbuf("Abvr", (128, 512), F32)
    ctx.sync.dma_start(out=H["ident"][:, :], in_=consts["ident128b"])
    ctx.sync.dma_start(out=H["ones"][:, :], in_=consts["ones128"])
    ctx.sync.dma_start(out=H["b2d"][:, 0:8], in_=W["bqk2d"])
    ctx.sync.dma_start(out=H["b2d"][:, 8:16], in_=W["bout2d"])
    ctx.sync.dma_start(out=H["b2d"][:, 16:24], in_=W["g2d"])
    ctx.sync.dma_start(out=H["b2d"][:, 24:32], in_=W["be2d"])
    ctx.gpsimd.dma_start(out=H["bvrow"][0:1, :], in_=W["bv_row"])
    ctx.gpsimd.dma_start(out=H["onesk1"][0:1, :], in_=consts["ones128"][0:1, :])
    ld = ctx.snap("Y", "G")
    H["wqk"] = ctx.sbuf("Awqk", (128, NCH * 1024), BF16)
    H["wv"] = ctx.sbuf("Awv", (128, NCH * 512), BF16)
    H["wout"] = ctx.sbuf("Awout", (128, 4 * 1024), BF16)
    for c in range(NCH):
        ctx.gpsimd.dma_start(out=H["wqk"][:, c * 1024:(c + 1) * 1024],
                             in_=W["wqkT"][c])
        ctx.gpsimd.dma_start(out=H["wv"][:, c * 512:(c + 1) * 512],
                             in_=W["wv_rhs"][c])
    for m in range(NCH):
        ctx.gpsimd.dma_start(out=H["wout"][:, m * 512:(m + 1) * 512],
                             in_=W["woutT"][m])
    H["ready"] = ctx.snap("Y", "G")
    ctx.push_scope()
    ps = ctx.psum("Abvps", (128, 512), F32)
    mk = ctx.tensor.matmul(ps[:, :], H["onesk1"][0:1, :], H["bvrow"][0:1, :],
                           start=True, stop=True, wait=ld, tinc=True)
    H["bv_done"] = [ctx.scalar.activation(out=H["bvrep"][:, :], in_=ps[:, :],
                                          func=AF.Identity, wait=[mk])]
    ctx.pop_scope()
    return H


def emit_phase_A_seq(ctx, W, H, x_fm, stream_out, s):
    qk_sb = ctx.sbuf("Aqk", (128, NCH * T), BF16)
    v_sb = ctx.sbuf("Av", (128, NTT * 512), BF16)
    o_fm = ctx.sbuf("Aofm", (128, 4 * T), BF16)

    ps_s = [ctx.psum(f"Aps{i}", (128, 512), F32) for i in range(4)]
    ps_t = [ctx.psum(f"Apt{i}", (128, 512), BF16) for i in range(2)]
    ps_pv = ctx.psum("Apv", (128, 64), F32)
    ps_ot = ctx.psum("Aot", (128, 128), BF16)

    ctx.push_scope()
    x_sb = ctx.sbuf("Ax", (128, NCH * T), BF16)
    for c in range(NCH):
        ctx.gpsimd.dma_start(out=x_sb[:, c * T:(c + 1) * T],
                             in_=x_fm[s, c, :, 0:T],
                             wait=(H["ready"] if c == 0 else None))
    x_ready = ctx.snap("G")

    ev_pr = [None, None, None, None]
    for m in range(NCH):
        for ns in range(4):
            pslot = ns % 4
            pp = ps_s[pslot]
            for k in range(NCH):
                w = None
                if k == 0:
                    w = list(x_ready) + H["bv_done"]
                    if ev_pr[pslot]:
                        w = w + [ev_pr[pslot]]
                ctx.tensor.matmul(pp[:, :],
                                  H["wqk"][:, m * 1024 + k * 128: m * 1024 + (k + 1) * 128],
                                  x_sb[:, k * T + ns * 512: k * T + (ns + 1) * 512],
                                  start=(k == 0), stop=(k == NCH - 1),
                                  wait=w, tinc=(k == NCH - 1))
            mm_now = ctx.snap("T")
            ev_pr[pslot] = ctx.scalar.activation(
                out=qk_sb[:, m * T + ns * 512: m * T + (ns + 1) * 512],
                in_=pp[:, :], func=AF.Identity,
                bias=H["bqk2d"][:, m:m + 1], wait=mm_now)
    qk_done = ctx.snap("S")

    for mt in range(NTT):
        pslot = mt % 4
        pp = ps_s[pslot]
        for k in range(NCH):
            w = None
            if k == 0:
                w = [ev_pr[pslot]] if ev_pr[pslot] else list(x_ready)
            ctx.tensor.matmul(pp[:, :],
                              x_sb[:, k * T + mt * 128: k * T + (mt + 1) * 128],
                              H["wv"][:, k * 512:(k + 1) * 512],
                              start=(k == 0), stop=(k == NCH - 1),
                              wait=w, tinc=(k == NCH - 1))
        mm_now = ctx.snap("T")
        ev_pr[pslot] = ctx.vector.tensor_add(v_sb[:, mt * 512:(mt + 1) * 512],
                                             pp[:, :], H["bvrep"][:, :],
                                             wait=mm_now)
    v_done = [ev_pr[i] for i in range(4) if ev_pr[i]]
    ctx.pop_scope()

    ctx.push_scope()
    p_sb = [ctx.sbuf(f"Ap{i}", (128, T), BF16) for i in range(2)]
    pt_sb = [ctx.sbuf(f"Apt{i}", (128, T), BF16) for i in range(2)]
    otm_sb = [ctx.sbuf(f"Aotm{i}", (128, 64), BF16) for i in range(2)]
    sm_sb = ctx.sbuf("Asm", (128, 16), F32)
    mcols = sm_sb[:, 0:4]
    mm_ = sm_sb[:, 4:5]
    msc = sm_sb[:, 5:6]
    acc4 = sm_sb[:, 8:12]
    rs = sm_sb[:, 6:7]
    rr = sm_sb[:, 7:8]

    exp_marks = [None, None, None, None]
    tev_marks = [None, None]
    pv_mm_slot = [None, None]
    pv_mark = None
    ot_mark = None
    otm_mark = [None, None]
    pt_done = [None, None]

    units = [(h, it) for h in range(8) for it in range(NTT)]
    for ui, (h, it) in enumerate(units):
        par = h % 2
        qrows = slice(par * 64, par * 64 + 64)
        qcols = slice((h // 2) * T + it * 128, (h // 2) * T + (it + 1) * 128)
        uslot = ui % 2
        for j in range(4):
            w = []
            if exp_marks[j]:
                w.append(exp_marks[j])
            if ui == 0 and j == 0:
                w = w + qk_done + v_done
            ctx.tensor.matmul(
                ps_s[j][:, :], qk_sb[qrows, qcols],
                qk_sb[qrows, (4 + h // 2) * T + j * 512:(4 + h // 2) * T + (j + 1) * 512],
                start=True, stop=True, wait=(w or None), tinc=True,
                tile_position=(par * 64, 0))
        mm_s = ctx.snap("T")
        mk = None
        for j in range(4):
            mk = ctx.vector.tensor_reduce(mcols[:, j:j + 1], ps_s[j][:, :],
                                          AX.X, ALU.max,
                                          wait=(mm_s if j == 0 else [mk]))
        mk = ctx.vector.tensor_reduce(mm_, mcols, AX.X, ALU.max, wait=[mk])
        mk_msc = ctx.vector.tensor_single_scalar(msc, mm_, -A_SCALE, ALU.mult,
                                                 wait=[mk])
        emk = []
        for j in range(4):
            w = [mk_msc]
            if j == 0 and pt_done[uslot]:
                w.append(pt_done[uslot])
            emk.append(ctx.scalar.activation(
                out=p_sb[uslot][:, j * 512:(j + 1) * 512], in_=ps_s[j][:, :],
                func=AF.Exp, scale=A_SCALE, bias=msc,
                accum_out=acc4[:, j:j + 1],
                wait=(w if j == 0 else [mk_msc])))
            exp_marks[j] = emk[-1]
        mk = ctx.vector.tensor_reduce(rs, acc4, AX.X, ALU.add, wait=[emk[-1]])
        rmk = ctx.vector.reciprocal(rr, rs, wait=[mk])
        tmarks = []
        for jt in range(NTT):
            tb = (jt // 4) % 2
            w = [emk[-1]]
            if tev_marks[tb] and jt % 4 == 0:
                w.append(tev_marks[tb])
            tmarks.append(ctx.tensor.transpose(
                ps_t[tb][:, (jt % 4) * 128:(jt % 4 + 1) * 128],
                p_sb[uslot][:, jt * 128:(jt + 1) * 128],
                H["ident"][:, :], wait=w, tinc=True))
            if jt % 4 == 3:
                wv2 = [tmarks[-1]]
                if jt == 3 and pv_mm_slot[uslot]:
                    wv2 = wv2 + pv_mm_slot[uslot]
                tev_marks[tb] = ctx.vector.tensor_copy(
                    pt_sb[uslot][:, (jt - 3) * 128:(jt + 1) * 128],
                    ps_t[tb][:, :], wait=wv2)
        pt_done[uslot] = tev_marks[(NTT - 1) // 4 % 2]
        for jt in range(NTT):
            w = None
            if jt == 0:
                w = [tev_marks[0], tev_marks[1]]
                if pv_mark:
                    w.append(pv_mark)
            ctx.tensor.matmul(ps_pv[:, :],
                              pt_sb[uslot][:, jt * 128:(jt + 1) * 128],
                              v_sb[:, jt * 512 + h * 64: jt * 512 + (h + 1) * 64],
                              start=(jt == 0), stop=(jt == NTT - 1),
                              wait=w, tinc=(jt == NTT - 1))
        mm_pv = ctx.snap("T")
        pv_mm_slot[uslot] = mm_pv
        w = mm_pv + [rmk]
        if otm_mark[uslot]:
            w.append(otm_mark[uslot])
        pv_mark = ctx.vector.tensor_scalar_mul(otm_sb[uslot][:, :], ps_pv[:, :],
                                               rr, wait=w)
        w = [pv_mark]
        if ot_mark:
            w.append(ot_mark)
        tmk = ctx.tensor.transpose(ps_ot[par * 64:par * 64 + 64, :],
                                   otm_sb[uslot][:, :], H["ident"][:, :],
                                   wait=w, tinc=True,
                                   tile_position=(0, par * 64))
        ot_mark = ctx.scalar.activation(
            out=o_fm[par * 64 + 0:par * 64 + 64,
                     (h // 2) * T + it * 128:(h // 2) * T + (it + 1) * 128],
            in_=ps_ot[par * 64:par * 64 + 64, :], func=AF.Identity, wait=[tmk])
        otm_mark[uslot] = ot_mark
    attn_done = ctx.snap("V", "S", "T")
    ctx.pop_scope()

    ctx.push_scope()
    xt_sb = ctx.sbuf("Axt", (128, NCH * 512), BF16)
    y_sb = ctx.sbuf("Ay", (128, NCH * 512), F32)
    zt_sb = ctx.sbuf("Azt", (128, 512), F32)
    ubf_sb = ctx.sbuf("Aubf", (128, NCH * 512), BF16)
    usq_sb = ctx.sbuf("Ausq", (128, NCH * 512), BF16)
    stat_sb = ctx.sbuf("Astat", (128, 4 * 512), F32)
    out_sb = ctx.sbuf("Aout2", (128, NCH * 512), BF16)
    ps_a = ps_s[0]
    ps_b = ps_s[1]

    stats_mm_prev = None
    norm_prev = None
    st_prev = None
    for ns in range(4):
        xw = list(attn_done)
        if st_prev:
            xw.append(st_prev)
        for c in range(NCH):
            ctx.gpsimd.dma_start(out=xt_sb[:, c * 512:(c + 1) * 512],
                                 in_=x_fm[s, c, :, ns * 512:(ns + 1) * 512],
                                 wait=(xw if c == 0 else None))
        xt_ready = ctx.snap("G")
        u_marks = []
        for m in range(NCH):
            pp = ps_s[m % 2]
            for k in range(4):
                w = None
                if k == 0:
                    w = xt_ready + attn_done
                    if m >= 2:
                        w = w + [u_marks[m - 2]]
                    elif norm_prev:
                        w = w + norm_prev
                ctx.tensor.matmul(pp[:, :],
                                  H["wout"][:, m * 512 + k * 128: m * 512 + (k + 1) * 128],
                                  o_fm[:, k * T + ns * 512: k * T + (ns + 1) * 512],
                                  start=(k == 0), stop=(k == 3),
                                  wait=w, tinc=(k == 3))
            mm_now = ctx.snap("T")
            u_marks.append(ctx.scalar.activation(
                out=y_sb[:, m * 512:(m + 1) * 512], in_=pp[:, :],
                func=AF.Identity, bias=H["bout2d"][:, m:m + 1], wait=mm_now))
        dep_u = [u_marks[-1]]

        cm = []
        for c in range(NCH):
            ycols = y_sb[:, c * 512:(c + 1) * 512]
            w = list(dep_u)
            if c == 0 and stats_mm_prev:
                w += stats_mm_prev
            cm.append(ctx.vector.tensor_copy(ubf_sb[:, c * 512:(c + 1) * 512],
                                             ycols, wait=(w if c == 0 else dep_u)))
            cm.append(ctx.scalar.activation(out=usq_sb[:, c * 512:(c + 1) * 512],
                                            in_=ycols, func=AF.Square,
                                            wait=(w if c == 0 else dep_u)))
        dep = [cm[-2], cm[-1]] + (stats_mm_prev or [])
        mu, rstd = emit_ln_stats(ctx, 512, LN_EPS_A,
                                 lambda c: ubf_sb[:, c * 512:(c + 1) * 512],
                                 lambda c: usq_sb[:, c * 512:(c + 1) * 512],
                                 H["ones"], ps_a, ps_b, stat_sb, dep)
        stats_mm_prev = ctx.snap("T")
        rstd_done = ctx.snap("S")
        umk = []
        for c in range(NCH):
            ycols = y_sb[:, c * 512:(c + 1) * 512]
            w = list(rstd_done) if c == 0 else None
            mk = ctx.vector.tensor_sub(zt_sb[:, :], ycols, mu, wait=w)
            mk = ctx.vector.tensor_mul(zt_sb[:, :], zt_sb[:, :], rstd, wait=[mk])
            mk = ctx.vector.tensor_scalar(zt_sb[:, :], zt_sb[:, :],
                                          H["g2d"][:, c:c + 1],
                                          H["be2d"][:, c:c + 1],
                                          ALU.mult, ALU.add, wait=[mk])
            mk = ctx.vector.scalar_tensor_tensor(
                ycols, zt_sb[:, :], KAPPA, xt_sb[:, c * 512:(c + 1) * 512],
                ALU.mult, ALU.add, wait=[mk])
            umk.append(mk)
        dep_u2 = [umk[-1]]

        cm = []
        for c in range(NCH):
            ucols = y_sb[:, c * 512:(c + 1) * 512]
            w = list(dep_u2) + stats_mm_prev
            cm.append(ctx.vector.tensor_copy(ubf_sb[:, c * 512:(c + 1) * 512],
                                             ucols, wait=(w if c == 0 else dep_u2)))
            cm.append(ctx.scalar.activation(out=usq_sb[:, c * 512:(c + 1) * 512],
                                            in_=ucols, func=AF.Square,
                                            wait=(w if c == 0 else dep_u2)))
        dep = [cm[-2], cm[-1]] + stats_mm_prev
        mu, rstd = emit_ln_stats(ctx, 512, LN_EPS_OUTER,
                                 lambda c: ubf_sb[:, c * 512:(c + 1) * 512],
                                 lambda c: usq_sb[:, c * 512:(c + 1) * 512],
                                 H["ones"], ps_a, ps_b, stat_sb, dep)
        stats_mm_prev = ctx.snap("T")
        rstd_done = ctx.snap("S")
        last = None
        for c in range(NCH):
            ucols = y_sb[:, c * 512:(c + 1) * 512]
            w = list(rstd_done) if c == 0 else None
            mk = ctx.vector.tensor_sub(ucols, ucols, mu, wait=w)
            last = ctx.vector.tensor_mul(out_sb[:, c * 512:(c + 1) * 512],
                                         ucols, rstd, wait=[mk])
        norm_prev = [last] + stats_mm_prev
        for c in range(NCH):
            st_prev = ctx.sync.dma_start(
                out=stream_out[s, c, :, ns * 512:(ns + 1) * 512],
                in_=out_sb[:, c * 512:(c + 1) * 512],
                wait=([last] if c == 0 else None))
    ctx.pop_scope()
    ctx.sync.wait_ge("Y", ctx.c("Y"))


def prep_A(inputs):
    W = {}
    Wqk = inputs["L0_attn_W_qk"]
    perm = np.zeros(1024, dtype=np.int64)
    for j in range(8):
        for p in range(128):
            if j < 4:
                h = 2 * j + p // 64
                perm[j * 128 + p] = h * 128 + (p % 64) * 2
            else:
                h = 2 * (j - 4) + p // 64
                perm[j * 128 + p] = h * 128 + (p % 64) * 2 + 1
    Wqk_p = np.ascontiguousarray(Wqk[:, perm])
    W["wqkT"] = tiles_lhsT(Wqk_p)
    W["bqk2d"] = vec2d(inputs["L0_attn_b_qk"][perm])
    W["wv_rhs"] = np.ascontiguousarray(inputs["L0_attn_W_v"].reshape(8, 128, 512))
    W["bv_row"] = np.ascontiguousarray(inputs["L0_attn_b_v"].reshape(1, 512))
    W["woutT"] = tiles_lhsT(inputs["L0_attn_W_out"])
    W["bout2d"] = vec2d(inputs["L0_attn_b_out"])
    W["g2d"] = vec2d(inputs["L0_attn_ln_gamma"])
    W["be2d"] = vec2d(inputs["L0_attn_ln_beta"])
    return W


def declare_A_weights(nc):
    aps = {}
    aps["wqkT"] = nc.dram_tensor("AwqkT", [8, 128, 1024], F32, kind="ExternalInput").ap()
    aps["wv_rhs"] = nc.dram_tensor("Awv_rhs", [8, 128, 512], F32, kind="ExternalInput").ap()
    aps["woutT"] = nc.dram_tensor("AwoutT", [8, 128, 512], F32, kind="ExternalInput").ap()
    aps["bv_row"] = nc.dram_tensor("Abv_row", [1, 512], F32, kind="ExternalInput").ap()
    for n in ("bqk2d", "bout2d", "g2d", "be2d"):
        aps[n] = nc.dram_tensor("A" + n, [128, 8], F32, kind="ExternalInput").ap()
    return aps


def build_A_standalone():
    nc = bass.Bass("TRN2", target_bir_lowering=False)
    x_fm = nc.dram_tensor("x_fm", [SEQ, NCH, 128, T], F32, kind="ExternalInput").ap()
    stream_out = nc.dram_tensor("stream_out", [SEQ, NCH, 128, T], BF16,
                                kind="ExternalOutput").ap()
    with contextlib.ExitStack() as es:
        ctx = Ctx(nc, es)
        ctx.init_epscols()
        W = declare_A_weights(nc)
        consts = declare_consts(nc)
        ctx.push_scope()
        H = emit_phase_A_consts(ctx, W, consts)
        for s in range(SEQ):
            ctx.push_scope()
            emit_phase_A_seq(ctx, W, H, x_fm, stream_out, s)
            ctx.flush(f"A_s{s}")
            ctx.pop_scope()
        ctx.pop_scope()
    return nc


def run_A(inputs, x_fm_percore):
    nc = build_A_standalone()
    W = dmap_prefix(prep_A(inputs), "A")
    in_maps = []
    for c in range(NCORES):
        m = dict(W)
        m.update(consts_np())
        m["x_fm"] = x_fm_percore[c]
        in_maps.append(m)
    res = run_bass_kernel_spmd(nc, in_maps, core_ids=list(range(NCORES)))
    return [r["stream_out"] for r in res.results]


def prep_D(inputs):
    W = {}
    W["w1_T"] = tiles_lhsT(np.ascontiguousarray(inputs["L1_ffn_w1_w"].T))
    W["w2_T"] = tiles_lhsT(np.ascontiguousarray(inputs["L1_ffn_w2_w"].T))
    W["h1_T"] = tiles_lhsT(np.ascontiguousarray(inputs["L1_ffn_h1_w"].T))
    W["h2_T"] = tiles_lhsT(np.ascontiguousarray(inputs["L1_ffn_h2_w"].T))
    W["w3_T"] = tiles_lhsT(np.ascontiguousarray(inputs["L1_ffn_w3_w"].T))
    for n in ("w1", "w2", "h1", "h2", "w3"):
        W[n + "_s2d"] = vec2d(inputs[f"L1_ffn_{n}_s"])
    return W


def consts_np():
    sel8 = np.zeros((8, 1024), dtype=BF16NP)
    for e in range(8):
        sel8[e, e * 128:(e + 1) * 128] = 1
    return {"ones128": np.ones((128, 128), dtype=BF16NP),
            "ident128b": np.eye(128, dtype=BF16NP),
            "ident128f": np.eye(128, dtype=np.float32),
            "sel8": sel8}


def declare_consts(nc):
    return {"ones128": nc.dram_tensor("ones128", [128, 128], BF16, kind="ExternalInput").ap(),
            "ident128b": nc.dram_tensor("ident128b", [128, 128], BF16, kind="ExternalInput").ap(),
            "ident128f": nc.dram_tensor("ident128f", [128, 128], F32, kind="ExternalInput").ap(),
            "sel8": nc.dram_tensor("sel8", [8, 1024], BF16, kind="ExternalInput").ap()}


def declare_D_weights(nc):
    aps = {}
    for name, shape in [
        ("w1_T", [HCH, 128, NCH * 128]), ("w2_T", [HCH, 128, NCH * 128]),
        ("h1_T", [HCH, 128, HCH * 128]), ("h2_T", [HCH, 128, HCH * 128]),
        ("w3_T", [NCH, 128, HCH * 128]),
    ]:
        aps[name] = nc.dram_tensor("D" + name, shape, F32, kind="ExternalInput").ap()
    for name in ("w1_s2d", "w2_s2d", "h1_s2d", "h2_s2d"):
        aps[name] = nc.dram_tensor("D" + name, [128, 32], F32, kind="ExternalInput").ap()
    aps["w3_s2d"] = nc.dram_tensor("Dw3_s2d", [128, 8], F32, kind="ExternalInput").ap()
    return aps


def dmap_prefix(W, prefix):
    return {prefix + k: v for k, v in W.items()}


def build_D_standalone():
    nc = bass.Bass("TRN2", target_bir_lowering=False)
    stream_in = nc.dram_tensor("stream_in", [SEQ, NCH, 128, T], BF16,
                               kind="ExternalInput").ap()
    y_fm = nc.dram_tensor("y_fm", [SEQ, NCH, 128, T], F32,
                          kind="ExternalOutput").ap()
    actA = nc.dram_tensor("actA", [HCH, 128, NTOK], BF16, kind="Internal").ap()
    actB = nc.dram_tensor("actB", [HCH, 128, NTOK], BF16, kind="Internal").ap()
    with contextlib.ExitStack() as es:
        ctx = Ctx(nc, es)
        ctx.init_epscols()
        W = declare_D_weights(nc)
        consts = declare_consts(nc)
        emit_phase_D(ctx, W, stream_in, actA, actB, y_fm, consts)
    return nc


def run_D(inputs, stream_bf16_percore):
    nc = build_D_standalone()
    W = dmap_prefix(prep_D(inputs), "D")
    in_maps = []
    for c in range(NCORES):
        m = dict(W)
        m.update(consts_np())
        m["stream_in"] = stream_bf16_percore[c]
        in_maps.append(m)
    res = run_bass_kernel_spmd(nc, in_maps, core_ids=list(range(NCORES)))
    return [r["y_fm"] for r in res.results]


def build_fused():
    nc = bass.Bass("TRN2", target_bir_lowering=False)
    x_fm = nc.dram_tensor("x_fm", [SEQ, NCH, 128, T], F32, kind="ExternalInput").ap()
    y_fm = nc.dram_tensor("y_fm", [SEQ, NCH, 128, T], F32, kind="ExternalOutput").ap()
    stream = nc.dram_tensor("stream", [SEQ, NCH, 128, T], BF16, kind="Internal").ap()
    actA = nc.dram_tensor("actA", [HCH, 128, NTOK], BF16, kind="Internal").ap()
    actB = nc.dram_tensor("actB", [HCH, 128, NTOK], BF16, kind="Internal").ap()
    with contextlib.ExitStack() as es:
        ctx = Ctx(nc, es)
        ctx.init_epscols()
        consts = declare_consts(nc)
        consts.update(declare_C_consts(nc))
        WA = declare_A_weights(nc)
        WB = declare_B_weights(nc)
        WC = declare_C_weights(nc)
        WD = declare_D_weights(nc)
        ctx.push_scope()
        HA = emit_phase_A_consts(ctx, WA, consts)
        for s in range(SEQ):
            ctx.push_scope()
            emit_phase_A_seq(ctx, WA, HA, x_fm, stream, s)
            ctx.flush(f"A{s}")
            ctx.pop_scope()
        ctx.pop_scope()
        ctx.push_scope()
        emit_phase_B(ctx, WB, stream, stream, consts)
        ctx.flush("B")
        ctx.pop_scope()
        ctx.push_scope()
        HC = emit_phase_C_consts(ctx, WC, consts)
        for s in range(SEQ):
            ctx.push_scope()
            emit_phase_C_seq(ctx, WC, HC, stream, stream, s)
            ctx.flush(f"C{s}")
            ctx.pop_scope()
        ctx.pop_scope()
        ctx.push_scope()
        emit_phase_D(ctx, WD, stream, actA, actB, y_fm, consts)
        ctx.pop_scope()
    return nc


def kernel_fused(inputs):
    x = inputs["L0_attn_x"]
    x_fm = [np.ascontiguousarray(
        x[4 * c:4 * c + 4].transpose(0, 2, 1).reshape(SEQ, NCH, 128, T)
        .astype(np.float32)) for c in range(NCORES)]
    nc = build_fused()
    Wall = {}
    Wall.update(dmap_prefix(prep_A(inputs), "A"))
    Wall.update(dmap_prefix(prep_B(inputs), "B"))
    Wall.update(dmap_prefix(prep_C(inputs), "C"))
    Wall.update(dmap_prefix(prep_D(inputs), "D"))
    cn = consts_np()
    cn.update(consts_C_np())
    in_maps = []
    for c in range(NCORES):
        m = dict(Wall)
        m.update(cn)
        m["x_fm"] = x_fm[c]
        in_maps.append(m)
    res = run_bass_kernel_spmd(nc, in_maps, core_ids=list(range(NCORES)))
    full = np.concatenate([np.asarray(r["y_fm"]) for r in res.results], axis=0)
    return np.ascontiguousarray(
        full.reshape(32, 1024, 2048).transpose(0, 2, 1)).astype(np.float32)


def kernel(**inputs):
    inputs = {k: np.asarray(v) for k, v in inputs.items()}
    return kernel_fused(inputs)


# version 04
# speedup vs baseline: 1.0787x; 1.0344x over previous
import contextlib
import numpy as np
import ml_dtypes
import concourse.bass as bass
import concourse.mybir as mybir
from concourse.bass_utils import run_bass_kernel_spmd

F32 = mybir.dt.float32
BF16 = mybir.dt.bfloat16
AF = mybir.ActivationFunctionType
ALU = mybir.AluOpType
AX = mybir.AxisListType
BF16NP = ml_dtypes.bfloat16

NCORES = 8
SEQ = 4
T = 2048
DM = 1024
NCH = DM // 128
NTOK = SEQ * T
DH = 4096
HCH = DH // 128

LN_EPS_OUTER = 1e-5
LN_EPS_A = 1e-5
LN_EPS_B = 1e-6
CTX_EPS = 1e-6
RSQRT2 = 2.0 ** (-0.5)
KAPPA = RSQRT2 / (1.0 + RSQRT2)

ENGINES = ("tensor", "vector", "scalar", "gpsimd", "sync")
AUTOSEM = {"tensor": ("T", 1), "vector": ("V", 1), "scalar": ("S", 1),
           "gpsimd": ("G", 16), "sync": ("Y", 16)}
NOINC_OPS = {"wait_ge", "nop", "engine_nop"}


class EQ:
    def __init__(self, ctx, name):
        object.__setattr__(self, "_ctx", ctx)
        object.__setattr__(self, "_name", name)

    def __getattr__(self, opname):
        ctx = self._ctx
        name = self._name

        def call(*args, wait=None, tinc=True, **kwargs):
            if opname == "wait_ge":
                s, v = args

                def run_wait(eng, s=s, v=v):
                    eng.wait_ge(ctx.sems[s], v)
                ctx.q[name].append(run_wait)
                return None
            sem, step = AUTOSEM[name]
            do_inc = opname not in NOINC_OPS and (name != "tensor" or tinc)
            if do_inc:
                ctx.cnt[sem] = ctx.cnt.get(sem, 0) + step
            mark = (sem, ctx.cnt.get(sem, 0)) if do_inc else None

            def run(eng, args=args, kwargs=kwargs, wait=wait, mark=mark):
                if wait:
                    seen = {}
                    for s, v in wait:
                        if v > 0 and seen.get(s, -1) < v:
                            seen[s] = v
                    for s, v in seen.items():
                        eng.wait_ge(ctx.sems[s], v)
                inst = getattr(eng, opname)(*args, **kwargs)
                if mark is not None:
                    inst.then_inc(ctx.sems[mark[0]], AUTOSEM[name][1])
                return inst

            ctx.q[name].append(run)
            return mark

        return call


class Ctx:
    def __init__(self, nc, es):
        self.nc = nc
        self.es = es
        self.scopes = [es]
        self.q = {e: [] for e in ENGINES}
        self.sems = {}
        self.cnt = {}
        for s in ("T", "V", "S", "G", "Y"):
            self.sems[s] = es.enter_context(nc.semaphore(s))
            self.cnt[s] = 0
        self.tensor = EQ(self, "tensor")
        self.vector = EQ(self, "vector")
        self.scalar = EQ(self, "scalar")
        self.gpsimd = EQ(self, "gpsimd")
        self.sync = EQ(self, "sync")

    def c(self, sem):
        return self.cnt.get(sem, 0)

    def snap(self, *sems):
        return [(s, self.cnt.get(s, 0)) for s in sems]

    def flush(self, name=None):
        with self.nc.Block(name) as block:
            for e in ENGINES:
                ops = self.q[e]
                if ops:
                    getattr(block, e)(lambda eng, ops=tuple(ops): [f(eng) for f in ops])
        for e in ENGINES:
            self.q[e] = []

    def push_scope(self):
        self.scopes.append(contextlib.ExitStack())

    def pop_scope(self):
        self.scopes.pop().close()

    def _uname(self, name):
        self._uid = getattr(self, "_uid", 0) + 1
        return f"{name}_{self._uid}"

    def sbuf(self, name, shape, dtype):
        return self.scopes[-1].enter_context(
            self.nc.sbuf_tensor(self._uname(name), list(shape), dtype))

    def psum(self, name, shape, dtype=F32):
        return self.scopes[-1].enter_context(
            self.nc.psum_tensor(self._uname(name), list(shape), dtype))

    def init_epscols(self):
        self.eps_sb = self.sbuf("epsconsts", (128, 4), F32)
        self.eps_cols = {}
        for i, v in enumerate((1e-5, 1e-6)):
            self.vector.memset(self.eps_sb[:, i:i + 1], v)
            self.eps_cols[v] = i

    def epscol(self, v):
        return self.eps_sb[:, self.eps_cols[v]:self.eps_cols[v] + 1]


def tiles_lhsT(wT):
    I_, O_ = wT.shape
    KC, Q = I_ // 128, O_ // 128
    t = wT.reshape(KC, 128, Q, 128).transpose(2, 1, 0, 3).reshape(Q, 128, KC * 128)
    return np.ascontiguousarray(t)


def vec2d(v):
    return np.ascontiguousarray(np.asarray(v).reshape(-1, 128).T)


def emit_ln_stats(ctx, NTt, eps, val_chunks, sq_chunks, ones_sb, ps_s1, ps_s2,
                  stat_sb, dep):
    for c in range(NCH):
        ctx.tensor.matmul(ps_s1[:, 0:NTt], ones_sb[:, :], val_chunks(c),
                          start=(c == 0), stop=(c == NCH - 1),
                          wait=(dep if c == 0 else None), tinc=False)
        ctx.tensor.matmul(ps_s2[:, 0:NTt], ones_sb[:, :], sq_chunks(c),
                          start=(c == 0), stop=(c == NCH - 1),
                          tinc=(c == NCH - 1))
    stats_mm = ctx.snap("T")
    mu = stat_sb[:, 0 * NTt:1 * NTt]
    musq = stat_sb[:, 1 * NTt:2 * NTt]
    var = stat_sb[:, 2 * NTt:3 * NTt]
    rstd = stat_sb[:, 3 * NTt:4 * NTt]
    m_mu = ctx.scalar.activation(out=mu, in_=ps_s1[:, 0:NTt], func=AF.Copy,
                                 scale=1.0 / DM, wait=stats_mm + dep)
    m_musq = ctx.scalar.activation(out=musq, in_=mu, func=AF.Square,
                                   wait=[m_mu])
    m_var = ctx.vector.scalar_tensor_tensor(var, ps_s2[:, 0:NTt], 1.0 / DM,
                                            musq, ALU.mult, ALU.subtract,
                                            wait=[m_musq] + stats_mm)
    m_ln = ctx.scalar.activation(out=var, in_=var, func=AF.Ln,
                                 bias=ctx.epscol(eps), wait=[m_var])
    ctx.scalar.activation(out=rstd, in_=var, func=AF.Exp, scale=-0.5,
                          wait=[m_ln])
    return mu, rstd


def emit_mm_stage(ctx, tag, w_ap, scale_sb, scale_col0, src_slab, dst_slab,
                  KC, Q, act, extra=None):
    NT = 512
    NGRP = 2048
    ngroups = NTOK // NGRP
    ntiles = NGRP // NT

    w_sb = [ctx.sbuf(f"{tag}w{i}", (128, KC * 128), BF16) for i in range(3)]
    src_sb = ctx.sbuf(f"{tag}src", (128, KC * NGRP), BF16)
    t1_sb = [ctx.sbuf(f"{tag}t1_{i}", (128, NGRP), BF16) for i in range(2)] \
        if extra is not None else None
    dst_sb = [ctx.sbuf(f"{tag}dst{i}", (128, NGRP), BF16) for i in range(2)]
    ps = [ctx.psum(f"{tag}ps{i}", (128, NT), F32) for i in range(2)]

    mm_after_user = []
    ev_after_user = []
    st_after_user = []
    ev_after_tile = []
    mm_after_group = []

    for g in range(ngroups):
        n0 = g * NGRP
        w = mm_after_group[g - 1] if g >= 1 else None
        for k in range(KC):
            ctx.sync.dma_start(out=src_sb[:, k * NGRP:(k + 1) * NGRP],
                               in_=src_slab(k, n0, NGRP),
                               wait=(w if k == 0 else None))
        sl_ready = ctx.snap("Y")

        for m in range(Q):
            u = g * Q + m
            wslot = u % 3
            w = mm_after_user[u - 3] if u >= 3 else None
            ctx.gpsimd.dma_start(out=w_sb[wslot][:, :], in_=w_ap[m], wait=w)
            wl_ready = ctx.snap("G")

            if extra is not None:
                w = ev_after_user[u - 2] if u >= 2 else None
                ctx.sync.dma_start(out=t1_sb[m % 2][:, :],
                                   in_=extra["t1"](m, n0, NGRP), wait=w)
            sl_ready2 = ctx.snap("Y")

            for ni in range(ntiles):
                pslot = (u * ntiles + ni) % 2
                tile_idx = u * ntiles + ni
                for k in range(KC):
                    w = None
                    if k == 0:
                        w = wl_ready + sl_ready2
                        if tile_idx >= 2:
                            w = w + [ev_after_tile[tile_idx - 2]]
                    ctx.tensor.matmul(
                        ps[pslot][:, :],
                        w_sb[wslot][:, k * 128:(k + 1) * 128],
                        src_sb[:, k * NGRP + ni * NT: k * NGRP + (ni + 1) * NT],
                        start=(k == 0), stop=(k == KC - 1),
                        wait=w, tinc=(k == KC - 1))
                mm_now = ctx.snap("T")
                dcols = dst_sb[m % 2][:, ni * NT:(ni + 1) * NT]
                wlist = list(mm_now)
                if ni == 0 and u >= 2:
                    wlist.append(st_after_user[u - 2])
                if extra is None:
                    mk = ctx.scalar.activation(
                        out=dcols, in_=ps[pslot][:, :], func=act,
                        scale=scale_sb[:, scale_col0 + m:scale_col0 + m + 1],
                        wait=wlist)
                else:
                    mk = ctx.vector.scalar_tensor_tensor(
                        dcols, ps[pslot][:, :],
                        scale_sb[:, scale_col0 + m:scale_col0 + m + 1],
                        t1_sb[m % 2][:, ni * NT:(ni + 1) * NT],
                        ALU.mult, ALU.mult, wait=wlist)
                ev_after_tile.append(mk)
            mm_after_user.append(ctx.snap("T"))
            ev_after_user.append([ev_after_tile[-1]])
            mk = ctx.sync.dma_start(out=dst_slab(m, n0, NGRP),
                                    in_=dst_sb[m % 2][:, :],
                                    wait=[ev_after_tile[-1]])
            st_after_user.append(mk)
        mm_after_group.append(ctx.snap("T"))
    ctx.sync.wait_ge("Y", ctx.c("Y"))


def emit_phase_D(ctx, W, stream_in, actA, actB, y_fm, consts):
    sc = ctx.sbuf("Dscales", (128, 160), F32)
    ctx.sync.dma_start(out=sc[:, 0:32], in_=W["w1_s2d"])
    ctx.sync.dma_start(out=sc[:, 32:64], in_=W["w2_s2d"])
    ctx.sync.dma_start(out=sc[:, 64:96], in_=W["h1_s2d"])
    ctx.sync.dma_start(out=sc[:, 96:128], in_=W["h2_s2d"])
    ctx.sync.dma_start(out=sc[:, 128:136], in_=W["w3_s2d"])
    ctx.sync.wait_ge("Y", ctx.c("Y"))
    ctx.flush("D_scales")

    def stream_slab(k, n0, nn):
        s, t0 = n0 // T, n0 % T
        return stream_in[s, k, :, t0:t0 + nn]

    def mk_slab(dram):
        return lambda m, n0, nn: dram[m, :, n0:n0 + nn]

    stages = [
        ("D_w1", lambda: emit_mm_stage(ctx, "Dw1", W["w1_T"], sc, 0, stream_slab,
                                       mk_slab(actA), KC=NCH, Q=HCH, act=AF.Silu)),
        ("D_w2", lambda: emit_mm_stage(ctx, "Dw2", W["w2_T"], sc, 32, stream_slab,
                                       mk_slab(actB), KC=NCH, Q=HCH, act=None,
                                       extra={"t1": mk_slab(actA)})),
        ("D_h1", lambda: emit_mm_stage(ctx, "Dh1", W["h1_T"], sc, 64, mk_slab(actB),
                                       mk_slab(actA), KC=HCH, Q=HCH, act=AF.Gelu)),
        ("D_h2", lambda: emit_mm_stage(ctx, "Dh2", W["h2_T"], sc, 96, mk_slab(actA),
                                       mk_slab(actB), KC=HCH, Q=HCH, act=AF.Gelu)),
        ("D_tail", lambda: emit_phase_D_tail(ctx, W, stream_in, actB, y_fm, sc, consts)),
    ]
    for name, fn in stages:
        ctx.push_scope()
        fn()
        ctx.flush(name)
        ctx.pop_scope()


def emit_phase_D_tail(ctx, W, stream_in, actB, y_fm, sc, consts):
    NT = 256
    KC, Q = HCH, NCH
    w_ap = W["w3_T"]

    w_sb = ctx.sbuf("Dt_w", (128, Q * KC * 128), BF16)
    for m in range(Q):
        ctx.gpsimd.dma_start(out=w_sb[:, m * KC * 128:(m + 1) * KC * 128],
                             in_=w_ap[m])
    wl_ready = ctx.snap("G")

    src_sb = [ctx.sbuf(f"Dt_src{i}", (128, KC * NT), BF16) for i in range(2)]
    x_sb = [ctx.sbuf(f"Dt_x{i}", (128, NCH * NT), BF16) for i in range(2)]
    u_sb = ctx.sbuf("Dt_u", (128, NCH * NT), F32)
    ubf_sb = ctx.sbuf("Dt_ubf", (128, NCH * NT), BF16)
    usq_sb = ctx.sbuf("Dt_usq", (128, NCH * NT), BF16)
    stat_sb = ctx.sbuf("Dt_stat", (128, 4 * NT), F32)
    out_sb = [ctx.sbuf(f"Dt_out{i}", (128, NCH * NT), F32) for i in range(2)]
    ones_sb = ctx.sbuf("Dt_ones", (128, 128), BF16)
    ctx.sync.dma_start(out=ones_sb[:, :], in_=consts["ones128"])

    ps = [ctx.psum(f"Dt_ps{i}", (128, NT), F32) for i in range(2)]
    ps_s1 = ctx.psum("Dt_s1", (128, NT), F32)
    ps_s2 = ctx.psum("Dt_s2", (128, NT), F32)

    ntiles = NTOK // NT
    mm_after_tile = []
    norm_done_tile = []
    st_after_tile = []
    stats_mm_prev = None

    for ni in range(ntiles):
        n0 = ni * NT
        s, t0 = n0 // T, n0 % T
        slot = ni % 2
        w = None
        if ni >= 2:
            w = mm_after_tile[ni - 2] + norm_done_tile[ni - 2]
        for k in range(KC):
            ctx.sync.dma_start(out=src_sb[slot][:, k * NT:(k + 1) * NT],
                               in_=actB[k, :, n0:n0 + NT],
                               wait=(w if k == 0 else None))
        for c in range(NCH):
            ctx.sync.dma_start(out=x_sb[slot][:, c * NT:(c + 1) * NT],
                               in_=stream_in[s, c, :, t0:t0 + NT])
        sl_ready = ctx.snap("Y")

        tile_ev_u = []
        for m in range(Q):
            pslot = m % 2
            for k in range(KC):
                w = None
                if k == 0:
                    w = sl_ready + wl_ready
                    if ni >= 1 and m == 0:
                        w = w + norm_done_tile[ni - 1]
                    if m >= 2:
                        w = w + [tile_ev_u[m - 2]]
                ctx.tensor.matmul(ps[pslot][:, :],
                                  w_sb[:, (m * KC + k) * 128:(m * KC + k + 1) * 128],
                                  src_sb[slot][:, k * NT:(k + 1) * NT],
                                  start=(k == 0), stop=(k == KC - 1),
                                  wait=w, tinc=(k == KC - 1))
            mm_now = ctx.snap("T")
            mk = ctx.vector.scalar_tensor_tensor(
                u_sb[:, m * NT:(m + 1) * NT], ps[pslot][:, :],
                sc[:, 128 + m:128 + m + 1], x_sb[slot][:, m * NT:(m + 1) * NT],
                ALU.mult, ALU.add, wait=mm_now)
            tile_ev_u.append(mk)
        mm_after_tile.append(ctx.snap("T"))

        dep_u = [tile_ev_u[-1]]
        cm = []
        for c in range(NCH):
            w = list(dep_u)
            if stats_mm_prev and c == 0:
                w += stats_mm_prev
            cm.append(ctx.vector.tensor_copy(
                ubf_sb[:, c * NT:(c + 1) * NT], u_sb[:, c * NT:(c + 1) * NT],
                wait=(w if c == 0 else dep_u)))
            cm.append(ctx.scalar.activation(
                out=usq_sb[:, c * NT:(c + 1) * NT],
                in_=u_sb[:, c * NT:(c + 1) * NT], func=AF.Square,
                wait=(w if c == 0 else dep_u)))
        dep = [cm[-2], cm[-1]]
        mu, rstd = emit_ln_stats(ctx, NT, LN_EPS_OUTER,
                                 lambda c: ubf_sb[:, c * NT:(c + 1) * NT],
                                 lambda c: usq_sb[:, c * NT:(c + 1) * NT],
                                 ones_sb, ps_s1, ps_s2, stat_sb, dep)
        stats_mm_prev = ctx.snap("T")
        rstd_done = ctx.snap("S")
        oslot = ni % 2
        last = None
        for c in range(NCH):
            ucols = u_sb[:, c * NT:(c + 1) * NT]
            w = list(rstd_done) if c == 0 else []
            if ni >= 2 and c == 0:
                w.append(st_after_tile[ni - 2])
            mk = ctx.vector.tensor_sub(ucols, ucols, mu, wait=(w or None))
            last = ctx.vector.tensor_mul(out_sb[oslot][:, c * NT:(c + 1) * NT],
                                         ucols, rstd, wait=[mk])
        norm_done_tile.append([last])
        stmk = None
        for c in range(NCH):
            stmk = ctx.sync.dma_start(out=y_fm[s, c, :, t0:t0 + NT],
                                      in_=out_sb[oslot][:, c * NT:(c + 1) * NT],
                                      wait=([last] if c == 0 else None))
        st_after_tile.append(stmk)
    ctx.sync.wait_ge("Y", ctx.c("Y"))


B_BIG = 30000.0


def emit_phase_B(ctx, W, stream_in, stream_out, consts, dbg=None):
    TM = 1024
    NS = TM // 512
    npass = NTOK // TM
    NE = 8

    gate_w = ctx.sbuf("Bgatew", (128, 64), BF16)
    identf = ctx.sbuf("Bidf", (128, 128), F32)
    ones_sb = ctx.sbuf("Bones", (128, 128), BF16)
    sel8 = ctx.sbuf("Bsel8", (8, 1024), BF16)
    gb2d = ctx.sbuf("Bgb", (128, 448), F32)
    gamma2d = gb2d[:, 0:64]
    beta2d = gb2d[:, 64:128]
    b1_2d = gb2d[:, 128:384]
    b2_2d = gb2d[:, 384:448]
    ctx.gpsimd.dma_start(out=gate_w[:, :], in_=W["gate_w"])
    ctx.sync.dma_start(out=identf[:, :], in_=consts["ident128f"])
    ctx.sync.dma_start(out=ones_sb[:, :], in_=consts["ones128"])
    ctx.sync.dma_start(out=sel8[:, :], in_=consts["sel8"])
    ctx.sync.dma_start(out=gb2d[:, 0:64], in_=W["gamma2d"])
    ctx.sync.dma_start(out=gb2d[:, 64:128], in_=W["beta2d"])
    ctx.sync.dma_start(out=gb2d[:, 128:384], in_=W["b1_2d"])
    ctx.sync.dma_start(out=gb2d[:, 384:448], in_=W["b2_2d"])
    consts_ready = ctx.snap("Y", "G")

    x_sb = ctx.sbuf("Bx", (128, NCH * TM), BF16)
    sq_sb = ctx.sbuf("Bsq", (128, NCH * 512), BF16)
    xhat_sb = ctx.sbuf("Bxhat", (128, NCH * TM), BF16)
    xh_scr = ctx.sbuf("Bxhscr", (128, 512), F32)
    stat_sb = ctx.sbuf("Bstat", (128, 4 * 512), F32)
    nrm_sb = ctx.sbuf("Bnrm", (128, NCH * TM), BF16)
    h_sb = ctx.sbuf("Bh", (128, HCH * TM), BF16)
    u_sb = h_sb.bitcast(F32)
    y_sb = ctx.sbuf("By", (128, NCH * TM), F32)
    tmp_sb = ctx.sbuf("Btmp", (128, 512), F32)
    cwb_sb = ctx.sbuf("Bcwb", (128, TM), BF16)
    scores_sb = ctx.sbuf("Bscores", (8, TM), F32)
    s_tm = ctx.sbuf("Bstm", (128, 64), F32)
    g_scr = ctx.sbuf("Bgscr", (128, 48), F32)
    cw_tm = ctx.sbuf("Bcwtm", (128, 64), F32)
    cw_fm = ctx.sbuf("Bcwfm", (8, TM), BF16)
    w1slab = [ctx.sbuf(f"Bw1s{i}", (128, 1024), BF16) for i in range(4)]
    w2slab = [ctx.sbuf(f"Bw2s{i}", (128, 4096), BF16) for i in range(2)]

    ps_h = [ctx.psum(f"Bpsh{i}", (128, 512), F32) for i in range(2)]
    ps_y = [ctx.psum(f"Bpsy{i}", (128, 512), F32) for i in range(2)]
    ps_cw = ctx.psum("Bpscw", (128, 1024), F32)
    ps_s1 = ctx.psum("Bps1", (128, 512), F32)
    ps_s2 = ctx.psum("Bps2", (128, 512), F32)

    free_h = [None, None]
    free_y = [None, None]
    free_cw = None
    w1_users = []
    w2_users = []
    u_read_x = None
    st_prev = None
    stats_mm_prev = None
    evac_prev_e = None

    for p in range(npass):
        s, t0 = p // (T // TM), (p % (T // TM)) * TM

        for c in range(NCH):
            w = list(consts_ready)
            if p >= 1 and c == 0:
                w.append(u_read_x)
            ctx.sync.dma_start(out=x_sb[:, c * TM:(c + 1) * TM],
                               in_=stream_in[s, c, :, t0:t0 + TM],
                               wait=(w if c == 0 else None))
        x_ready = ctx.snap("Y")

        for ns in range(NS):
            for k in range(NCH):
                w = None
                if k == 0:
                    w = x_ready + consts_ready
                    if free_y[ns]:
                        w = w + [free_y[ns]]
                ctx.tensor.matmul(ps_y[ns][0:8, :], gate_w[:, k * 8:(k + 1) * 8],
                                  x_sb[:, k * TM + ns * 512: k * TM + (ns + 1) * 512],
                                  start=(k == 0), stop=(k == NCH - 1),
                                  wait=w, tinc=(k == NCH - 1))
            mm_now = ctx.snap("T")
            free_y[ns] = ctx.scalar.activation(
                out=scores_sb[0:8, ns * 512:(ns + 1) * 512],
                in_=ps_y[ns][0:8, :], func=AF.Copy, wait=mm_now)

        for tau in range(8):
            w = None
            if tau == 0:
                w = [free_y[0], free_y[1]]
                if free_h[0]:
                    w.append(free_h[0])
            ctx.tensor.transpose(ps_h[0][:, tau * 8:(tau + 1) * 8],
                                 scores_sb[0:8, tau * 128:(tau + 1) * 128],
                                 identf[0:8, 0:8], wait=w, tinc=(tau == 7))
        mm_now = ctx.snap("T")
        stm_mk = ctx.vector.tensor_copy(s_tm[:, :], ps_h[0][:, 0:64], wait=mm_now)
        free_h[0] = stm_mk

        m1n = g_scr[:, 0:1]
        m2n = g_scr[:, 1:2]
        den = g_scr[:, 2:3]
        den2 = g_scr[:, 4:5]
        rden = g_scr[:, 3:4]
        sh = g_scr[:, 8:16]
        gmask = g_scr[:, 16:24]
        gm = g_scr[:, 24:32]
        gem = g_scr[:, 32:40]
        gex = g_scr[:, 40:48]
        mk = stm_mk
        for tau in range(8):
            S = s_tm[:, tau * 8:(tau + 1) * 8]
            mk1 = ctx.vector.tensor_reduce(m1n, S, AX.X, ALU.max, negate=True,
                                           wait=[mk])
            mk_gex = ctx.scalar.activation(out=gex, in_=S, func=AF.Exp,
                                           bias=m1n, wait=[mk1])
            mk_sh = ctx.scalar.activation(out=sh, in_=S, func=AF.Identity,
                                          bias=m1n, wait=[mk1])
            mk = ctx.vector.tensor_single_scalar(gmask, sh, 0.0, ALU.is_ge,
                                                 wait=[mk_sh])
            mk = ctx.vector.scalar_tensor_tensor(gm, gmask, -B_BIG, sh,
                                                 ALU.mult, ALU.add, wait=[mk])
            mk2 = ctx.vector.tensor_reduce(m2n, gm, AX.X, ALU.max, negate=True,
                                           wait=[mk])
            mk_gd = ctx.scalar.activation(out=gm, in_=sh, func=AF.Identity,
                                          bias=m2n, wait=[mk2, mk_gex])
            mk = ctx.vector.tensor_single_scalar(gmask, gm, 0.0, ALU.is_ge,
                                                 wait=[mk_gd])
            mk = ctx.vector.tensor_mul(gem, gex, gmask, wait=[mk])
            mkd = ctx.vector.tensor_reduce(den, gem, AX.X, ALU.add, wait=[mk])
            mk_d2 = ctx.scalar.activation(out=den2, in_=den, func=AF.Identity,
                                          wait=[mkd])
            mk_rd = ctx.vector.reciprocal(rden, den2, wait=[mk_d2])
            mk = ctx.scalar.activation(out=cw_tm[:, tau * 8:(tau + 1) * 8],
                                       in_=gem, func=AF.Identity, scale=rden,
                                       wait=[mk_rd, mkd])
        cw_done = mk
        if dbg is not None:
            ctx.sync.dma_start(out=dbg["cwtm"][p], in_=cw_tm[:, :], wait=[cw_done])
            ctx.sync.dma_start(out=dbg["stm"][p], in_=s_tm[:, :], wait=[cw_done])

        for tau in range(8):
            w = None
            if tau == 0:
                w = [cw_done]
                if free_cw:
                    w.append(free_cw)
            ctx.tensor.transpose(ps_cw[0:8, tau * 128:(tau + 1) * 128],
                                 cw_tm[:, tau * 8:(tau + 1) * 8],
                                 identf[:, :], wait=w, tinc=(tau == 7))
        mm_now = ctx.snap("T")
        free_cw = ctx.vector.tensor_copy(cw_fm[0:8, :], ps_cw[0:8, :], wait=mm_now)
        if dbg is not None:
            ctx.sync.dma_start(out=dbg["cw"][p], in_=cw_fm[0:8, :], wait=[free_cw])

        xhat_last = None
        for half in range(2):
            sq_last = None
            for c in range(NCH):
                xc = x_sb[:, c * TM + half * 512: c * TM + (half + 1) * 512]
                w = None
                if c == 0:
                    w = list(x_ready) + (stats_mm_prev or [])
                    if half == 0 and p >= 1 and st_prev:
                        w.append(st_prev)
                sq_last = ctx.scalar.activation(
                    out=sq_sb[:, c * 512:(c + 1) * 512], in_=xc, func=AF.Square,
                    wait=w)
            dep = [sq_last] + x_ready + (stats_mm_prev or [])
            mu, rstd = emit_ln_stats(
                ctx, 512, LN_EPS_B,
                lambda c: x_sb[:, c * TM + half * 512: c * TM + (half + 1) * 512],
                lambda c: sq_sb[:, c * 512:(c + 1) * 512],
                ones_sb, ps_s1, ps_s2, stat_sb, dep)
            stats_mm_prev = ctx.snap("T")
            rstd_done = ctx.snap("S")
            for c in range(NCH):
                xc = x_sb[:, c * TM + half * 512: c * TM + (half + 1) * 512]
                w = list(rstd_done) if c == 0 else None
                if c == 0 and p >= 1 and st_prev:
                    w.append(st_prev)
                mk = ctx.vector.tensor_sub(xh_scr[:, :], xc, mu, wait=w)
                xhat_last = ctx.vector.tensor_mul(
                    xhat_sb[:, c * TM + half * 512: c * TM + (half + 1) * 512],
                    xh_scr[:, :], rstd, wait=[mk])
        xhat_done = [xhat_last]
        if dbg is not None:
            for c in range(NCH):
                ctx.sync.dma_start(out=dbg["xhat"][p, c],
                                   in_=xhat_sb[:, c * TM:(c + 1) * TM],
                                   wait=(xhat_done if c == 0 else None))

        mm_w1_done = None
        y_marks = [None] * (NCH * NS)
        for e in range(NE):
            for c in range(NCH):
                w = None
                if c == 0:
                    w = list(xhat_done)
                    if mm_w1_done:
                        w += mm_w1_done
                    if stats_mm_prev:
                        w += stats_mm_prev
                ctx.vector.tensor_scalar(nrm_sb[:, c * TM:(c + 1) * TM],
                                         xhat_sb[:, c * TM:(c + 1) * TM],
                                         gamma2d[:, e * 8 + c:e * 8 + c + 1],
                                         beta2d[:, e * 8 + c:e * 8 + c + 1],
                                         ALU.mult, ALU.add, wait=w)
            nrm_done = ctx.snap("V")

            for ns in range(NS):
                w = None
                if ns == 0:
                    w = [free_cw]
                    if evac_prev_e:
                        w.append(evac_prev_e)
                ctx.tensor.matmul(ps_cw[:, ns * 512:(ns + 1) * 512],
                                  sel8[0:8, e * 128:(e + 1) * 128],
                                  cw_fm[0:8, ns * 512:(ns + 1) * 512],
                                  start=True, stop=True, wait=w, tinc=(ns == NS - 1))
            mm_now = ctx.snap("T")
            free_cw = ctx.scalar.activation(out=cwb_sb[:, :], in_=ps_cw[:, :],
                                            func=AF.Copy, wait=mm_now)

            for m in range(HCH):
                iu = len(w1_users)
                slot = iu % 4
                w = w1_users[iu - 4] if iu >= 4 else None
                ctx.gpsimd.dma_start(out=w1slab[slot][:, :], in_=W["w1T"][e, m],
                                     wait=w)
                wl_ready = ctx.snap("G")
                for ns in range(NS):
                    for k in range(NCH):
                        w = None
                        if k == 0:
                            w = wl_ready + nrm_done
                            if free_h[ns]:
                                w = w + [free_h[ns]]
                        ctx.tensor.matmul(
                            ps_h[ns][:, :], w1slab[slot][:, k * 128:(k + 1) * 128],
                            nrm_sb[:, k * TM + ns * 512: k * TM + (ns + 1) * 512],
                            start=(k == 0), stop=(k == NCH - 1),
                            wait=w, tinc=(k == NCH - 1))
                    mm_now = ctx.snap("T")
                    free_h[ns] = ctx.scalar.activation(
                        out=h_sb[:, m * TM + ns * 512: m * TM + (ns + 1) * 512],
                        in_=ps_h[ns][:, :], func=AF.Relu,
                        bias=b1_2d[:, e * HCH + m:e * HCH + m + 1], wait=mm_now)
                w1_users.append(ctx.snap("T"))
            mm_w1_done = ctx.snap("T")
            h_done = ctx.snap("S")

            for j in range(NCH):
                iu = len(w2_users)
                slot = iu % 2
                w = w2_users[iu - 2] if iu >= 2 else None
                ctx.gpsimd.dma_start(out=w2slab[slot][:, :], in_=W["w2T"][e, j],
                                     wait=w)
                wl_ready = ctx.snap("G")
                for ns in range(NS):
                    for k in range(HCH):
                        w = None
                        if k == 0:
                            w = wl_ready + h_done
                            if free_y[ns]:
                                w = w + [free_y[ns]]
                        ctx.tensor.matmul(
                            ps_y[ns][:, :], w2slab[slot][:, k * 128:(k + 1) * 128],
                            h_sb[:, k * TM + ns * 512: k * TM + (ns + 1) * 512],
                            start=(k == 0), stop=(k == HCH - 1),
                            wait=w, tinc=(k == HCH - 1))
                    mm_now = ctx.snap("T")
                    ycols = y_sb[:, j * TM + ns * 512: j * TM + (ns + 1) * 512]
                    b2col = b2_2d[:, e * NCH + j:e * NCH + j + 1]
                    cwcols = cwb_sb[:, ns * 512:(ns + 1) * 512]
                    if e == 0:
                        mk = ctx.vector.scalar_tensor_tensor(
                            ycols, ps_y[ns][:, :], b2col, cwcols,
                            ALU.add, ALU.mult, wait=mm_now + [free_cw])
                    else:
                        mk = ctx.vector.scalar_tensor_tensor(
                            tmp_sb[:, :], ps_y[ns][:, :], b2col, cwcols,
                            ALU.add, ALU.mult,
                            wait=mm_now + [free_cw, y_marks[j * NS + ns]])
                        mk = ctx.vector.tensor_add(ycols, ycols, tmp_sb[:, :],
                                                   wait=[mk])
                    free_y[ns] = mk
                    y_marks[j * NS + ns] = mk
                w2_users.append(ctx.snap("T"))
            evac_prev_e = free_y[1]
        mm_all = ctx.snap("T")
        if dbg is not None:
            for c in range(NCH):
                ctx.sync.dma_start(out=dbg["y"][p, c],
                                   in_=y_sb[:, c * TM:(c + 1) * TM],
                                   wait=([free_y[0], free_y[1]] if c == 0 else None))

        u_last = None
        for c in range(NCH):
            w = [y_marks[c * NS], y_marks[c * NS + 1]]
            if c == 0:
                w += mm_all
            u_last = ctx.vector.scalar_tensor_tensor(
                u_sb[:, c * TM:(c + 1) * TM], y_sb[:, c * TM:(c + 1) * TM],
                0.5, x_sb[:, c * TM:(c + 1) * TM], ALU.mult, ALU.add, wait=w)
        u_read_x = u_last

        for half in range(2):
            cm = []
            for c in range(NCH):
                ucols = u_sb[:, c * TM + half * 512: c * TM + (half + 1) * 512]
                w = [u_read_x]
                if c == 0:
                    w = w + stats_mm_prev
                cm.append(ctx.vector.tensor_copy(
                    nrm_sb[:, c * 512:(c + 1) * 512], ucols,
                    wait=(w if c == 0 else [u_read_x])))
                cm.append(ctx.scalar.activation(
                    out=sq_sb[:, c * 512:(c + 1) * 512], in_=ucols,
                    func=AF.Square, wait=(w if c == 0 else [u_read_x])))
            dep = [cm[-2], cm[-1]] + stats_mm_prev
            mu, rstd = emit_ln_stats(
                ctx, 512, LN_EPS_OUTER,
                lambda c: nrm_sb[:, c * 512:(c + 1) * 512],
                lambda c: sq_sb[:, c * 512:(c + 1) * 512],
                ones_sb, ps_s1, ps_s2, stat_sb, dep)
            stats_mm_prev = ctx.snap("T")
            rstd_done = ctx.snap("S")
            for c in range(NCH):
                ucols = u_sb[:, c * TM + half * 512: c * TM + (half + 1) * 512]
                ocols = xhat_sb[:, c * TM + half * 512: c * TM + (half + 1) * 512]
                w = list(rstd_done) if c == 0 else None
                mk = ctx.vector.tensor_sub(ucols, ucols, mu, wait=w)
                mk = ctx.vector.tensor_mul(ocols, ucols, rstd, wait=[mk])
                st_prev = ctx.sync.dma_start(
                    out=stream_out[s, c, :, t0 + half * 512: t0 + (half + 1) * 512],
                    in_=ocols, wait=[mk])
    ctx.sync.wait_ge("Y", ctx.c("Y"))


def prep_B(inputs):
    W = {}
    W1 = inputs["L0_ffn_W1"]
    W2 = inputs["L0_ffn_W2"]
    W["w1T"] = np.stack([tiles_lhsT(W1[e]) for e in range(8)])
    W["w2T"] = np.stack([tiles_lhsT(W2[e]) for e in range(8)])
    Wg = inputs["L0_ffn_Wg"]
    W["gate_w"] = np.ascontiguousarray(
        Wg.reshape(8, 128, 8).transpose(1, 0, 2).reshape(128, 64))
    W["gamma2d"] = np.ascontiguousarray(
        inputs["L0_ffn_gamma"].reshape(8, 8, 128).transpose(2, 0, 1).reshape(128, 64))
    W["beta2d"] = np.ascontiguousarray(
        inputs["L0_ffn_beta"].reshape(8, 8, 128).transpose(2, 0, 1).reshape(128, 64))
    W["b1_2d"] = np.ascontiguousarray(
        inputs["L0_ffn_b1"].reshape(8, 32, 128).transpose(2, 0, 1).reshape(128, 256))
    W["b2_2d"] = np.ascontiguousarray(
        inputs["L0_ffn_b2"].reshape(8, 8, 128).transpose(2, 0, 1).reshape(128, 64))
    return W


def declare_B_weights(nc):
    aps = {}
    aps["w1T"] = nc.dram_tensor("Bw1T", [8, 32, 128, 1024], F32, kind="ExternalInput").ap()
    aps["w2T"] = nc.dram_tensor("Bw2T", [8, 8, 128, 4096], F32, kind="ExternalInput").ap()
    aps["gate_w"] = nc.dram_tensor("Bgate_w", [128, 64], F32, kind="ExternalInput").ap()
    for n, cols in (("gamma2d", 64), ("beta2d", 64), ("b1_2d", 256), ("b2_2d", 64)):
        aps[n] = nc.dram_tensor("B" + n, [128, cols], F32, kind="ExternalInput").ap()
    return aps


def build_B_standalone(debug=False):
    nc = bass.Bass("TRN2", target_bir_lowering=False)
    stream_in = nc.dram_tensor("stream_in", [SEQ, NCH, 128, T], BF16,
                               kind="ExternalInput").ap()
    stream_out = nc.dram_tensor("stream_out", [SEQ, NCH, 128, T], BF16,
                                kind="ExternalOutput").ap()
    dbg = None
    if debug:
        dbg = {
            "cwtm": nc.dram_tensor("dbg_cwtm", [8, 128, 64], F32, kind="ExternalOutput").ap(),
            "stm": nc.dram_tensor("dbg_stm", [8, 128, 64], F32, kind="ExternalOutput").ap(),
            "cw": nc.dram_tensor("dbg_cw", [8, 8, 1024], BF16, kind="ExternalOutput").ap(),
            "xhat": nc.dram_tensor("dbg_xhat", [8, NCH, 128, 1024], BF16, kind="ExternalOutput").ap(),
            "y": nc.dram_tensor("dbg_y", [8, NCH, 128, 1024], F32, kind="ExternalOutput").ap(),
        }
    with contextlib.ExitStack() as es:
        ctx = Ctx(nc, es)
        ctx.init_epscols()
        W = declare_B_weights(nc)
        consts = declare_consts(nc)
        ctx.push_scope()
        emit_phase_B(ctx, W, stream_in, stream_out, consts, dbg=dbg)
        ctx.flush("B")
        ctx.pop_scope()
    return nc


def run_B(inputs, stream_bf16_percore, debug=False):
    nc = build_B_standalone(debug)
    W = dmap_prefix(prep_B(inputs), "B")
    in_maps = []
    for c in range(NCORES):
        m = dict(W)
        m.update(consts_np())
        m["stream_in"] = stream_bf16_percore[c]
        in_maps.append(m)
    res = run_bass_kernel_spmd(nc, in_maps, core_ids=list(range(NCORES)))
    if debug:
        return res.results
    return [r["stream_out"] for r in res.results]



C_ES = 64.0 ** (-0.25)
C_SECTIONS = {"p1", "p2", "p3"}
C_WAVES = (0, 1, 2, 3)
C_INJECT = False
NBUK = T // 64
NTT = T // 128


def emit_phase_C_consts(ctx, W, consts):
    H = {}
    H["selq16"] = ctx.sbuf("Cselq", (128, 128), BF16)
    H["selb16"] = ctx.sbuf("Cselb", (16, 1024), BF16)
    H["b2ind"] = ctx.sbuf("Cb2i", (128, 2), BF16)
    H["ones"] = ctx.sbuf("Cones", (128, 128), BF16)
    H["ones32"] = ctx.sbuf("Cones32", (128, 32), F32)
    H["bo2d"] = ctx.sbuf("Cbo", (128, 8), F32)
    ctx.sync.dma_start(out=H["selq16"][:, :], in_=consts["selq16"])
    ctx.sync.dma_start(out=H["selb16"][:, :], in_=consts["selb16"])
    ctx.sync.dma_start(out=H["b2ind"][:, :], in_=consts["b2ind"])
    ctx.sync.dma_start(out=H["ones"][:, :], in_=consts["ones128"])
    ctx.sync.dma_start(out=H["bo2d"][:, :], in_=W["bo2d"])
    ctx.vector.memset(H["ones32"][:, :], 1.0)
    H["ready"] = ctx.snap("Y", "V")
    return H


def emit_phase_C_seq(ctx, W, H, stream_in, stream_out, s, dbg=None):
    attn_sb = ctx.sbuf("Cattn", (128, NCH * T), BF16)
    ksum_sb = ctx.sbuf("Cks", (128, 256), F32)
    kcum_sb = ctx.sbuf("Ckc", (128, 256), F32)
    rkc_sb = ctx.sbuf("Crkc", (128, 256), F32)

    ps_a = ctx.psum("Cpsa", (128, 512), F32)
    ps_b = ctx.psum("Cpsb", (128, 512), F32)

    ctx.push_scope()
    qe_sb = ctx.sbuf("Cqe", (128, NCH * T), BF16)
    fk_sb = ctx.sbuf("Cfk", (128, NTT * 1024), BF16)
    v_sb = ctx.sbuf("Cv", (128, NTT * 1024), BF16)

    ctx.push_scope()
    x_sb = ctx.sbuf("Cx", (128, NCH * T), BF16)
    wq_sb = [ctx.sbuf(f"Cwq{i}", (128, 1024), BF16) for i in range(2)]
    wkv_sb = ctx.sbuf("Cwkv", (128, NCH * 1024), BF16)
    rq_sb = ctx.sbuf("Crq", (16, T), BF16)
    rq32_sb = ctx.sbuf("Crq32", (16, 512), F32)
    ks_scr = ctx.sbuf("Ckss", (128, 6 * 512), F32)

    for c in range(NCH):
        ctx.sync.dma_start(out=x_sb[:, c * T:(c + 1) * T],
                           in_=stream_in[s, c, :, 0:T],
                           wait=(H["ready"] if c == 0 else None))
    x_ready = ctx.snap("Y")

    if "p1" in C_SECTIONS:
        evac_prev = [None, None]
        for m in range(NCH):
            slot = m % 2
            w = evac_prev[slot]
            ctx.gpsimd.dma_start(out=wq_sb[slot][:, :], in_=W["wqT"][m],
                                 wait=([w] if w else None))
            wl = ctx.snap("G")
            for ns in range(4):
                pslot = ns % 2
                pp = [ps_a, ps_b][pslot]
                for k in range(NCH):
                    w = None
                    if k == 0:
                        w = wl + x_ready
                        if evac_prev[pslot]:
                            w = w + [evac_prev[pslot]]
                    ctx.tensor.matmul(pp[:, :], wq_sb[slot][:, k * 128:(k + 1) * 128],
                                      x_sb[:, k * T + ns * 512: k * T + (ns + 1) * 512],
                                      start=(k == 0), stop=(k == NCH - 1),
                                      wait=w, tinc=(k == NCH - 1))
                mm_now = ctx.snap("T")
                evac_prev[pslot] = ctx.scalar.activation(
                    out=qe_sb[:, m * T + ns * 512: m * T + (ns + 1) * 512],
                    in_=pp[:, :], func=AF.Exp, scale=C_ES, wait=mm_now)
        qe_done = ctx.snap("S")

        rq_marks = []
        for ns in range(4):
            pslot = ns % 2
            pp = [ps_a, ps_b][pslot]
            for c in range(NCH):
                w = None
                if c == 0:
                    w = list(qe_done)
                    if ns < 2:
                        w += [evac_prev[0], evac_prev[1]]
                    if ns >= 2:
                        w += [rq_marks[ns - 2]]
                ctx.tensor.matmul(pp[0:16, :], H["selq16"][:, c * 16:(c + 1) * 16],
                                  qe_sb[:, c * T + ns * 512: c * T + (ns + 1) * 512],
                                  start=(c == 0), stop=(c == NCH - 1),
                                  wait=w, tinc=(c == NCH - 1))
            mm_now = ctx.snap("T")
            w = list(mm_now)
            if rq_marks:
                w.append(rq_marks[-1])
            mkr = ctx.vector.reciprocal(rq32_sb[0:16, :], pp[0:16, :], wait=w)
            rq_marks.append(ctx.scalar.activation(
                out=rq_sb[0:16, ns * 512:(ns + 1) * 512], in_=rq32_sb[0:16, :],
                func=AF.Identity, wait=[mkr]))

        rb_sb = ctx.sbuf("Crb", (128, 2 * 512), BF16)
        sq_marks = []
        cp_marks = []
        for c in range(NCH):
            for ns in range(4):
                pslot = ns % 2
                pp = [ps_a, ps_b][pslot]
                idx = c * 4 + ns
                w = [rq_marks[ns]]
                if idx >= 2:
                    w.append(cp_marks[idx - 2])
                else:
                    w.append(rq_marks[ns + 2])
                ctx.tensor.matmul(pp[:, :], H["selb16"][0:16, c * 128:(c + 1) * 128],
                                  rq_sb[0:16, ns * 512:(ns + 1) * 512],
                                  start=True, stop=True, wait=w, tinc=True)
                mm_now = ctx.snap("T")
                rbc = rb_sb[:, pslot * 512:(pslot + 1) * 512]
                w2 = list(mm_now)
                if idx >= 2:
                    w2.append(sq_marks[idx - 2])
                cp = ctx.scalar.activation(out=rbc, in_=pp[:, :], func=AF.Identity,
                                           wait=w2)
                cp_marks.append(cp)
                cols = qe_sb[:, c * T + ns * 512: c * T + (ns + 1) * 512]
                sq_marks.append(ctx.vector.tensor_mul(cols, cols, rbc, wait=[cp]))
        sq_done = [sq_marks[-1], sq_marks[-2]]

        if dbg is not None and s == 0:
            ctx.sync.dma_start(out=dbg["rq"], in_=rq_sb[0:16, :], wait=sq_done)
            for c in range(NCH):
                ctx.sync.dma_start(out=dbg["sq"][c], in_=qe_sb[:, c * T:(c + 1) * T],
                                   wait=(sq_done if c == 0 else None))

    else:
        sq_done = list(x_ready)
    if "p2" in C_SECTIONS:
        for which, wsrc, dst in (("k", W["wkT_rhs"], fk_sb), ("v", W["wvT_rhs"], v_sb)):
            wv_gate = ctx.snap("T") if which == "v" else None
            for c in range(NCH):
                ctx.gpsimd.dma_start(out=wkv_sb[:, c * 1024:(c + 1) * 1024],
                                     in_=wsrc[c],
                                     wait=(wv_gate if c == 0 else None))
            wl = ctx.snap("G")
            ev_pr = [None, None]
            for m in range(NTT):
                for osl in range(2):
                    pslot = (m * 2 + osl) % 2
                    pp = [ps_a, ps_b][pslot]
                    for k in range(NCH):
                        w = None
                        if k == 0:
                            w = wl + x_ready
                            if which == "k" and m == 0 and osl == 0:
                                w = w + sq_done
                            if ev_pr[pslot]:
                                w = w + [ev_pr[pslot]]
                        ctx.tensor.matmul(
                            pp[:, :],
                            x_sb[:, k * T + m * 128: k * T + (m + 1) * 128],
                            wkv_sb[:, k * 1024 + osl * 512: k * 1024 + (osl + 1) * 512],
                            start=(k == 0), stop=(k == NCH - 1),
                            wait=w, tinc=(k == NCH - 1))
                    mm_now = ctx.snap("T")
                    dcols = dst[:, m * 1024 + osl * 512: m * 1024 + (osl + 1) * 512]
                    if which == "v":
                        ev_pr[pslot] = ctx.scalar.activation(
                            out=dcols, in_=pp[:, :], func=AF.Identity, wait=mm_now)
                    else:
                        off = pslot * 1536
                        ks = ks_scr[:, off:off + 512]
                        t2 = ks_scr[:, off + 512:off + 1024]
                        t3 = ks_scr[:, off + 1024:off + 1536]
                        a1 = ctx.scalar.activation(out=ks, in_=pp[:, :],
                                                   func=AF.Identity, scale=C_ES,
                                                   wait=mm_now + ([ev_pr[pslot]] if ev_pr[pslot] else []))
                        a2 = ctx.vector.tensor_single_scalar(t2, ks, 0.0, ALU.min,
                                                             wait=[a1])
                        a3 = ctx.scalar.activation(out=t3, in_=t2, func=AF.Exp,
                                                   wait=[a2])
                        a4 = ctx.vector.tensor_single_scalar(t2, ks, 0.0, ALU.max,
                                                             wait=[a1, a3])
                        ev_pr[pslot] = ctx.vector.tensor_add(dcols, t3, t2,
                                                             wait=[a4, a3])
            if which == "k":
                fk_done = [ev_pr[0], ev_pr[1]]
            else:
                v_done = [ev_pr[0], ev_pr[1]]

    else:
        fk_done = list(x_ready)
        v_done = list(x_ready)
    if dbg is not None and s == 0:
        for m in range(NTT):
            ctx.sync.dma_start(out=dbg["fk"][m], in_=fk_sb[:, m * 1024:(m + 1) * 1024],
                               wait=(fk_done + v_done if m == 0 else None))
            ctx.sync.dma_start(out=dbg["v"][m], in_=v_sb[:, m * 1024:(m + 1) * 1024])
    ctx.pop_scope()

    if "p3" in C_SECTIONS:
        for c in range(NCH):
            for m in range(NTT):
                w = None
                if c == 0 and m == 0:
                    w = fk_done + v_done
                ctx.tensor.matmul(ps_a[:, c * 32 + 2 * m: c * 32 + 2 * m + 2],
                                  fk_sb[:, m * 1024 + c * 128: m * 1024 + (c + 1) * 128],
                                  H["b2ind"][:, :], start=True, stop=True,
                                  wait=w, tinc=(c == NCH - 1 and m == NTT - 1))
        mm_now = ctx.snap("T")
        mkk = ctx.scalar.activation(out=ksum_sb[:, :], in_=ps_a[:, 0:256],
                                    func=AF.Identity, wait=mm_now)
        scan_mk = mkk
        for c in range(NCH):
            scan_mk = ctx.vector.tensor_tensor_scan(
                kcum_sb[:, c * 32:(c + 1) * 32], ksum_sb[:, c * 32:(c + 1) * 32],
                H["ones32"][:, 0:32], 0.0, ALU.add, ALU.mult,
                wait=([mkk] if c == 0 else None))
        mke = ctx.vector.tensor_single_scalar(kcum_sb[:, :], kcum_sb[:, :], CTX_EPS,
                                              ALU.add, wait=[scan_mk])
        mkr = ctx.vector.reciprocal(rkc_sb[:, :], kcum_sb[:, :], wait=[mke])
        rkc_done = [mkr]
        if dbg is not None and s == 0:
            ctx.sync.dma_start(out=dbg["rkc"], in_=rkc_sb[:, :], wait=[mkr])


    else:
        rkc_done = list(x_ready)
    if dbg is not None and C_INJECT and s == 0:
        mk = None
        w0 = sq_done + fk_done + v_done + rkc_done
        for c in range(NCH):
            mk = ctx.gpsimd.dma_start(out=qe_sb[:, c * T:(c + 1) * T],
                                      in_=dbg["inj_sq"][c],
                                      wait=(w0 if c == 0 else None))
        for m in range(NTT):
            mk = ctx.gpsimd.dma_start(out=fk_sb[:, m * 1024:(m + 1) * 1024],
                                      in_=dbg["inj_fk"][m])
            mk = ctx.gpsimd.dma_start(out=v_sb[:, m * 1024:(m + 1) * 1024],
                                      in_=dbg["inj_v"][m])
        mk2 = ctx.sync.dma_start(out=rkc_sb[:, :], in_=dbg["inj_rkc"])
        sq_done = [mk, mk2]
        fk_done = [mk]
        v_done = [mk]
        rkc_done = [mk2]
    z_marks = []
    for c in range(NCH):
        z_marks.append(ctx.vector.memset(attn_sb[:, c * T:c * T + 64], 0.0))
    zero_done = [z_marks[-1]]

    ctx.push_scope()
    ctx_ps = [ctx.psum(f"Cctx{i}", (128, 64), F32) for i in range(4)]
    ps_c = ctx.psum("Cpsc", (128, 512), F32)
    ps_d = ctx.psum("Cpsd", (128, 512), F32)
    aps = [ps_a, ps_b, ps_c, ps_d]
    ctxt_sb = ctx.sbuf("Cctxt", (128, 4 * 64), BF16)
    free_attn = [None, None, None, None]
    wave_end = []

    for wave in C_WAVES:
        chunks = [2 * wave, 2 * wave + 1]
        heads = [2 * c + par for c in chunks for par in range(2)]
        ev_ctx = {h: None for h in heads}
        mm_ctx = {h: None for h in heads}
        mm_attn_prev = {h: None for h in heads}
        for u in range(NBUK):
            for hi, h in enumerate(heads):
                c = h // 2
                ci = chunks.index(c)
                par = h % 2
                bank = ctx_ps[ci * 2 + par]
                rowsl = slice(par * 64, par * 64 + 64)
                tcols = slice((u // 2) * 1024 + h * 64, (u // 2) * 1024 + h * 64 + 64)
                trows = slice((u % 2) * 64, (u % 2) * 64 + 64)
                w = []
                if ev_ctx[h]:
                    w.append(ev_ctx[h])
                if u == 0:
                    w = w + fk_done + v_done + zero_done
                    if wave_end:
                        w = w + wave_end
                mm_ctx[h] = ctx.tensor.matmul(
                    bank[rowsl, :],
                    fk_sb[trows, tcols], v_sb[trows, tcols],
                    start=(u == 0), stop=(u == NBUK - 1),
                    wait=(w or None), tinc=True,
                    tile_position=((u % 2) * 64, par * 64))
                if u < NBUK - 1:
                    ev_ctx[h] = ctx.scalar.activation(
                        out=ctxt_sb[rowsl, (ci * 2 + par) * 64:
                                    (ci * 2 + par) * 64 + 64],
                        in_=bank[rowsl, :], func=AF.Identity,
                        scale=rkc_sb[rowsl.start:rowsl.start + 64,
                                     c * 32 + u: c * 32 + u + 1],
                        wait=[mm_ctx[h]] + (rkc_done if u == 0 else []))
                    if dbg is not None and s == 0 and wave == 0 and h == 0 and u == 0:
                        dmk = ctx.sync.dma_start(out=dbg["ctxt0"],
                                                 in_=ctxt_sb[0:64, 0:64],
                                                 wait=[ev_ctx[h]])
                        ev_ctx[h] = dmk
                    au = u + 1
                    aslot = ((au - 1) // 8) % 2
                    pa = aps[ci * 2 + aslot]
                    w = [ev_ctx[h]]
                    if (au - 1) % 8 == 0 and par == 0 and free_attn[ci * 2 + aslot]:
                        w.append(free_attn[ci * 2 + aslot])
                    if hi == 0 and u == 0 and not wave_end:
                        w = w + sq_done + rkc_done
                    mm_attn_prev[h] = ctx.tensor.matmul(
                        pa[rowsl, ((au - 1) % 8) * 64: ((au - 1) % 8) * 64 + 64],
                        ctxt_sb[rowsl, (ci * 2 + par) * 64:
                                (ci * 2 + par) * 64 + 64],
                        qe_sb[rowsl, c * T + au * 64: c * T + (au + 1) * 64],
                        start=True, stop=True, wait=w, tinc=True,
                        tile_position=(par * 64, par * 64))
            if u >= 8 and (u % 8 == 0):
                g = u // 8 - 1
                aslot = g % 2
                for ci, c in enumerate(chunks):
                    pa = aps[ci * 2 + aslot]
                    wv = [mm_attn_prev[2 * c], mm_attn_prev[2 * c + 1]]
                    free_attn[ci * 2 + aslot] = ctx.vector.tensor_copy(
                        attn_sb[:, c * T + (8 * g + 1) * 64: c * T + (8 * g + 9) * 64],
                        pa[:, :], wait=wv)
        g = 3
        aslot = g % 2
        wave_end = []
        for ci, c in enumerate(chunks):
            pa = aps[ci * 2 + aslot]
            wv = [mm_attn_prev[2 * c], mm_attn_prev[2 * c + 1]]
            last = ctx.vector.tensor_copy(
                attn_sb[:, c * T + (8 * g + 1) * 64: c * T + (8 * g + 8) * 64],
                pa[:, 0:448], wait=wv)
            free_attn[ci * 2 + aslot] = last
            wave_end.append(last)
        wave_end = wave_end + [mm_ctx[h] for h in heads]
    ctx.pop_scope()
    if dbg is not None and s == 0:
        for c in range(NCH):
            ctx.sync.dma_start(out=dbg["attn"][c], in_=attn_sb[:, c * T:(c + 1) * T],
                               wait=(ctx.snap("V") if c == 0 else None))
    ctx.pop_scope()
    attn_done = ctx.snap("V", "T")

    ctx.push_scope()
    wo_sb = ctx.sbuf("Cwo", (128, NCH * 1024), BF16)
    u_sb = ctx.sbuf("Cu", (128, NCH * 512), F32)
    xt_sb = ctx.sbuf("Cxt", (128, NCH * 512), BF16)
    ubf_sb = ctx.sbuf("Cubf", (128, NCH * 512), BF16)
    usq_sb = ctx.sbuf("Cusq", (128, NCH * 512), BF16)
    stat_sb = ctx.sbuf("Cstat", (128, 4 * 512), F32)
    out_sb = ctx.sbuf("Cout", (128, NCH * 512), BF16)
    for m in range(NCH):
        ctx.gpsimd.dma_start(out=wo_sb[:, m * 1024:(m + 1) * 1024],
                             in_=W["woT"][m],
                             wait=(attn_done if m == 0 else None))
    wl = ctx.snap("G")
    stats_mm_prev = None
    norm_prev = None
    st_prev = None
    for ns in range(4):
        xw = list(attn_done)
        if st_prev:
            xw = xw + [st_prev]
        for c in range(NCH):
            ctx.sync.dma_start(out=xt_sb[:, c * 512:(c + 1) * 512],
                               in_=stream_in[s, c, :, ns * 512:(ns + 1) * 512],
                               wait=(xw if c == 0 else None))
        xt_ready = ctx.snap("Y")
        u_marks = []
        for m in range(NCH):
            pslot = m % 2
            pp = [ps_a, ps_b][pslot]
            for k in range(NCH):
                w = None
                if k == 0:
                    w = wl + xt_ready + attn_done
                    if m >= 2:
                        w = w + [u_marks[m - 2]]
                    elif norm_prev:
                        w = w + norm_prev
                ctx.tensor.matmul(pp[:, :],
                                  wo_sb[:, m * 1024 + k * 128: m * 1024 + (k + 1) * 128],
                                  attn_sb[:, k * T + ns * 512: k * T + (ns + 1) * 512],
                                  start=(k == 0), stop=(k == NCH - 1),
                                  wait=w, tinc=(k == NCH - 1))
            mm_now = ctx.snap("T")
            u_marks.append(ctx.vector.scalar_tensor_tensor(
                u_sb[:, m * 512:(m + 1) * 512], pp[:, :],
                H["bo2d"][:, m:m + 1], xt_sb[:, m * 512:(m + 1) * 512],
                ALU.add, ALU.add, wait=mm_now))
        dep_u = [u_marks[-1]]
        cm = []
        for c in range(NCH):
            ucols = u_sb[:, c * 512:(c + 1) * 512]
            w = list(dep_u)
            if c == 0 and stats_mm_prev:
                w += stats_mm_prev
            cm.append(ctx.vector.tensor_copy(ubf_sb[:, c * 512:(c + 1) * 512],
                                             ucols, wait=(w if c == 0 else dep_u)))
            cm.append(ctx.scalar.activation(out=usq_sb[:, c * 512:(c + 1) * 512],
                                            in_=ucols, func=AF.Square,
                                            wait=(w if c == 0 else dep_u)))
        dep = [cm[-2], cm[-1]]
        mu, rstd = emit_ln_stats(ctx, 512, LN_EPS_OUTER,
                                 lambda c: ubf_sb[:, c * 512:(c + 1) * 512],
                                 lambda c: usq_sb[:, c * 512:(c + 1) * 512],
                                 H["ones"], ps_a, ps_b, stat_sb, dep)
        stats_mm_prev = ctx.snap("T")
        rstd_done = ctx.snap("S")
        last = None
        for c in range(NCH):
            ucols = u_sb[:, c * 512:(c + 1) * 512]
            w = list(rstd_done) if c == 0 else None
            mk = ctx.vector.tensor_sub(ucols, ucols, mu, wait=w)
            last = ctx.vector.tensor_mul(out_sb[:, c * 512:(c + 1) * 512],
                                         ucols, rstd, wait=[mk])
        norm_prev = [last] + stats_mm_prev
        for c in range(NCH):
            st_prev = ctx.sync.dma_start(
                out=stream_out[s, c, :, ns * 512:(ns + 1) * 512],
                in_=out_sb[:, c * 512:(c + 1) * 512],
                wait=([last] if c == 0 else None))
    ctx.pop_scope()
    ctx.sync.wait_ge("Y", ctx.c("Y"))


def prep_C(inputs):
    W = {}
    W["wqT"] = tiles_lhsT(np.ascontiguousarray(inputs["L1_attn_Wq"].T))
    W["woT"] = tiles_lhsT(np.ascontiguousarray(inputs["L1_attn_Wo"].T))
    W["wkT_rhs"] = np.ascontiguousarray(inputs["L1_attn_Wk"].T).reshape(8, 128, 1024)
    W["wvT_rhs"] = np.ascontiguousarray(inputs["L1_attn_Wv"].T).reshape(8, 128, 1024)
    W["bo2d"] = vec2d(inputs["L1_attn_bo"])
    return W


def consts_C_np():
    selq = np.zeros((128, 128), dtype=BF16NP)
    selb = np.zeros((16, 1024), dtype=BF16NP)
    for c in range(8):
        for p in range(128):
            h = 2 * c + (p // 64)
            selq[p, c * 16 + h] = 1
            selb[h, c * 128 + p] = 1
    b2 = np.zeros((128, 2), dtype=BF16NP)
    b2[0:64, 0] = 1
    b2[64:128, 1] = 1
    return {"selq16": selq, "selb16": selb, "b2ind": b2}


def declare_C_consts(nc):
    return {"selq16": nc.dram_tensor("selq16", [128, 128], BF16, kind="ExternalInput").ap(),
            "selb16": nc.dram_tensor("selb16", [16, 1024], BF16, kind="ExternalInput").ap(),
            "b2ind": nc.dram_tensor("b2ind", [128, 2], BF16, kind="ExternalInput").ap()}


def declare_C_weights(nc):
    aps = {}
    aps["wqT"] = nc.dram_tensor("CwqT", [8, 128, 1024], F32, kind="ExternalInput").ap()
    aps["woT"] = nc.dram_tensor("CwoT", [8, 128, 1024], F32, kind="ExternalInput").ap()
    aps["wkT_rhs"] = nc.dram_tensor("CwkT_rhs", [8, 128, 1024], F32, kind="ExternalInput").ap()
    aps["wvT_rhs"] = nc.dram_tensor("CwvT_rhs", [8, 128, 1024], F32, kind="ExternalInput").ap()
    aps["bo2d"] = nc.dram_tensor("Cbo2d", [128, 8], F32, kind="ExternalInput").ap()
    return aps


def build_C_standalone(debug=False):
    nc = bass.Bass("TRN2", target_bir_lowering=False)
    stream_in = nc.dram_tensor("stream_in", [SEQ, NCH, 128, T], BF16,
                               kind="ExternalInput").ap()
    stream_out = nc.dram_tensor("stream_out", [SEQ, NCH, 128, T], BF16,
                                kind="ExternalOutput").ap()
    dbg = None
    if debug:
        dbg = {
            "sq": nc.dram_tensor("dbg_sq", [NCH, 128, T], BF16, kind="ExternalOutput").ap(),
            "fk": nc.dram_tensor("dbg_fk", [NTT, 128, 1024], BF16, kind="ExternalOutput").ap(),
            "v": nc.dram_tensor("dbg_v", [NTT, 128, 1024], BF16, kind="ExternalOutput").ap(),
            "rkc": nc.dram_tensor("dbg_rkc", [128, 256], F32, kind="ExternalOutput").ap(),
            "rq": nc.dram_tensor("dbg_rq", [16, T], BF16, kind="ExternalOutput").ap(),
            "ctxt0": nc.dram_tensor("dbg_ctxt0", [64, 64], BF16, kind="ExternalOutput").ap(),
            "inj_sq": nc.dram_tensor("inj_sq", [NCH, 128, T], BF16, kind="ExternalInput").ap(),
            "inj_fk": nc.dram_tensor("inj_fk", [NTT, 128, 1024], BF16, kind="ExternalInput").ap(),
            "inj_v": nc.dram_tensor("inj_v", [NTT, 128, 1024], BF16, kind="ExternalInput").ap(),
            "inj_rkc": nc.dram_tensor("inj_rkc", [128, 256], F32, kind="ExternalInput").ap(),

            "attn": nc.dram_tensor("dbg_attn", [NCH, 128, T], BF16, kind="ExternalOutput").ap(),
        }
    with contextlib.ExitStack() as es:
        ctx = Ctx(nc, es)
        ctx.init_epscols()
        W = declare_C_weights(nc)
        consts = declare_consts(nc)
        consts.update(declare_C_consts(nc))
        ctx.push_scope()
        H = emit_phase_C_consts(ctx, W, consts)
        for s in range(SEQ):
            ctx.push_scope()
            emit_phase_C_seq(ctx, W, H, stream_in, stream_out, s, dbg=dbg)
            ctx.flush(f"C_s{s}")
            ctx.pop_scope()
        ctx.pop_scope()
    return nc


def run_C(inputs, stream_bf16_percore, debug=False):
    nc = build_C_standalone(debug)
    W = dmap_prefix(prep_C(inputs), "C")
    in_maps = []
    for c in range(NCORES):
        m = dict(W)
        m.update(consts_np())
        m.update(consts_C_np())
        m["stream_in"] = stream_bf16_percore[c]
        if debug:
            m["inj_sq"] = np.zeros((NCH, 128, T), BF16NP)
            m["inj_fk"] = np.zeros((NTT, 128, 1024), BF16NP)
            m["inj_v"] = np.zeros((NTT, 128, 1024), BF16NP)
            m["inj_rkc"] = np.zeros((128, 256), np.float32)
        in_maps.append(m)
    res = run_bass_kernel_spmd(nc, in_maps, core_ids=list(range(NCORES)))
    if debug:
        return res.results
    return [r["stream_out"] for r in res.results]



A_SCALE = 64.0 ** (-0.5)


def emit_phase_A_consts(ctx, W, consts):
    H = {}
    H["ident"] = ctx.sbuf("Aid", (128, 128), BF16)
    H["ones"] = ctx.sbuf("Aones", (128, 128), BF16)
    H["b2d"] = ctx.sbuf("Ab2d", (128, 8 + 8 + 8 + 8), F32)
    H["bqk2d"] = H["b2d"][:, 0:8]
    H["bout2d"] = H["b2d"][:, 8:16]
    H["g2d"] = H["b2d"][:, 16:24]
    H["be2d"] = H["b2d"][:, 24:32]
    H["bvrow"] = ctx.sbuf("Abv", (1, 512), BF16)
    H["onesk1"] = ctx.sbuf("Ak1", (1, 128), BF16)
    H["bvrep"] = ctx.sbuf("Abvr", (128, 512), F32)
    ctx.sync.dma_start(out=H["ident"][:, :], in_=consts["ident128b"])
    ctx.sync.dma_start(out=H["ones"][:, :], in_=consts["ones128"])
    ctx.sync.dma_start(out=H["b2d"][:, 0:8], in_=W["bqk2d"])
    ctx.sync.dma_start(out=H["b2d"][:, 8:16], in_=W["bout2d"])
    ctx.sync.dma_start(out=H["b2d"][:, 16:24], in_=W["g2d"])
    ctx.sync.dma_start(out=H["b2d"][:, 24:32], in_=W["be2d"])
    ctx.gpsimd.dma_start(out=H["bvrow"][0:1, :], in_=W["bv_row"])
    ctx.gpsimd.dma_start(out=H["onesk1"][0:1, :], in_=consts["ones128"][0:1, :])
    ld = ctx.snap("Y", "G")
    H["wqk"] = ctx.sbuf("Awqk", (128, NCH * 1024), BF16)
    H["wv"] = ctx.sbuf("Awv", (128, NCH * 512), BF16)
    H["wout"] = ctx.sbuf("Awout", (128, 4 * 1024), BF16)
    for c in range(NCH):
        ctx.gpsimd.dma_start(out=H["wqk"][:, c * 1024:(c + 1) * 1024],
                             in_=W["wqkT"][c])
        ctx.gpsimd.dma_start(out=H["wv"][:, c * 512:(c + 1) * 512],
                             in_=W["wv_rhs"][c])
    for m in range(NCH):
        ctx.gpsimd.dma_start(out=H["wout"][:, m * 512:(m + 1) * 512],
                             in_=W["woutT"][m])
    H["ready"] = ctx.snap("Y", "G")
    ctx.push_scope()
    ps = ctx.psum("Abvps", (128, 512), F32)
    mk = ctx.tensor.matmul(ps[:, :], H["onesk1"][0:1, :], H["bvrow"][0:1, :],
                           start=True, stop=True, wait=ld, tinc=True)
    H["bv_done"] = [ctx.scalar.activation(out=H["bvrep"][:, :], in_=ps[:, :],
                                          func=AF.Identity, wait=[mk])]
    ctx.pop_scope()
    return H


def emit_phase_A_seq(ctx, W, H, x_fm, stream_out, s):
    qk_sb = ctx.sbuf("Aqk", (128, NCH * T), BF16)
    v_sb = ctx.sbuf("Av", (128, NTT * 512), BF16)
    o_fm = ctx.sbuf("Aofm", (128, 4 * T), BF16)

    ps_s = [ctx.psum(f"Aps{i}", (128, 512), F32) for i in range(4)]
    ps_t = [ctx.psum(f"Apt{i}", (128, 512), BF16) for i in range(2)]
    ps_pv = ctx.psum("Apv", (128, 64), F32)
    ps_ot = ctx.psum("Aot", (128, 128), BF16)

    ctx.push_scope()
    x_sb = ctx.sbuf("Ax", (128, NCH * T), BF16)
    for c in range(NCH):
        ctx.gpsimd.dma_start(out=x_sb[:, c * T:(c + 1) * T],
                             in_=x_fm[s, c, :, 0:T],
                             wait=(H["ready"] if c == 0 else None))
    x_ready = ctx.snap("G")

    ev_pr = [None, None, None, None]
    for m in range(NCH):
        for ns in range(4):
            pslot = ns % 4
            pp = ps_s[pslot]
            for k in range(NCH):
                w = None
                if k == 0:
                    w = list(x_ready) + H["bv_done"]
                    if ev_pr[pslot]:
                        w = w + [ev_pr[pslot]]
                ctx.tensor.matmul(pp[:, :],
                                  H["wqk"][:, m * 1024 + k * 128: m * 1024 + (k + 1) * 128],
                                  x_sb[:, k * T + ns * 512: k * T + (ns + 1) * 512],
                                  start=(k == 0), stop=(k == NCH - 1),
                                  wait=w, tinc=(k == NCH - 1))
            mm_now = ctx.snap("T")
            ev_pr[pslot] = ctx.scalar.activation(
                out=qk_sb[:, m * T + ns * 512: m * T + (ns + 1) * 512],
                in_=pp[:, :], func=AF.Identity,
                bias=H["bqk2d"][:, m:m + 1], wait=mm_now)
    qk_done = ctx.snap("S")

    for mt in range(NTT):
        pslot = mt % 4
        pp = ps_s[pslot]
        for k in range(NCH):
            w = None
            if k == 0:
                w = [ev_pr[pslot]] if ev_pr[pslot] else list(x_ready)
            ctx.tensor.matmul(pp[:, :],
                              x_sb[:, k * T + mt * 128: k * T + (mt + 1) * 128],
                              H["wv"][:, k * 512:(k + 1) * 512],
                              start=(k == 0), stop=(k == NCH - 1),
                              wait=w, tinc=(k == NCH - 1))
        mm_now = ctx.snap("T")
        ev_pr[pslot] = ctx.vector.tensor_add(v_sb[:, mt * 512:(mt + 1) * 512],
                                             pp[:, :], H["bvrep"][:, :],
                                             wait=mm_now)
    v_done = [ev_pr[i] for i in range(4) if ev_pr[i]]
    ctx.pop_scope()

    ctx.push_scope()
    p_sb = [ctx.sbuf(f"Ap{i}", (128, T), BF16) for i in range(2)]
    pt_sb = [ctx.sbuf(f"Apt{i}", (128, T), BF16) for i in range(2)]
    otm_sb = [ctx.sbuf(f"Aotm{i}", (128, 64), BF16) for i in range(2)]
    sm_sb = ctx.sbuf("Asm", (128, 16), F32)
    mcols = sm_sb[:, 0:4]
    mm_ = sm_sb[:, 4:5]
    msc = sm_sb[:, 5:6]
    acc4 = sm_sb[:, 8:12]
    rs = sm_sb[:, 6:7]
    rr = sm_sb[:, 7:8]

    exp_marks = [None, None, None, None]
    tev_marks = [None, None]
    pv_mm_slot = [None, None]
    pv_mark = None
    ot_mark = None
    otm_mark = [None, None]
    pt_done = [None, None]

    units = [(h, it) for h in range(8) for it in range(NTT)]
    for ui, (h, it) in enumerate(units):
        par = h % 2
        qrows = slice(par * 64, par * 64 + 64)
        qcols = slice((h // 2) * T + it * 128, (h // 2) * T + (it + 1) * 128)
        uslot = ui % 2
        for j in range(4):
            w = []
            if exp_marks[j]:
                w.append(exp_marks[j])
            if ui == 0 and j == 0:
                w = w + qk_done + v_done
            ctx.tensor.matmul(
                ps_s[j][:, :], qk_sb[qrows, qcols],
                qk_sb[qrows, (4 + h // 2) * T + j * 512:(4 + h // 2) * T + (j + 1) * 512],
                start=True, stop=True, wait=(w or None), tinc=True,
                tile_position=(par * 64, 0))
        mm_s = ctx.snap("T")
        mk = None
        for j in range(4):
            mk = ctx.vector.tensor_reduce(mcols[:, j:j + 1], ps_s[j][:, :],
                                          AX.X, ALU.max,
                                          wait=(mm_s if j == 0 else [mk]))
        mk = ctx.vector.tensor_reduce(mm_, mcols, AX.X, ALU.max, wait=[mk])
        mk_msc = ctx.vector.tensor_single_scalar(msc, mm_, -A_SCALE, ALU.mult,
                                                 wait=[mk])
        emk = []
        for j in range(4):
            w = [mk_msc]
            if j == 0 and pt_done[uslot]:
                w.append(pt_done[uslot])
            emk.append(ctx.scalar.activation(
                out=p_sb[uslot][:, j * 512:(j + 1) * 512], in_=ps_s[j][:, :],
                func=AF.Exp, scale=A_SCALE, bias=msc,
                accum_out=acc4[:, j:j + 1],
                wait=(w if j == 0 else [mk_msc])))
            exp_marks[j] = emk[-1]
        mk = ctx.vector.tensor_reduce(rs, acc4, AX.X, ALU.add, wait=[emk[-1]])
        rmk = ctx.vector.reciprocal(rr, rs, wait=[mk])
        tmarks = []
        for jt in range(NTT):
            tb = (jt // 4) % 2
            w = [emk[-1]]
            if tev_marks[tb] and jt % 4 == 0:
                w.append(tev_marks[tb])
            tmarks.append(ctx.tensor.transpose(
                ps_t[tb][:, (jt % 4) * 128:(jt % 4 + 1) * 128],
                p_sb[uslot][:, jt * 128:(jt + 1) * 128],
                H["ident"][:, :], wait=w, tinc=True))
            if jt % 4 == 3:
                wv2 = [tmarks[-1]]
                if jt == 3 and pv_mm_slot[uslot]:
                    wv2 = wv2 + pv_mm_slot[uslot]
                tev_marks[tb] = ctx.vector.tensor_copy(
                    pt_sb[uslot][:, (jt - 3) * 128:(jt + 1) * 128],
                    ps_t[tb][:, :], wait=wv2)
        pt_done[uslot] = tev_marks[(NTT - 1) // 4 % 2]
        for jt in range(NTT):
            w = None
            if jt == 0:
                w = [tev_marks[0], tev_marks[1]]
                if pv_mark:
                    w.append(pv_mark)
            ctx.tensor.matmul(ps_pv[:, :],
                              pt_sb[uslot][:, jt * 128:(jt + 1) * 128],
                              v_sb[:, jt * 512 + h * 64: jt * 512 + (h + 1) * 64],
                              start=(jt == 0), stop=(jt == NTT - 1),
                              wait=w, tinc=(jt == NTT - 1))
        mm_pv = ctx.snap("T")
        pv_mm_slot[uslot] = mm_pv
        w = mm_pv + [rmk]
        if otm_mark[uslot]:
            w.append(otm_mark[uslot])
        pv_mark = ctx.vector.tensor_scalar_mul(otm_sb[uslot][:, :], ps_pv[:, :],
                                               rr, wait=w)
        w = [pv_mark]
        if ot_mark:
            w.append(ot_mark)
        tmk = ctx.tensor.transpose(ps_ot[par * 64:par * 64 + 64, :],
                                   otm_sb[uslot][:, :], H["ident"][:, :],
                                   wait=w, tinc=True,
                                   tile_position=(0, par * 64))
        ot_mark = ctx.scalar.activation(
            out=o_fm[par * 64 + 0:par * 64 + 64,
                     (h // 2) * T + it * 128:(h // 2) * T + (it + 1) * 128],
            in_=ps_ot[par * 64:par * 64 + 64, :], func=AF.Identity, wait=[tmk])
        otm_mark[uslot] = ot_mark
    attn_done = ctx.snap("V", "S", "T")
    ctx.pop_scope()

    ctx.push_scope()
    xt_sb = ctx.sbuf("Axt", (128, NCH * 512), BF16)
    y_sb = ctx.sbuf("Ay", (128, NCH * 512), F32)
    zt_sb = ctx.sbuf("Azt", (128, 512), F32)
    ubf_sb = ctx.sbuf("Aubf", (128, NCH * 512), BF16)
    usq_sb = ctx.sbuf("Ausq", (128, NCH * 512), BF16)
    stat_sb = ctx.sbuf("Astat", (128, 4 * 512), F32)
    out_sb = ctx.sbuf("Aout2", (128, NCH * 512), BF16)
    ps_a = ps_s[0]
    ps_b = ps_s[1]

    stats_mm_prev = None
    norm_prev = None
    st_prev = None
    for ns in range(4):
        xw = list(attn_done)
        if st_prev:
            xw.append(st_prev)
        for c in range(NCH):
            ctx.gpsimd.dma_start(out=xt_sb[:, c * 512:(c + 1) * 512],
                                 in_=x_fm[s, c, :, ns * 512:(ns + 1) * 512],
                                 wait=(xw if c == 0 else None))
        xt_ready = ctx.snap("G")
        u_marks = []
        for m in range(NCH):
            pp = ps_s[m % 2]
            for k in range(4):
                w = None
                if k == 0:
                    w = xt_ready + attn_done
                    if m >= 2:
                        w = w + [u_marks[m - 2]]
                    elif norm_prev:
                        w = w + norm_prev
                ctx.tensor.matmul(pp[:, :],
                                  H["wout"][:, m * 512 + k * 128: m * 512 + (k + 1) * 128],
                                  o_fm[:, k * T + ns * 512: k * T + (ns + 1) * 512],
                                  start=(k == 0), stop=(k == 3),
                                  wait=w, tinc=(k == 3))
            mm_now = ctx.snap("T")
            u_marks.append(ctx.scalar.activation(
                out=y_sb[:, m * 512:(m + 1) * 512], in_=pp[:, :],
                func=AF.Identity, bias=H["bout2d"][:, m:m + 1], wait=mm_now))
        dep_u = [u_marks[-1]]

        cm = []
        for c in range(NCH):
            ycols = y_sb[:, c * 512:(c + 1) * 512]
            w = list(dep_u)
            if c == 0 and stats_mm_prev:
                w += stats_mm_prev
            cm.append(ctx.vector.tensor_copy(ubf_sb[:, c * 512:(c + 1) * 512],
                                             ycols, wait=(w if c == 0 else dep_u)))
            cm.append(ctx.scalar.activation(out=usq_sb[:, c * 512:(c + 1) * 512],
                                            in_=ycols, func=AF.Square,
                                            wait=(w if c == 0 else dep_u)))
        dep = [cm[-2], cm[-1]] + (stats_mm_prev or [])
        mu, rstd = emit_ln_stats(ctx, 512, LN_EPS_A,
                                 lambda c: ubf_sb[:, c * 512:(c + 1) * 512],
                                 lambda c: usq_sb[:, c * 512:(c + 1) * 512],
                                 H["ones"], ps_a, ps_b, stat_sb, dep)
        stats_mm_prev = ctx.snap("T")
        rstd_done = ctx.snap("S")
        umk = []
        for c in range(NCH):
            ycols = y_sb[:, c * 512:(c + 1) * 512]
            w = list(rstd_done) if c == 0 else None
            mk = ctx.vector.tensor_sub(zt_sb[:, :], ycols, mu, wait=w)
            mk = ctx.vector.tensor_mul(zt_sb[:, :], zt_sb[:, :], rstd, wait=[mk])
            mk = ctx.vector.tensor_scalar(zt_sb[:, :], zt_sb[:, :],
                                          H["g2d"][:, c:c + 1],
                                          H["be2d"][:, c:c + 1],
                                          ALU.mult, ALU.add, wait=[mk])
            mk = ctx.vector.scalar_tensor_tensor(
                ycols, zt_sb[:, :], KAPPA, xt_sb[:, c * 512:(c + 1) * 512],
                ALU.mult, ALU.add, wait=[mk])
            umk.append(mk)
        dep_u2 = [umk[-1]]

        cm = []
        for c in range(NCH):
            ucols = y_sb[:, c * 512:(c + 1) * 512]
            w = list(dep_u2) + stats_mm_prev
            cm.append(ctx.vector.tensor_copy(ubf_sb[:, c * 512:(c + 1) * 512],
                                             ucols, wait=(w if c == 0 else dep_u2)))
            cm.append(ctx.scalar.activation(out=usq_sb[:, c * 512:(c + 1) * 512],
                                            in_=ucols, func=AF.Square,
                                            wait=(w if c == 0 else dep_u2)))
        dep = [cm[-2], cm[-1]] + stats_mm_prev
        mu, rstd = emit_ln_stats(ctx, 512, LN_EPS_OUTER,
                                 lambda c: ubf_sb[:, c * 512:(c + 1) * 512],
                                 lambda c: usq_sb[:, c * 512:(c + 1) * 512],
                                 H["ones"], ps_a, ps_b, stat_sb, dep)
        stats_mm_prev = ctx.snap("T")
        rstd_done = ctx.snap("S")
        last = None
        for c in range(NCH):
            ucols = y_sb[:, c * 512:(c + 1) * 512]
            w = list(rstd_done) if c == 0 else None
            mk = ctx.vector.tensor_sub(ucols, ucols, mu, wait=w)
            last = ctx.vector.tensor_mul(out_sb[:, c * 512:(c + 1) * 512],
                                         ucols, rstd, wait=[mk])
        norm_prev = [last] + stats_mm_prev
        for c in range(NCH):
            st_prev = ctx.sync.dma_start(
                out=stream_out[s, c, :, ns * 512:(ns + 1) * 512],
                in_=out_sb[:, c * 512:(c + 1) * 512],
                wait=([last] if c == 0 else None))
    ctx.pop_scope()
    ctx.sync.wait_ge("Y", ctx.c("Y"))


def prep_A(inputs):
    W = {}
    Wqk = inputs["L0_attn_W_qk"]
    perm = np.zeros(1024, dtype=np.int64)
    for j in range(8):
        for p in range(128):
            if j < 4:
                h = 2 * j + p // 64
                perm[j * 128 + p] = h * 128 + (p % 64) * 2
            else:
                h = 2 * (j - 4) + p // 64
                perm[j * 128 + p] = h * 128 + (p % 64) * 2 + 1
    Wqk_p = np.ascontiguousarray(Wqk[:, perm])
    W["wqkT"] = tiles_lhsT(Wqk_p)
    W["bqk2d"] = vec2d(inputs["L0_attn_b_qk"][perm])
    W["wv_rhs"] = np.ascontiguousarray(inputs["L0_attn_W_v"].reshape(8, 128, 512))
    W["bv_row"] = np.ascontiguousarray(inputs["L0_attn_b_v"].reshape(1, 512))
    W["woutT"] = tiles_lhsT(inputs["L0_attn_W_out"])
    W["bout2d"] = vec2d(inputs["L0_attn_b_out"])
    W["g2d"] = vec2d(inputs["L0_attn_ln_gamma"])
    W["be2d"] = vec2d(inputs["L0_attn_ln_beta"])
    return W


def declare_A_weights(nc):
    aps = {}
    aps["wqkT"] = nc.dram_tensor("AwqkT", [8, 128, 1024], F32, kind="ExternalInput").ap()
    aps["wv_rhs"] = nc.dram_tensor("Awv_rhs", [8, 128, 512], F32, kind="ExternalInput").ap()
    aps["woutT"] = nc.dram_tensor("AwoutT", [8, 128, 512], F32, kind="ExternalInput").ap()
    aps["bv_row"] = nc.dram_tensor("Abv_row", [1, 512], F32, kind="ExternalInput").ap()
    for n in ("bqk2d", "bout2d", "g2d", "be2d"):
        aps[n] = nc.dram_tensor("A" + n, [128, 8], F32, kind="ExternalInput").ap()
    return aps


def build_A_standalone():
    nc = bass.Bass("TRN2", target_bir_lowering=False)
    x_fm = nc.dram_tensor("x_fm", [SEQ, NCH, 128, T], F32, kind="ExternalInput").ap()
    stream_out = nc.dram_tensor("stream_out", [SEQ, NCH, 128, T], BF16,
                                kind="ExternalOutput").ap()
    with contextlib.ExitStack() as es:
        ctx = Ctx(nc, es)
        ctx.init_epscols()
        W = declare_A_weights(nc)
        consts = declare_consts(nc)
        ctx.push_scope()
        H = emit_phase_A_consts(ctx, W, consts)
        for s in range(SEQ):
            ctx.push_scope()
            emit_phase_A_seq(ctx, W, H, x_fm, stream_out, s)
            ctx.flush(f"A_s{s}")
            ctx.pop_scope()
        ctx.pop_scope()
    return nc


def run_A(inputs, x_fm_percore):
    nc = build_A_standalone()
    W = dmap_prefix(prep_A(inputs), "A")
    in_maps = []
    for c in range(NCORES):
        m = dict(W)
        m.update(consts_np())
        m["x_fm"] = x_fm_percore[c]
        in_maps.append(m)
    res = run_bass_kernel_spmd(nc, in_maps, core_ids=list(range(NCORES)))
    return [r["stream_out"] for r in res.results]


def prep_D(inputs):
    W = {}
    W["w1_T"] = tiles_lhsT(np.ascontiguousarray(inputs["L1_ffn_w1_w"].T))
    W["w2_T"] = tiles_lhsT(np.ascontiguousarray(inputs["L1_ffn_w2_w"].T))
    W["h1_T"] = tiles_lhsT(np.ascontiguousarray(inputs["L1_ffn_h1_w"].T))
    W["h2_T"] = tiles_lhsT(np.ascontiguousarray(inputs["L1_ffn_h2_w"].T))
    W["w3_T"] = tiles_lhsT(np.ascontiguousarray(inputs["L1_ffn_w3_w"].T))
    for n in ("w1", "w2", "h1", "h2", "w3"):
        W[n + "_s2d"] = vec2d(inputs[f"L1_ffn_{n}_s"])
    return W


def consts_np():
    sel8 = np.zeros((8, 1024), dtype=BF16NP)
    for e in range(8):
        sel8[e, e * 128:(e + 1) * 128] = 1
    return {"ones128": np.ones((128, 128), dtype=BF16NP),
            "ident128b": np.eye(128, dtype=BF16NP),
            "ident128f": np.eye(128, dtype=np.float32),
            "sel8": sel8}


def declare_consts(nc):
    return {"ones128": nc.dram_tensor("ones128", [128, 128], BF16, kind="ExternalInput").ap(),
            "ident128b": nc.dram_tensor("ident128b", [128, 128], BF16, kind="ExternalInput").ap(),
            "ident128f": nc.dram_tensor("ident128f", [128, 128], F32, kind="ExternalInput").ap(),
            "sel8": nc.dram_tensor("sel8", [8, 1024], BF16, kind="ExternalInput").ap()}


def declare_D_weights(nc):
    aps = {}
    for name, shape in [
        ("w1_T", [HCH, 128, NCH * 128]), ("w2_T", [HCH, 128, NCH * 128]),
        ("h1_T", [HCH, 128, HCH * 128]), ("h2_T", [HCH, 128, HCH * 128]),
        ("w3_T", [NCH, 128, HCH * 128]),
    ]:
        aps[name] = nc.dram_tensor("D" + name, shape, F32, kind="ExternalInput").ap()
    for name in ("w1_s2d", "w2_s2d", "h1_s2d", "h2_s2d"):
        aps[name] = nc.dram_tensor("D" + name, [128, 32], F32, kind="ExternalInput").ap()
    aps["w3_s2d"] = nc.dram_tensor("Dw3_s2d", [128, 8], F32, kind="ExternalInput").ap()
    return aps


def dmap_prefix(W, prefix):
    return {prefix + k: v for k, v in W.items()}


def build_D_standalone():
    nc = bass.Bass("TRN2", target_bir_lowering=False)
    stream_in = nc.dram_tensor("stream_in", [SEQ, NCH, 128, T], BF16,
                               kind="ExternalInput").ap()
    y_fm = nc.dram_tensor("y_fm", [SEQ, NCH, 128, T], F32,
                          kind="ExternalOutput").ap()
    actA = nc.dram_tensor("actA", [HCH, 128, NTOK], BF16, kind="Internal").ap()
    actB = nc.dram_tensor("actB", [HCH, 128, NTOK], BF16, kind="Internal").ap()
    with contextlib.ExitStack() as es:
        ctx = Ctx(nc, es)
        ctx.init_epscols()
        W = declare_D_weights(nc)
        consts = declare_consts(nc)
        emit_phase_D(ctx, W, stream_in, actA, actB, y_fm, consts)
    return nc


def run_D(inputs, stream_bf16_percore):
    nc = build_D_standalone()
    W = dmap_prefix(prep_D(inputs), "D")
    in_maps = []
    for c in range(NCORES):
        m = dict(W)
        m.update(consts_np())
        m["stream_in"] = stream_bf16_percore[c]
        in_maps.append(m)
    res = run_bass_kernel_spmd(nc, in_maps, core_ids=list(range(NCORES)))
    return [r["y_fm"] for r in res.results]


def build_fused():
    nc = bass.Bass("TRN2", target_bir_lowering=False)
    x_fm = nc.dram_tensor("x_fm", [SEQ, NCH, 128, T], F32, kind="ExternalInput").ap()
    y_fm = nc.dram_tensor("y_fm", [SEQ, NCH, 128, T], F32, kind="ExternalOutput").ap()
    stream = nc.dram_tensor("stream", [SEQ, NCH, 128, T], BF16, kind="Internal").ap()
    actA = nc.dram_tensor("actA", [HCH, 128, NTOK], BF16, kind="Internal").ap()
    actB = nc.dram_tensor("actB", [HCH, 128, NTOK], BF16, kind="Internal").ap()
    with contextlib.ExitStack() as es:
        ctx = Ctx(nc, es)
        ctx.init_epscols()
        consts = declare_consts(nc)
        consts.update(declare_C_consts(nc))
        WA = declare_A_weights(nc)
        WB = declare_B_weights(nc)
        WC = declare_C_weights(nc)
        WD = declare_D_weights(nc)
        ctx.push_scope()
        HA = emit_phase_A_consts(ctx, WA, consts)
        for s in range(SEQ):
            ctx.push_scope()
            emit_phase_A_seq(ctx, WA, HA, x_fm, stream, s)
            ctx.flush(f"A{s}")
            ctx.pop_scope()
        ctx.pop_scope()
        ctx.push_scope()
        emit_phase_B(ctx, WB, stream, stream, consts)
        ctx.flush("B")
        ctx.pop_scope()
        ctx.push_scope()
        HC = emit_phase_C_consts(ctx, WC, consts)
        for s in range(SEQ):
            ctx.push_scope()
            emit_phase_C_seq(ctx, WC, HC, stream, stream, s)
            ctx.flush(f"C{s}")
            ctx.pop_scope()
        ctx.pop_scope()
        ctx.push_scope()
        emit_phase_D(ctx, WD, stream, actA, actB, y_fm, consts)
        ctx.pop_scope()
    return nc


def kernel_fused(inputs):
    x = inputs["L0_attn_x"]
    x_fm = [np.ascontiguousarray(
        x[4 * c:4 * c + 4].transpose(0, 2, 1).reshape(SEQ, NCH, 128, T)
        .astype(np.float32)) for c in range(NCORES)]
    nc = build_fused()
    Wall = {}
    Wall.update(dmap_prefix(prep_A(inputs), "A"))
    Wall.update(dmap_prefix(prep_B(inputs), "B"))
    Wall.update(dmap_prefix(prep_C(inputs), "C"))
    Wall.update(dmap_prefix(prep_D(inputs), "D"))
    cn = consts_np()
    cn.update(consts_C_np())
    in_maps = []
    for c in range(NCORES):
        m = dict(Wall)
        m.update(cn)
        m["x_fm"] = x_fm[c]
        in_maps.append(m)
    res = run_bass_kernel_spmd(nc, in_maps, core_ids=list(range(NCORES)))
    full = np.concatenate([np.asarray(r["y_fm"]) for r in res.results], axis=0)
    return np.ascontiguousarray(
        full.reshape(32, 1024, 2048).transpose(0, 2, 1)).astype(np.float32)


def kernel(**inputs):
    inputs = {k: np.asarray(v) for k, v in inputs.items()}
    return kernel_fused(inputs)


# version 05 (final)
# speedup vs baseline: 1.1436x; 1.0601x over previous
import contextlib
import numpy as np
import ml_dtypes
import concourse.bass as bass
import concourse.mybir as mybir
from concourse.bass_utils import run_bass_kernel_spmd

F32 = mybir.dt.float32
BF16 = mybir.dt.bfloat16
AF = mybir.ActivationFunctionType
ALU = mybir.AluOpType
AX = mybir.AxisListType
BF16NP = ml_dtypes.bfloat16

NCORES = 8
SEQ = 4
T = 2048
DM = 1024
NCH = DM // 128
NTOK = SEQ * T
DH = 4096
HCH = DH // 128

LN_EPS_OUTER = 1e-5
LN_EPS_A = 1e-5
LN_EPS_B = 1e-6
CTX_EPS = 1e-6
RSQRT2 = 2.0 ** (-0.5)
KAPPA = RSQRT2 / (1.0 + RSQRT2)

ENGINES = ("tensor", "vector", "scalar", "gpsimd", "sync")
AUTOSEM = {"tensor": ("T", 1), "vector": ("V", 1), "scalar": ("S", 1),
           "gpsimd": ("G", 16), "sync": ("Y", 16)}
NOINC_OPS = {"wait_ge", "nop", "engine_nop"}


class EQ:
    def __init__(self, ctx, name):
        object.__setattr__(self, "_ctx", ctx)
        object.__setattr__(self, "_name", name)

    def __getattr__(self, opname):
        ctx = self._ctx
        name = self._name

        def call(*args, wait=None, tinc=True, **kwargs):
            if opname == "wait_ge":
                s, v = args

                def run_wait(eng, s=s, v=v):
                    eng.wait_ge(ctx.sems[s], v)
                ctx.q[name].append(run_wait)
                return None
            sem, step = AUTOSEM[name]
            do_inc = opname not in NOINC_OPS and (name != "tensor" or tinc)
            if do_inc:
                ctx.cnt[sem] = ctx.cnt.get(sem, 0) + step
            mark = (sem, ctx.cnt.get(sem, 0)) if do_inc else None

            def run(eng, args=args, kwargs=kwargs, wait=wait, mark=mark):
                if wait:
                    seen = {}
                    for s, v in wait:
                        if v > 0 and seen.get(s, -1) < v:
                            seen[s] = v
                    for s, v in seen.items():
                        eng.wait_ge(ctx.sems[s], v)
                inst = getattr(eng, opname)(*args, **kwargs)
                if mark is not None:
                    inst.then_inc(ctx.sems[mark[0]], AUTOSEM[name][1])
                return inst

            ctx.q[name].append(run)
            return mark

        return call


class Ctx:
    def __init__(self, nc, es):
        self.nc = nc
        self.es = es
        self.scopes = [es]
        self.q = {e: [] for e in ENGINES}
        self.sems = {}
        self.cnt = {}
        for s in ("T", "V", "S", "G", "Y"):
            self.sems[s] = es.enter_context(nc.semaphore(s))
            self.cnt[s] = 0
        self.tensor = EQ(self, "tensor")
        self.vector = EQ(self, "vector")
        self.scalar = EQ(self, "scalar")
        self.gpsimd = EQ(self, "gpsimd")
        self.sync = EQ(self, "sync")

    def c(self, sem):
        return self.cnt.get(sem, 0)

    def snap(self, *sems):
        return [(s, self.cnt.get(s, 0)) for s in sems]

    def flush(self, name=None):
        with self.nc.Block(name) as block:
            for e in ENGINES:
                ops = self.q[e]
                if ops:
                    getattr(block, e)(lambda eng, ops=tuple(ops): [f(eng) for f in ops])
        for e in ENGINES:
            self.q[e] = []

    def push_scope(self):
        self.scopes.append(contextlib.ExitStack())

    def pop_scope(self):
        self.scopes.pop().close()

    def _uname(self, name):
        self._uid = getattr(self, "_uid", 0) + 1
        return f"{name}_{self._uid}"

    def sbuf(self, name, shape, dtype):
        return self.scopes[-1].enter_context(
            self.nc.sbuf_tensor(self._uname(name), list(shape), dtype))

    def psum(self, name, shape, dtype=F32):
        return self.scopes[-1].enter_context(
            self.nc.psum_tensor(self._uname(name), list(shape), dtype))

    def init_epscols(self):
        self.eps_sb = self.sbuf("epsconsts", (128, 4), F32)
        self.eps_cols = {}
        for i, v in enumerate((1e-5, 1e-6)):
            self.vector.memset(self.eps_sb[:, i:i + 1], v)
            self.eps_cols[v] = i

    def epscol(self, v):
        return self.eps_sb[:, self.eps_cols[v]:self.eps_cols[v] + 1]


def tiles_lhsT(wT):
    I_, O_ = wT.shape
    KC, Q = I_ // 128, O_ // 128
    t = wT.reshape(KC, 128, Q, 128).transpose(2, 1, 0, 3).reshape(Q, 128, KC * 128)
    return np.ascontiguousarray(t)


def vec2d(v):
    return np.ascontiguousarray(np.asarray(v).reshape(-1, 128).T)


def emit_ln_stats(ctx, NTt, eps, val_chunks, sq_chunks, ones_sb, ps_s1, ps_s2,
                  stat_sb, dep):
    for c in range(NCH):
        ctx.tensor.matmul(ps_s1[:, 0:NTt], ones_sb[:, :], val_chunks(c),
                          start=(c == 0), stop=(c == NCH - 1),
                          wait=(dep if c == 0 else None), tinc=False)
        ctx.tensor.matmul(ps_s2[:, 0:NTt], ones_sb[:, :], sq_chunks(c),
                          start=(c == 0), stop=(c == NCH - 1),
                          tinc=(c == NCH - 1))
    stats_mm = ctx.snap("T")
    mu = stat_sb[:, 0 * NTt:1 * NTt]
    musq = stat_sb[:, 1 * NTt:2 * NTt]
    var = stat_sb[:, 2 * NTt:3 * NTt]
    rstd = stat_sb[:, 3 * NTt:4 * NTt]
    m_mu = ctx.scalar.activation(out=mu, in_=ps_s1[:, 0:NTt], func=AF.Copy,
                                 scale=1.0 / DM, wait=stats_mm + dep)
    m_musq = ctx.scalar.activation(out=musq, in_=mu, func=AF.Square,
                                   wait=[m_mu])
    m_var = ctx.vector.scalar_tensor_tensor(var, ps_s2[:, 0:NTt], 1.0 / DM,
                                            musq, ALU.mult, ALU.subtract,
                                            wait=[m_musq] + stats_mm)
    m_ln = ctx.scalar.activation(out=var, in_=var, func=AF.Ln,
                                 bias=ctx.epscol(eps), wait=[m_var])
    ctx.scalar.activation(out=rstd, in_=var, func=AF.Exp, scale=-0.5,
                          wait=[m_ln])
    return mu, rstd


def emit_mm_stage(ctx, tag, w_ap, scale_sb, scale_col0, src_slab, dst_slab,
                  KC, Q, act, extra=None):
    NT = 512
    NGRP = 1024
    ngroups = NTOK // NGRP
    ntiles = NGRP // NT

    w_sb = [ctx.sbuf(f"{tag}w{i}", (128, KC * 128), BF16) for i in range(3)]
    src_sb = [ctx.sbuf(f"{tag}src{i}", (128, KC * NGRP), BF16) for i in range(2)]
    t1_sb = [ctx.sbuf(f"{tag}t1_{i}", (128, NGRP), BF16) for i in range(2)] \
        if extra is not None else None
    dst_sb = [ctx.sbuf(f"{tag}dst{i}", (128, NGRP), BF16) for i in range(2)]
    ps = [ctx.psum(f"{tag}ps{i}", (128, NT), F32) for i in range(2)]

    mm_after_user = []
    ev_after_user = []
    st_after_user = []
    ev_after_tile = []
    mm_after_group = []
    sl_group = {}

    def issue_src_loads(gg):
        sslot = gg % 2
        n0g = gg * NGRP
        w = mm_after_group[gg - 2] if gg >= 2 else None
        for k in range(KC):
            ctx.sync.dma_start(out=src_sb[sslot][:, k * NGRP:(k + 1) * NGRP],
                               in_=src_slab(k, n0g, NGRP),
                               wait=(w if k == 0 else None))
        sl_group[gg] = ctx.snap("Y")

    for g in range(ngroups):
        n0 = g * NGRP
        if g == 0:
            issue_src_loads(0)
            if ngroups > 1:
                issue_src_loads(1)
        elif g + 1 < ngroups:
            issue_src_loads(g + 1)
        sl_ready = sl_group[g]
        sslot = g % 2

        for m in range(Q):
            u = g * Q + m
            wslot = u % 3
            w = mm_after_user[u - 3] if u >= 3 else None
            ctx.gpsimd.dma_start(out=w_sb[wslot][:, :], in_=w_ap[m], wait=w)
            wl_ready = ctx.snap("G")

            t1_mk = None
            if extra is not None:
                w = ev_after_user[u - 2] if u >= 2 else None
                t1_mk = ctx.sync.dma_start(out=t1_sb[m % 2][:, :],
                                           in_=extra["t1"](m, n0, NGRP), wait=w)

            for ni in range(ntiles):
                pslot = (u * ntiles + ni) % 2
                tile_idx = u * ntiles + ni
                for k in range(KC):
                    w = None
                    if k == 0:
                        w = wl_ready + sl_ready + ([t1_mk] if t1_mk else [])
                        if tile_idx >= 2:
                            w = w + [ev_after_tile[tile_idx - 2]]
                    ctx.tensor.matmul(
                        ps[pslot][:, :],
                        w_sb[wslot][:, k * 128:(k + 1) * 128],
                        src_sb[sslot][:, k * NGRP + ni * NT: k * NGRP + (ni + 1) * NT],
                        start=(k == 0), stop=(k == KC - 1),
                        wait=w, tinc=(k == KC - 1))
                mm_now = ctx.snap("T")
                dcols = dst_sb[m % 2][:, ni * NT:(ni + 1) * NT]
                wlist = list(mm_now)
                if ni == 0 and u >= 2:
                    wlist.append(st_after_user[u - 2])
                if extra is None:
                    mk = ctx.scalar.activation(
                        out=dcols, in_=ps[pslot][:, :], func=act,
                        scale=scale_sb[:, scale_col0 + m:scale_col0 + m + 1],
                        wait=wlist)
                else:
                    mk = ctx.vector.scalar_tensor_tensor(
                        dcols, ps[pslot][:, :],
                        scale_sb[:, scale_col0 + m:scale_col0 + m + 1],
                        t1_sb[m % 2][:, ni * NT:(ni + 1) * NT],
                        ALU.mult, ALU.mult, wait=wlist)
                ev_after_tile.append(mk)
            mm_after_user.append(ctx.snap("T"))
            ev_after_user.append([ev_after_tile[-1]])
            mk = ctx.sync.dma_start(out=dst_slab(m, n0, NGRP),
                                    in_=dst_sb[m % 2][:, :],
                                    wait=[ev_after_tile[-1]])
            st_after_user.append(mk)
        mm_after_group.append(ctx.snap("T"))
    ctx.sync.wait_ge("Y", ctx.c("Y"))


def emit_phase_D(ctx, W, stream_in, actA, actB, y_fm, consts):
    sc = ctx.sbuf("Dscales", (128, 160), F32)
    ctx.sync.dma_start(out=sc[:, 0:32], in_=W["w1_s2d"])
    ctx.sync.dma_start(out=sc[:, 32:64], in_=W["w2_s2d"])
    ctx.sync.dma_start(out=sc[:, 64:96], in_=W["h1_s2d"])
    ctx.sync.dma_start(out=sc[:, 96:128], in_=W["h2_s2d"])
    ctx.sync.dma_start(out=sc[:, 128:136], in_=W["w3_s2d"])
    ctx.sync.wait_ge("Y", ctx.c("Y"))
    ctx.flush("D_scales")

    def stream_slab(k, n0, nn):
        s, t0 = n0 // T, n0 % T
        return stream_in[s, k, :, t0:t0 + nn]

    def mk_slab(dram):
        return lambda m, n0, nn: dram[m, :, n0:n0 + nn]

    stages = [
        ("D_w1", lambda: emit_mm_stage(ctx, "Dw1", W["w1_T"], sc, 0, stream_slab,
                                       mk_slab(actA), KC=NCH, Q=HCH, act=AF.Silu)),
        ("D_w2", lambda: emit_mm_stage(ctx, "Dw2", W["w2_T"], sc, 32, stream_slab,
                                       mk_slab(actB), KC=NCH, Q=HCH, act=None,
                                       extra={"t1": mk_slab(actA)})),
        ("D_h1", lambda: emit_mm_stage(ctx, "Dh1", W["h1_T"], sc, 64, mk_slab(actB),
                                       mk_slab(actA), KC=HCH, Q=HCH, act=AF.Gelu)),
        ("D_h2", lambda: emit_mm_stage(ctx, "Dh2", W["h2_T"], sc, 96, mk_slab(actA),
                                       mk_slab(actB), KC=HCH, Q=HCH, act=AF.Gelu)),
        ("D_tail", lambda: emit_phase_D_tail(ctx, W, stream_in, actB, y_fm, sc, consts)),
    ]
    for name, fn in stages:
        ctx.push_scope()
        fn()
        ctx.flush(name)
        ctx.pop_scope()


def emit_phase_D_tail(ctx, W, stream_in, actB, y_fm, sc, consts):
    NT = 256
    KC, Q = HCH, NCH
    w_ap = W["w3_T"]

    w_sb = ctx.sbuf("Dt_w", (128, Q * KC * 128), BF16)
    for m in range(Q):
        ctx.gpsimd.dma_start(out=w_sb[:, m * KC * 128:(m + 1) * KC * 128],
                             in_=w_ap[m])
    wl_ready = ctx.snap("G")

    src_sb = [ctx.sbuf(f"Dt_src{i}", (128, KC * NT), BF16) for i in range(2)]
    x_sb = [ctx.sbuf(f"Dt_x{i}", (128, NCH * NT), BF16) for i in range(2)]
    u_sb = ctx.sbuf("Dt_u", (128, NCH * NT), F32)
    ubf_sb = ctx.sbuf("Dt_ubf", (128, NCH * NT), BF16)
    usq_sb = ctx.sbuf("Dt_usq", (128, NCH * NT), BF16)
    stat_sb = ctx.sbuf("Dt_stat", (128, 4 * NT), F32)
    out_sb = [ctx.sbuf(f"Dt_out{i}", (128, NCH * NT), F32) for i in range(2)]
    ones_sb = ctx.sbuf("Dt_ones", (128, 128), BF16)
    ctx.sync.dma_start(out=ones_sb[:, :], in_=consts["ones128"])

    ps = [ctx.psum(f"Dt_ps{i}", (128, NT), F32) for i in range(2)]
    ps_s1 = ctx.psum("Dt_s1", (128, NT), F32)
    ps_s2 = ctx.psum("Dt_s2", (128, NT), F32)

    ntiles = NTOK // NT
    mm_after_tile = []
    norm_done_tile = []
    st_after_tile = []
    stats_mm_prev = None

    for ni in range(ntiles):
        n0 = ni * NT
        s, t0 = n0 // T, n0 % T
        slot = ni % 2
        w = None
        if ni >= 2:
            w = mm_after_tile[ni - 2] + norm_done_tile[ni - 2]
        for k in range(KC):
            ctx.sync.dma_start(out=src_sb[slot][:, k * NT:(k + 1) * NT],
                               in_=actB[k, :, n0:n0 + NT],
                               wait=(w if k == 0 else None))
        for c in range(NCH):
            ctx.sync.dma_start(out=x_sb[slot][:, c * NT:(c + 1) * NT],
                               in_=stream_in[s, c, :, t0:t0 + NT])
        sl_ready = ctx.snap("Y")

        tile_ev_u = []
        for m in range(Q):
            pslot = m % 2
            for k in range(KC):
                w = None
                if k == 0:
                    w = sl_ready + wl_ready
                    if ni >= 1 and m == 0:
                        w = w + norm_done_tile[ni - 1]
                    if m >= 2:
                        w = w + [tile_ev_u[m - 2]]
                ctx.tensor.matmul(ps[pslot][:, :],
                                  w_sb[:, (m * KC + k) * 128:(m * KC + k + 1) * 128],
                                  src_sb[slot][:, k * NT:(k + 1) * NT],
                                  start=(k == 0), stop=(k == KC - 1),
                                  wait=w, tinc=(k == KC - 1))
            mm_now = ctx.snap("T")
            mk = ctx.vector.scalar_tensor_tensor(
                u_sb[:, m * NT:(m + 1) * NT], ps[pslot][:, :],
                sc[:, 128 + m:128 + m + 1], x_sb[slot][:, m * NT:(m + 1) * NT],
                ALU.mult, ALU.add, wait=mm_now)
            tile_ev_u.append(mk)
        mm_after_tile.append(ctx.snap("T"))

        dep_u = [tile_ev_u[-1]]
        cm = []
        for c in range(NCH):
            w = list(dep_u)
            if stats_mm_prev and c == 0:
                w += stats_mm_prev
            cm.append(ctx.vector.tensor_copy(
                ubf_sb[:, c * NT:(c + 1) * NT], u_sb[:, c * NT:(c + 1) * NT],
                wait=(w if c == 0 else dep_u)))
            cm.append(ctx.scalar.activation(
                out=usq_sb[:, c * NT:(c + 1) * NT],
                in_=u_sb[:, c * NT:(c + 1) * NT], func=AF.Square,
                wait=(w if c == 0 else dep_u)))
        dep = [cm[-2], cm[-1]]
        mu, rstd = emit_ln_stats(ctx, NT, LN_EPS_OUTER,
                                 lambda c: ubf_sb[:, c * NT:(c + 1) * NT],
                                 lambda c: usq_sb[:, c * NT:(c + 1) * NT],
                                 ones_sb, ps_s1, ps_s2, stat_sb, dep)
        stats_mm_prev = ctx.snap("T")
        rstd_done = ctx.snap("S")
        oslot = ni % 2
        last = None
        for c in range(NCH):
            ucols = u_sb[:, c * NT:(c + 1) * NT]
            w = list(rstd_done) if c == 0 else []
            if ni >= 2 and c == 0:
                w.append(st_after_tile[ni - 2])
            mk = ctx.vector.tensor_sub(ucols, ucols, mu, wait=(w or None))
            last = ctx.vector.tensor_mul(out_sb[oslot][:, c * NT:(c + 1) * NT],
                                         ucols, rstd, wait=[mk])
        norm_done_tile.append([last])
        stmk = None
        for c in range(NCH):
            stmk = ctx.sync.dma_start(out=y_fm[s, c, :, t0:t0 + NT],
                                      in_=out_sb[oslot][:, c * NT:(c + 1) * NT],
                                      wait=([last] if c == 0 else None))
        st_after_tile.append(stmk)
    ctx.sync.wait_ge("Y", ctx.c("Y"))


B_BIG = 30000.0


def emit_phase_B(ctx, W, stream_in, stream_out, consts, dbg=None):
    TM = 1024
    NS = TM // 512
    npass = NTOK // TM
    NE = 8

    gate_w = ctx.sbuf("Bgatew", (128, 64), BF16)
    identf = ctx.sbuf("Bidf", (128, 128), F32)
    ones_sb = ctx.sbuf("Bones", (128, 128), BF16)
    sel8 = ctx.sbuf("Bsel8", (8, 1024), BF16)
    gb2d = ctx.sbuf("Bgb", (128, 448), F32)
    gamma2d = gb2d[:, 0:64]
    beta2d = gb2d[:, 64:128]
    b1_2d = gb2d[:, 128:384]
    b2_2d = gb2d[:, 384:448]
    ctx.gpsimd.dma_start(out=gate_w[:, :], in_=W["gate_w"])
    ctx.sync.dma_start(out=identf[:, :], in_=consts["ident128f"])
    ctx.sync.dma_start(out=ones_sb[:, :], in_=consts["ones128"])
    ctx.sync.dma_start(out=sel8[:, :], in_=consts["sel8"])
    ctx.sync.dma_start(out=gb2d[:, 0:64], in_=W["gamma2d"])
    ctx.sync.dma_start(out=gb2d[:, 64:128], in_=W["beta2d"])
    ctx.sync.dma_start(out=gb2d[:, 128:384], in_=W["b1_2d"])
    ctx.sync.dma_start(out=gb2d[:, 384:448], in_=W["b2_2d"])
    consts_ready = ctx.snap("Y", "G")

    x_sb = ctx.sbuf("Bx", (128, NCH * TM), BF16)
    sq_sb = ctx.sbuf("Bsq", (128, NCH * 512), BF16)
    xhat_sb = ctx.sbuf("Bxhat", (128, NCH * TM), BF16)
    xh_scr = ctx.sbuf("Bxhscr", (128, 512), F32)
    stat_sb = ctx.sbuf("Bstat", (128, 4 * 512), F32)
    nrm_sb = ctx.sbuf("Bnrm", (128, NCH * TM), BF16)
    h_sb = ctx.sbuf("Bh", (128, HCH * TM), BF16)
    u_sb = h_sb.bitcast(F32)
    y_sb = ctx.sbuf("By", (128, NCH * TM), F32)
    tmp_sb = ctx.sbuf("Btmp", (128, 512), F32)
    cwb_sb = ctx.sbuf("Bcwb", (128, TM), BF16)
    scores_sb = ctx.sbuf("Bscores", (8, TM), F32)
    s_tm = ctx.sbuf("Bstm", (128, 64), F32)
    g_scr = ctx.sbuf("Bgscr", (128, 48), F32)
    cw_tm = ctx.sbuf("Bcwtm", (128, 64), F32)
    cw_fm = ctx.sbuf("Bcwfm", (8, TM), BF16)
    w1slab = [ctx.sbuf(f"Bw1s{i}", (128, 1024), BF16) for i in range(4)]
    w2slab = [ctx.sbuf(f"Bw2s{i}", (128, 4096), BF16) for i in range(2)]

    ps_h = [ctx.psum(f"Bpsh{i}", (128, 512), F32) for i in range(2)]
    ps_y = [ctx.psum(f"Bpsy{i}", (128, 512), F32) for i in range(2)]
    ps_cw = ctx.psum("Bpscw", (128, 1024), F32)
    ps_s1 = ctx.psum("Bps1", (128, 512), F32)
    ps_s2 = ctx.psum("Bps2", (128, 512), F32)

    free_h = [None, None]
    free_y = [None, None]
    free_cw = None
    w1_users = []
    w2_users = []
    u_read_x = None
    st_prev = None
    stats_mm_prev = None
    evac_prev_e = None

    for p in range(npass):
        s, t0 = p // (T // TM), (p % (T // TM)) * TM

        for c in range(NCH):
            w = list(consts_ready)
            if p >= 1 and c == 0:
                w.append(u_read_x)
            ctx.sync.dma_start(out=x_sb[:, c * TM:(c + 1) * TM],
                               in_=stream_in[s, c, :, t0:t0 + TM],
                               wait=(w if c == 0 else None))
        x_ready = ctx.snap("Y")

        for ns in range(NS):
            for k in range(NCH):
                w = None
                if k == 0:
                    w = x_ready + consts_ready
                    if free_y[ns]:
                        w = w + [free_y[ns]]
                ctx.tensor.matmul(ps_y[ns][0:8, :], gate_w[:, k * 8:(k + 1) * 8],
                                  x_sb[:, k * TM + ns * 512: k * TM + (ns + 1) * 512],
                                  start=(k == 0), stop=(k == NCH - 1),
                                  wait=w, tinc=(k == NCH - 1))
            mm_now = ctx.snap("T")
            free_y[ns] = ctx.scalar.activation(
                out=scores_sb[0:8, ns * 512:(ns + 1) * 512],
                in_=ps_y[ns][0:8, :], func=AF.Copy, wait=mm_now)

        for tau in range(8):
            w = None
            if tau == 0:
                w = [free_y[0], free_y[1]]
                if free_h[0]:
                    w.append(free_h[0])
            ctx.tensor.transpose(ps_h[0][:, tau * 8:(tau + 1) * 8],
                                 scores_sb[0:8, tau * 128:(tau + 1) * 128],
                                 identf[0:8, 0:8], wait=w, tinc=(tau == 7))
        mm_now = ctx.snap("T")
        stm_mk = ctx.vector.tensor_copy(s_tm[:, :], ps_h[0][:, 0:64], wait=mm_now)
        free_h[0] = stm_mk

        m1n = g_scr[:, 0:1]
        m2n = g_scr[:, 1:2]
        den = g_scr[:, 2:3]
        den2 = g_scr[:, 4:5]
        rden = g_scr[:, 3:4]
        sh = g_scr[:, 8:16]
        gmask = g_scr[:, 16:24]
        gm = g_scr[:, 24:32]
        gem = g_scr[:, 32:40]
        gex = g_scr[:, 40:48]
        mk = stm_mk
        for tau in range(8):
            S = s_tm[:, tau * 8:(tau + 1) * 8]
            mk1 = ctx.vector.tensor_reduce(m1n, S, AX.X, ALU.max, negate=True,
                                           wait=[mk])
            mk_gex = ctx.scalar.activation(out=gex, in_=S, func=AF.Exp,
                                           bias=m1n, wait=[mk1])
            mk_sh = ctx.scalar.activation(out=sh, in_=S, func=AF.Identity,
                                          bias=m1n, wait=[mk1])
            mk = ctx.vector.tensor_single_scalar(gmask, sh, 0.0, ALU.is_ge,
                                                 wait=[mk_sh])
            mk = ctx.vector.scalar_tensor_tensor(gm, gmask, -B_BIG, sh,
                                                 ALU.mult, ALU.add, wait=[mk])
            mk2 = ctx.vector.tensor_reduce(m2n, gm, AX.X, ALU.max, negate=True,
                                           wait=[mk])
            mk_gd = ctx.scalar.activation(out=gm, in_=sh, func=AF.Identity,
                                          bias=m2n, wait=[mk2, mk_gex])
            mk = ctx.vector.tensor_single_scalar(gmask, gm, 0.0, ALU.is_ge,
                                                 wait=[mk_gd])
            mk = ctx.vector.tensor_mul(gem, gex, gmask, wait=[mk])
            mkd = ctx.vector.tensor_reduce(den, gem, AX.X, ALU.add, wait=[mk])
            mk_d2 = ctx.scalar.activation(out=den2, in_=den, func=AF.Identity,
                                          wait=[mkd])
            mk_rd = ctx.vector.reciprocal(rden, den2, wait=[mk_d2])
            mk = ctx.scalar.activation(out=cw_tm[:, tau * 8:(tau + 1) * 8],
                                       in_=gem, func=AF.Identity, scale=rden,
                                       wait=[mk_rd, mkd])
        cw_done = mk
        if dbg is not None:
            ctx.sync.dma_start(out=dbg["cwtm"][p], in_=cw_tm[:, :], wait=[cw_done])
            ctx.sync.dma_start(out=dbg["stm"][p], in_=s_tm[:, :], wait=[cw_done])

        for tau in range(8):
            w = None
            if tau == 0:
                w = [cw_done]
                if free_cw:
                    w.append(free_cw)
            ctx.tensor.transpose(ps_cw[0:8, tau * 128:(tau + 1) * 128],
                                 cw_tm[:, tau * 8:(tau + 1) * 8],
                                 identf[:, :], wait=w, tinc=(tau == 7))
        mm_now = ctx.snap("T")
        free_cw = ctx.vector.tensor_copy(cw_fm[0:8, :], ps_cw[0:8, :], wait=mm_now)
        if dbg is not None:
            ctx.sync.dma_start(out=dbg["cw"][p], in_=cw_fm[0:8, :], wait=[free_cw])

        xhat_last = None
        for half in range(2):
            sq_last = None
            for c in range(NCH):
                xc = x_sb[:, c * TM + half * 512: c * TM + (half + 1) * 512]
                w = None
                if c == 0:
                    w = list(x_ready) + (stats_mm_prev or [])
                    if half == 0 and p >= 1 and st_prev:
                        w.append(st_prev)
                sq_last = ctx.scalar.activation(
                    out=sq_sb[:, c * 512:(c + 1) * 512], in_=xc, func=AF.Square,
                    wait=w)
            dep = [sq_last] + x_ready + (stats_mm_prev or [])
            mu, rstd = emit_ln_stats(
                ctx, 512, LN_EPS_B,
                lambda c: x_sb[:, c * TM + half * 512: c * TM + (half + 1) * 512],
                lambda c: sq_sb[:, c * 512:(c + 1) * 512],
                ones_sb, ps_s1, ps_s2, stat_sb, dep)
            stats_mm_prev = ctx.snap("T")
            rstd_done = ctx.snap("S")
            for c in range(NCH):
                xc = x_sb[:, c * TM + half * 512: c * TM + (half + 1) * 512]
                w = list(rstd_done) if c == 0 else None
                if c == 0 and p >= 1 and st_prev:
                    w.append(st_prev)
                mk = ctx.vector.tensor_sub(xh_scr[:, :], xc, mu, wait=w)
                xhat_last = ctx.vector.tensor_mul(
                    xhat_sb[:, c * TM + half * 512: c * TM + (half + 1) * 512],
                    xh_scr[:, :], rstd, wait=[mk])
        xhat_done = [xhat_last]
        if dbg is not None:
            for c in range(NCH):
                ctx.sync.dma_start(out=dbg["xhat"][p, c],
                                   in_=xhat_sb[:, c * TM:(c + 1) * TM],
                                   wait=(xhat_done if c == 0 else None))

        mm_w1_done = None
        y_marks = [None] * (NCH * NS)
        for e in range(NE):
            for c in range(NCH):
                w = None
                if c == 0:
                    w = list(xhat_done)
                    if mm_w1_done:
                        w += mm_w1_done
                    if stats_mm_prev:
                        w += stats_mm_prev
                ctx.vector.tensor_scalar(nrm_sb[:, c * TM:(c + 1) * TM],
                                         xhat_sb[:, c * TM:(c + 1) * TM],
                                         gamma2d[:, e * 8 + c:e * 8 + c + 1],
                                         beta2d[:, e * 8 + c:e * 8 + c + 1],
                                         ALU.mult, ALU.add, wait=w)
            nrm_done = ctx.snap("V")

            for ns in range(NS):
                w = None
                if ns == 0:
                    w = [free_cw]
                    if evac_prev_e:
                        w.append(evac_prev_e)
                ctx.tensor.matmul(ps_cw[:, ns * 512:(ns + 1) * 512],
                                  sel8[0:8, e * 128:(e + 1) * 128],
                                  cw_fm[0:8, ns * 512:(ns + 1) * 512],
                                  start=True, stop=True, wait=w, tinc=(ns == NS - 1))
            mm_now = ctx.snap("T")
            free_cw = ctx.scalar.activation(out=cwb_sb[:, :], in_=ps_cw[:, :],
                                            func=AF.Copy, wait=mm_now)

            for m in range(HCH):
                iu = len(w1_users)
                slot = iu % 4
                w = w1_users[iu - 4] if iu >= 4 else None
                ctx.gpsimd.dma_start(out=w1slab[slot][:, :], in_=W["w1T"][e, m],
                                     wait=w)
                wl_ready = ctx.snap("G")
                for ns in range(NS):
                    for k in range(NCH):
                        w = None
                        if k == 0:
                            w = wl_ready + nrm_done
                            if free_h[ns]:
                                w = w + [free_h[ns]]
                        ctx.tensor.matmul(
                            ps_h[ns][:, :], w1slab[slot][:, k * 128:(k + 1) * 128],
                            nrm_sb[:, k * TM + ns * 512: k * TM + (ns + 1) * 512],
                            start=(k == 0), stop=(k == NCH - 1),
                            wait=w, tinc=(k == NCH - 1))
                    mm_now = ctx.snap("T")
                    free_h[ns] = ctx.scalar.activation(
                        out=h_sb[:, m * TM + ns * 512: m * TM + (ns + 1) * 512],
                        in_=ps_h[ns][:, :], func=AF.Relu,
                        bias=b1_2d[:, e * HCH + m:e * HCH + m + 1], wait=mm_now)
                w1_users.append(ctx.snap("T"))
            mm_w1_done = ctx.snap("T")
            h_done = ctx.snap("S")

            for j in range(NCH):
                iu = len(w2_users)
                slot = iu % 2
                w = w2_users[iu - 2] if iu >= 2 else None
                ctx.gpsimd.dma_start(out=w2slab[slot][:, :], in_=W["w2T"][e, j],
                                     wait=w)
                wl_ready = ctx.snap("G")
                for ns in range(NS):
                    for k in range(HCH):
                        w = None
                        if k == 0:
                            w = wl_ready + h_done
                            if free_y[ns]:
                                w = w + [free_y[ns]]
                        ctx.tensor.matmul(
                            ps_y[ns][:, :], w2slab[slot][:, k * 128:(k + 1) * 128],
                            h_sb[:, k * TM + ns * 512: k * TM + (ns + 1) * 512],
                            start=(k == 0), stop=(k == HCH - 1),
                            wait=w, tinc=(k == HCH - 1))
                    mm_now = ctx.snap("T")
                    ycols = y_sb[:, j * TM + ns * 512: j * TM + (ns + 1) * 512]
                    b2col = b2_2d[:, e * NCH + j:e * NCH + j + 1]
                    cwcols = cwb_sb[:, ns * 512:(ns + 1) * 512]
                    if e == 0:
                        mk = ctx.vector.scalar_tensor_tensor(
                            ycols, ps_y[ns][:, :], b2col, cwcols,
                            ALU.add, ALU.mult, wait=mm_now + [free_cw])
                    else:
                        mk = ctx.vector.scalar_tensor_tensor(
                            tmp_sb[:, :], ps_y[ns][:, :], b2col, cwcols,
                            ALU.add, ALU.mult,
                            wait=mm_now + [free_cw, y_marks[j * NS + ns]])
                        mk = ctx.vector.tensor_add(ycols, ycols, tmp_sb[:, :],
                                                   wait=[mk])
                    free_y[ns] = mk
                    y_marks[j * NS + ns] = mk
                w2_users.append(ctx.snap("T"))
            evac_prev_e = free_y[1]
        mm_all = ctx.snap("T")
        if dbg is not None:
            for c in range(NCH):
                ctx.sync.dma_start(out=dbg["y"][p, c],
                                   in_=y_sb[:, c * TM:(c + 1) * TM],
                                   wait=([free_y[0], free_y[1]] if c == 0 else None))

        u_last = None
        for c in range(NCH):
            w = [y_marks[c * NS], y_marks[c * NS + 1]]
            if c == 0:
                w += mm_all
            u_last = ctx.vector.scalar_tensor_tensor(
                u_sb[:, c * TM:(c + 1) * TM], y_sb[:, c * TM:(c + 1) * TM],
                0.5, x_sb[:, c * TM:(c + 1) * TM], ALU.mult, ALU.add, wait=w)
        u_read_x = u_last

        for half in range(2):
            cm = []
            for c in range(NCH):
                ucols = u_sb[:, c * TM + half * 512: c * TM + (half + 1) * 512]
                w = [u_read_x]
                if c == 0:
                    w = w + stats_mm_prev
                cm.append(ctx.vector.tensor_copy(
                    nrm_sb[:, c * 512:(c + 1) * 512], ucols,
                    wait=(w if c == 0 else [u_read_x])))
                cm.append(ctx.scalar.activation(
                    out=sq_sb[:, c * 512:(c + 1) * 512], in_=ucols,
                    func=AF.Square, wait=(w if c == 0 else [u_read_x])))
            dep = [cm[-2], cm[-1]] + stats_mm_prev
            mu, rstd = emit_ln_stats(
                ctx, 512, LN_EPS_OUTER,
                lambda c: nrm_sb[:, c * 512:(c + 1) * 512],
                lambda c: sq_sb[:, c * 512:(c + 1) * 512],
                ones_sb, ps_s1, ps_s2, stat_sb, dep)
            stats_mm_prev = ctx.snap("T")
            rstd_done = ctx.snap("S")
            for c in range(NCH):
                ucols = u_sb[:, c * TM + half * 512: c * TM + (half + 1) * 512]
                ocols = xhat_sb[:, c * TM + half * 512: c * TM + (half + 1) * 512]
                w = list(rstd_done) if c == 0 else None
                mk = ctx.vector.tensor_sub(ucols, ucols, mu, wait=w)
                mk = ctx.vector.tensor_mul(ocols, ucols, rstd, wait=[mk])
                st_prev = ctx.sync.dma_start(
                    out=stream_out[s, c, :, t0 + half * 512: t0 + (half + 1) * 512],
                    in_=ocols, wait=[mk])
    ctx.sync.wait_ge("Y", ctx.c("Y"))


def prep_B(inputs):
    W = {}
    W1 = inputs["L0_ffn_W1"]
    W2 = inputs["L0_ffn_W2"]
    W["w1T"] = np.stack([tiles_lhsT(W1[e]) for e in range(8)])
    W["w2T"] = np.stack([tiles_lhsT(W2[e]) for e in range(8)])
    Wg = inputs["L0_ffn_Wg"]
    W["gate_w"] = np.ascontiguousarray(
        Wg.reshape(8, 128, 8).transpose(1, 0, 2).reshape(128, 64))
    W["gamma2d"] = np.ascontiguousarray(
        inputs["L0_ffn_gamma"].reshape(8, 8, 128).transpose(2, 0, 1).reshape(128, 64))
    W["beta2d"] = np.ascontiguousarray(
        inputs["L0_ffn_beta"].reshape(8, 8, 128).transpose(2, 0, 1).reshape(128, 64))
    W["b1_2d"] = np.ascontiguousarray(
        inputs["L0_ffn_b1"].reshape(8, 32, 128).transpose(2, 0, 1).reshape(128, 256))
    W["b2_2d"] = np.ascontiguousarray(
        inputs["L0_ffn_b2"].reshape(8, 8, 128).transpose(2, 0, 1).reshape(128, 64))
    return W


def declare_B_weights(nc):
    aps = {}
    aps["w1T"] = nc.dram_tensor("Bw1T", [8, 32, 128, 1024], F32, kind="ExternalInput").ap()
    aps["w2T"] = nc.dram_tensor("Bw2T", [8, 8, 128, 4096], F32, kind="ExternalInput").ap()
    aps["gate_w"] = nc.dram_tensor("Bgate_w", [128, 64], F32, kind="ExternalInput").ap()
    for n, cols in (("gamma2d", 64), ("beta2d", 64), ("b1_2d", 256), ("b2_2d", 64)):
        aps[n] = nc.dram_tensor("B" + n, [128, cols], F32, kind="ExternalInput").ap()
    return aps


def build_B_standalone(debug=False):
    nc = bass.Bass("TRN2", target_bir_lowering=False)
    stream_in = nc.dram_tensor("stream_in", [SEQ, NCH, 128, T], BF16,
                               kind="ExternalInput").ap()
    stream_out = nc.dram_tensor("stream_out", [SEQ, NCH, 128, T], BF16,
                                kind="ExternalOutput").ap()
    dbg = None
    if debug:
        dbg = {
            "cwtm": nc.dram_tensor("dbg_cwtm", [8, 128, 64], F32, kind="ExternalOutput").ap(),
            "stm": nc.dram_tensor("dbg_stm", [8, 128, 64], F32, kind="ExternalOutput").ap(),
            "cw": nc.dram_tensor("dbg_cw", [8, 8, 1024], BF16, kind="ExternalOutput").ap(),
            "xhat": nc.dram_tensor("dbg_xhat", [8, NCH, 128, 1024], BF16, kind="ExternalOutput").ap(),
            "y": nc.dram_tensor("dbg_y", [8, NCH, 128, 1024], F32, kind="ExternalOutput").ap(),
        }
    with contextlib.ExitStack() as es:
        ctx = Ctx(nc, es)
        ctx.init_epscols()
        W = declare_B_weights(nc)
        consts = declare_consts(nc)
        ctx.push_scope()
        emit_phase_B(ctx, W, stream_in, stream_out, consts, dbg=dbg)
        ctx.flush("B")
        ctx.pop_scope()
    return nc


def run_B(inputs, stream_bf16_percore, debug=False):
    nc = build_B_standalone(debug)
    W = dmap_prefix(prep_B(inputs), "B")
    in_maps = []
    for c in range(NCORES):
        m = dict(W)
        m.update(consts_np())
        m["stream_in"] = stream_bf16_percore[c]
        in_maps.append(m)
    res = run_bass_kernel_spmd(nc, in_maps, core_ids=list(range(NCORES)))
    if debug:
        return res.results
    return [r["stream_out"] for r in res.results]



C_ES = 64.0 ** (-0.25)
C_SECTIONS = {"p1", "p2", "p3"}
C_WAVES = (0, 1, 2, 3)
C_INJECT = False
NBUK = T // 64
NTT = T // 128


def emit_phase_C_consts(ctx, W, consts):
    H = {}
    H["selq16"] = ctx.sbuf("Cselq", (128, 128), BF16)
    H["selb16"] = ctx.sbuf("Cselb", (16, 1024), BF16)
    H["b2ind"] = ctx.sbuf("Cb2i", (128, 2), BF16)
    H["ones"] = ctx.sbuf("Cones", (128, 128), BF16)
    H["ones32"] = ctx.sbuf("Cones32", (128, 32), F32)
    H["bo2d"] = ctx.sbuf("Cbo", (128, 8), F32)
    ctx.sync.dma_start(out=H["selq16"][:, :], in_=consts["selq16"])
    ctx.sync.dma_start(out=H["selb16"][:, :], in_=consts["selb16"])
    ctx.sync.dma_start(out=H["b2ind"][:, :], in_=consts["b2ind"])
    ctx.sync.dma_start(out=H["ones"][:, :], in_=consts["ones128"])
    ctx.sync.dma_start(out=H["bo2d"][:, :], in_=W["bo2d"])
    ctx.vector.memset(H["ones32"][:, :], 1.0)
    H["ready"] = ctx.snap("Y", "V")
    return H


def emit_phase_C_seq(ctx, W, H, stream_in, stream_out, s, dbg=None):
    attn_sb = ctx.sbuf("Cattn", (128, NCH * T), BF16)
    ksum_sb = ctx.sbuf("Cks", (128, 256), F32)
    kcum_sb = ctx.sbuf("Ckc", (128, 256), F32)
    rkc_sb = ctx.sbuf("Crkc", (128, 256), F32)

    ps_a = ctx.psum("Cpsa", (128, 512), F32)
    ps_b = ctx.psum("Cpsb", (128, 512), F32)

    ctx.push_scope()
    qe_sb = ctx.sbuf("Cqe", (128, NCH * T), BF16)
    fk_sb = ctx.sbuf("Cfk", (128, NTT * 1024), BF16)
    v_sb = ctx.sbuf("Cv", (128, NTT * 1024), BF16)

    ctx.push_scope()
    x_sb = ctx.sbuf("Cx", (128, NCH * T), BF16)
    wq_sb = [ctx.sbuf(f"Cwq{i}", (128, 1024), BF16) for i in range(2)]
    wkv_sb = ctx.sbuf("Cwkv", (128, NCH * 1024), BF16)
    rq_sb = ctx.sbuf("Crq", (16, T), BF16)
    rq32_sb = ctx.sbuf("Crq32", (16, 512), F32)
    ks_scr = ctx.sbuf("Ckss", (128, 6 * 512), F32)

    for c in range(NCH):
        ctx.sync.dma_start(out=x_sb[:, c * T:(c + 1) * T],
                           in_=stream_in[s, c, :, 0:T],
                           wait=(H["ready"] if c == 0 else None))
    x_ready = ctx.snap("Y")

    if "p1" in C_SECTIONS:
        evac_prev = [None, None]
        for m in range(NCH):
            slot = m % 2
            w = evac_prev[slot]
            ctx.gpsimd.dma_start(out=wq_sb[slot][:, :], in_=W["wqT"][m],
                                 wait=([w] if w else None))
            wl = ctx.snap("G")
            for ns in range(4):
                pslot = ns % 2
                pp = [ps_a, ps_b][pslot]
                for k in range(NCH):
                    w = None
                    if k == 0:
                        w = wl + x_ready
                        if evac_prev[pslot]:
                            w = w + [evac_prev[pslot]]
                    ctx.tensor.matmul(pp[:, :], wq_sb[slot][:, k * 128:(k + 1) * 128],
                                      x_sb[:, k * T + ns * 512: k * T + (ns + 1) * 512],
                                      start=(k == 0), stop=(k == NCH - 1),
                                      wait=w, tinc=(k == NCH - 1))
                mm_now = ctx.snap("T")
                evac_prev[pslot] = ctx.scalar.activation(
                    out=qe_sb[:, m * T + ns * 512: m * T + (ns + 1) * 512],
                    in_=pp[:, :], func=AF.Exp, scale=C_ES, wait=mm_now)
        qe_done = ctx.snap("S")

        rq_marks = []
        for ns in range(4):
            pslot = ns % 2
            pp = [ps_a, ps_b][pslot]
            for c in range(NCH):
                w = None
                if c == 0:
                    w = list(qe_done)
                    if ns < 2:
                        w += [evac_prev[0], evac_prev[1]]
                    if ns >= 2:
                        w += [rq_marks[ns - 2]]
                ctx.tensor.matmul(pp[0:16, :], H["selq16"][:, c * 16:(c + 1) * 16],
                                  qe_sb[:, c * T + ns * 512: c * T + (ns + 1) * 512],
                                  start=(c == 0), stop=(c == NCH - 1),
                                  wait=w, tinc=(c == NCH - 1))
            mm_now = ctx.snap("T")
            w = list(mm_now)
            if rq_marks:
                w.append(rq_marks[-1])
            mkr = ctx.vector.reciprocal(rq32_sb[0:16, :], pp[0:16, :], wait=w)
            rq_marks.append(ctx.scalar.activation(
                out=rq_sb[0:16, ns * 512:(ns + 1) * 512], in_=rq32_sb[0:16, :],
                func=AF.Identity, wait=[mkr]))

        rb_sb = ctx.sbuf("Crb", (128, 2 * 512), BF16)
        sq_marks = []
        cp_marks = []
        for c in range(NCH):
            for ns in range(4):
                pslot = ns % 2
                pp = [ps_a, ps_b][pslot]
                idx = c * 4 + ns
                w = [rq_marks[ns]]
                if idx >= 2:
                    w.append(cp_marks[idx - 2])
                else:
                    w.append(rq_marks[ns + 2])
                ctx.tensor.matmul(pp[:, :], H["selb16"][0:16, c * 128:(c + 1) * 128],
                                  rq_sb[0:16, ns * 512:(ns + 1) * 512],
                                  start=True, stop=True, wait=w, tinc=True)
                mm_now = ctx.snap("T")
                rbc = rb_sb[:, pslot * 512:(pslot + 1) * 512]
                w2 = list(mm_now)
                if idx >= 2:
                    w2.append(sq_marks[idx - 2])
                cp = ctx.scalar.activation(out=rbc, in_=pp[:, :], func=AF.Identity,
                                           wait=w2)
                cp_marks.append(cp)
                cols = qe_sb[:, c * T + ns * 512: c * T + (ns + 1) * 512]
                sq_marks.append(ctx.vector.tensor_mul(cols, cols, rbc, wait=[cp]))
        sq_done = [sq_marks[-1], sq_marks[-2]]

        if dbg is not None and s == 0:
            ctx.sync.dma_start(out=dbg["rq"], in_=rq_sb[0:16, :], wait=sq_done)
            for c in range(NCH):
                ctx.sync.dma_start(out=dbg["sq"][c], in_=qe_sb[:, c * T:(c + 1) * T],
                                   wait=(sq_done if c == 0 else None))

    else:
        sq_done = list(x_ready)
    if "p2" in C_SECTIONS:
        for which, wsrc, dst in (("k", W["wkT_rhs"], fk_sb), ("v", W["wvT_rhs"], v_sb)):
            wv_gate = ctx.snap("T") if which == "v" else None
            for c in range(NCH):
                ctx.gpsimd.dma_start(out=wkv_sb[:, c * 1024:(c + 1) * 1024],
                                     in_=wsrc[c],
                                     wait=(wv_gate if c == 0 else None))
            wl = ctx.snap("G")
            ev_pr = [None, None]
            for m in range(NTT):
                for osl in range(2):
                    pslot = (m * 2 + osl) % 2
                    pp = [ps_a, ps_b][pslot]
                    for k in range(NCH):
                        w = None
                        if k == 0:
                            w = wl + x_ready
                            if which == "k" and m == 0 and osl == 0:
                                w = w + sq_done
                            if ev_pr[pslot]:
                                w = w + [ev_pr[pslot]]
                        ctx.tensor.matmul(
                            pp[:, :],
                            x_sb[:, k * T + m * 128: k * T + (m + 1) * 128],
                            wkv_sb[:, k * 1024 + osl * 512: k * 1024 + (osl + 1) * 512],
                            start=(k == 0), stop=(k == NCH - 1),
                            wait=w, tinc=(k == NCH - 1))
                    mm_now = ctx.snap("T")
                    dcols = dst[:, m * 1024 + osl * 512: m * 1024 + (osl + 1) * 512]
                    if which == "v":
                        ev_pr[pslot] = ctx.scalar.activation(
                            out=dcols, in_=pp[:, :], func=AF.Identity, wait=mm_now)
                    else:
                        off = pslot * 1536
                        ks = ks_scr[:, off:off + 512]
                        t2 = ks_scr[:, off + 512:off + 1024]
                        t3 = ks_scr[:, off + 1024:off + 1536]
                        a1 = ctx.scalar.activation(out=ks, in_=pp[:, :],
                                                   func=AF.Identity, scale=C_ES,
                                                   wait=mm_now + ([ev_pr[pslot]] if ev_pr[pslot] else []))
                        a2 = ctx.vector.tensor_single_scalar(t2, ks, 0.0, ALU.min,
                                                             wait=[a1])
                        a3 = ctx.scalar.activation(out=t3, in_=t2, func=AF.Exp,
                                                   wait=[a2])
                        a4 = ctx.vector.tensor_single_scalar(t2, ks, 0.0, ALU.max,
                                                             wait=[a1, a3])
                        ev_pr[pslot] = ctx.vector.tensor_add(dcols, t3, t2,
                                                             wait=[a4, a3])
            if which == "k":
                fk_done = [ev_pr[0], ev_pr[1]]
            else:
                v_done = [ev_pr[0], ev_pr[1]]

    else:
        fk_done = list(x_ready)
        v_done = list(x_ready)
    if dbg is not None and s == 0:
        for m in range(NTT):
            ctx.sync.dma_start(out=dbg["fk"][m], in_=fk_sb[:, m * 1024:(m + 1) * 1024],
                               wait=(fk_done + v_done if m == 0 else None))
            ctx.sync.dma_start(out=dbg["v"][m], in_=v_sb[:, m * 1024:(m + 1) * 1024])
    ctx.pop_scope()

    if "p3" in C_SECTIONS:
        for c in range(NCH):
            for m in range(NTT):
                w = None
                if c == 0 and m == 0:
                    w = fk_done + v_done
                ctx.tensor.matmul(ps_a[:, c * 32 + 2 * m: c * 32 + 2 * m + 2],
                                  fk_sb[:, m * 1024 + c * 128: m * 1024 + (c + 1) * 128],
                                  H["b2ind"][:, :], start=True, stop=True,
                                  wait=w, tinc=(c == NCH - 1 and m == NTT - 1))
        mm_now = ctx.snap("T")
        mkk = ctx.scalar.activation(out=ksum_sb[:, :], in_=ps_a[:, 0:256],
                                    func=AF.Identity, wait=mm_now)
        scan_mk = mkk
        for c in range(NCH):
            scan_mk = ctx.vector.tensor_tensor_scan(
                kcum_sb[:, c * 32:(c + 1) * 32], ksum_sb[:, c * 32:(c + 1) * 32],
                H["ones32"][:, 0:32], 0.0, ALU.add, ALU.mult,
                wait=([mkk] if c == 0 else None))
        mke = ctx.vector.tensor_single_scalar(kcum_sb[:, :], kcum_sb[:, :], CTX_EPS,
                                              ALU.add, wait=[scan_mk])
        mkr = ctx.vector.reciprocal(rkc_sb[:, :], kcum_sb[:, :], wait=[mke])
        rkc_done = [mkr]
        if dbg is not None and s == 0:
            ctx.sync.dma_start(out=dbg["rkc"], in_=rkc_sb[:, :], wait=[mkr])


    else:
        rkc_done = list(x_ready)
    if dbg is not None and C_INJECT and s == 0:
        mk = None
        w0 = sq_done + fk_done + v_done + rkc_done
        for c in range(NCH):
            mk = ctx.gpsimd.dma_start(out=qe_sb[:, c * T:(c + 1) * T],
                                      in_=dbg["inj_sq"][c],
                                      wait=(w0 if c == 0 else None))
        for m in range(NTT):
            mk = ctx.gpsimd.dma_start(out=fk_sb[:, m * 1024:(m + 1) * 1024],
                                      in_=dbg["inj_fk"][m])
            mk = ctx.gpsimd.dma_start(out=v_sb[:, m * 1024:(m + 1) * 1024],
                                      in_=dbg["inj_v"][m])
        mk2 = ctx.sync.dma_start(out=rkc_sb[:, :], in_=dbg["inj_rkc"])
        sq_done = [mk, mk2]
        fk_done = [mk]
        v_done = [mk]
        rkc_done = [mk2]
    z_marks = []
    for c in range(NCH):
        z_marks.append(ctx.vector.memset(attn_sb[:, c * T:c * T + 64], 0.0))
    zero_done = [z_marks[-1]]

    ctx.push_scope()
    ctx_ps = [ctx.psum(f"Cctx{i}", (128, 64), F32) for i in range(4)]
    ps_c = ctx.psum("Cpsc", (128, 512), F32)
    ps_d = ctx.psum("Cpsd", (128, 512), F32)
    aps = [ps_a, ps_b, ps_c, ps_d]
    ctxt_sb = ctx.sbuf("Cctxt", (128, 4 * 64), BF16)
    free_attn = [None, None, None, None]
    wave_end = []

    for wave in C_WAVES:
        chunks = [2 * wave, 2 * wave + 1]
        heads = [2 * c + par for c in chunks for par in range(2)]
        ev_ctx = {h: None for h in heads}
        mm_ctx = {h: None for h in heads}
        mm_attn_prev = {h: None for h in heads}
        for u in range(NBUK):
            for hi, h in enumerate(heads):
                c = h // 2
                ci = chunks.index(c)
                par = h % 2
                bank = ctx_ps[ci * 2 + par]
                rowsl = slice(par * 64, par * 64 + 64)
                tcols = slice((u // 2) * 1024 + h * 64, (u // 2) * 1024 + h * 64 + 64)
                trows = slice((u % 2) * 64, (u % 2) * 64 + 64)
                w = []
                if ev_ctx[h]:
                    w.append(ev_ctx[h])
                if u == 0:
                    w = w + fk_done + v_done + zero_done
                    if wave_end:
                        w = w + wave_end
                mm_ctx[h] = ctx.tensor.matmul(
                    bank[rowsl, :],
                    fk_sb[trows, tcols], v_sb[trows, tcols],
                    start=(u == 0), stop=(u == NBUK - 1),
                    wait=(w or None), tinc=True,
                    tile_position=((u % 2) * 64, par * 64))
                if u < NBUK - 1:
                    ev_ctx[h] = ctx.scalar.activation(
                        out=ctxt_sb[rowsl, (ci * 2 + par) * 64:
                                    (ci * 2 + par) * 64 + 64],
                        in_=bank[rowsl, :], func=AF.Identity,
                        scale=rkc_sb[rowsl.start:rowsl.start + 64,
                                     c * 32 + u: c * 32 + u + 1],
                        wait=[mm_ctx[h]] + (rkc_done if u == 0 else []))
                    if dbg is not None and s == 0 and wave == 0 and h == 0 and u == 0:
                        dmk = ctx.sync.dma_start(out=dbg["ctxt0"],
                                                 in_=ctxt_sb[0:64, 0:64],
                                                 wait=[ev_ctx[h]])
                        ev_ctx[h] = dmk
                    au = u + 1
                    aslot = ((au - 1) // 8) % 2
                    pa = aps[ci * 2 + aslot]
                    w = [ev_ctx[h]]
                    if (au - 1) % 8 == 0 and par == 0 and free_attn[ci * 2 + aslot]:
                        w.append(free_attn[ci * 2 + aslot])
                    if hi == 0 and u == 0 and not wave_end:
                        w = w + sq_done + rkc_done
                    mm_attn_prev[h] = ctx.tensor.matmul(
                        pa[rowsl, ((au - 1) % 8) * 64: ((au - 1) % 8) * 64 + 64],
                        ctxt_sb[rowsl, (ci * 2 + par) * 64:
                                (ci * 2 + par) * 64 + 64],
                        qe_sb[rowsl, c * T + au * 64: c * T + (au + 1) * 64],
                        start=True, stop=True, wait=w, tinc=True,
                        tile_position=(par * 64, par * 64))
            if u >= 8 and (u % 8 == 0):
                g = u // 8 - 1
                aslot = g % 2
                for ci, c in enumerate(chunks):
                    pa = aps[ci * 2 + aslot]
                    wv = [mm_attn_prev[2 * c], mm_attn_prev[2 * c + 1]]
                    free_attn[ci * 2 + aslot] = ctx.vector.tensor_copy(
                        attn_sb[:, c * T + (8 * g + 1) * 64: c * T + (8 * g + 9) * 64],
                        pa[:, :], wait=wv)
        g = 3
        aslot = g % 2
        wave_end = []
        for ci, c in enumerate(chunks):
            pa = aps[ci * 2 + aslot]
            wv = [mm_attn_prev[2 * c], mm_attn_prev[2 * c + 1]]
            last = ctx.vector.tensor_copy(
                attn_sb[:, c * T + (8 * g + 1) * 64: c * T + (8 * g + 8) * 64],
                pa[:, 0:448], wait=wv)
            free_attn[ci * 2 + aslot] = last
            wave_end.append(last)
        wave_end = wave_end + [mm_ctx[h] for h in heads]
    ctx.pop_scope()
    if dbg is not None and s == 0:
        for c in range(NCH):
            ctx.sync.dma_start(out=dbg["attn"][c], in_=attn_sb[:, c * T:(c + 1) * T],
                               wait=(ctx.snap("V") if c == 0 else None))
    ctx.pop_scope()
    attn_done = ctx.snap("V", "T")

    ctx.push_scope()
    wo_sb = ctx.sbuf("Cwo", (128, NCH * 1024), BF16)
    u_sb = ctx.sbuf("Cu", (128, NCH * 512), F32)
    xt_sb = ctx.sbuf("Cxt", (128, NCH * 512), BF16)
    ubf_sb = ctx.sbuf("Cubf", (128, NCH * 512), BF16)
    usq_sb = ctx.sbuf("Cusq", (128, NCH * 512), BF16)
    stat_sb = ctx.sbuf("Cstat", (128, 4 * 512), F32)
    out_sb = ctx.sbuf("Cout", (128, NCH * 512), BF16)
    for m in range(NCH):
        ctx.gpsimd.dma_start(out=wo_sb[:, m * 1024:(m + 1) * 1024],
                             in_=W["woT"][m],
                             wait=(attn_done if m == 0 else None))
    wl = ctx.snap("G")
    stats_mm_prev = None
    norm_prev = None
    st_prev = None
    for ns in range(4):
        xw = list(attn_done)
        if st_prev:
            xw = xw + [st_prev]
        for c in range(NCH):
            ctx.sync.dma_start(out=xt_sb[:, c * 512:(c + 1) * 512],
                               in_=stream_in[s, c, :, ns * 512:(ns + 1) * 512],
                               wait=(xw if c == 0 else None))
        xt_ready = ctx.snap("Y")
        u_marks = []
        for m in range(NCH):
            pslot = m % 2
            pp = [ps_a, ps_b][pslot]
            for k in range(NCH):
                w = None
                if k == 0:
                    w = wl + xt_ready + attn_done
                    if m >= 2:
                        w = w + [u_marks[m - 2]]
                    elif norm_prev:
                        w = w + norm_prev
                ctx.tensor.matmul(pp[:, :],
                                  wo_sb[:, m * 1024 + k * 128: m * 1024 + (k + 1) * 128],
                                  attn_sb[:, k * T + ns * 512: k * T + (ns + 1) * 512],
                                  start=(k == 0), stop=(k == NCH - 1),
                                  wait=w, tinc=(k == NCH - 1))
            mm_now = ctx.snap("T")
            u_marks.append(ctx.vector.scalar_tensor_tensor(
                u_sb[:, m * 512:(m + 1) * 512], pp[:, :],
                H["bo2d"][:, m:m + 1], xt_sb[:, m * 512:(m + 1) * 512],
                ALU.add, ALU.add, wait=mm_now))
        dep_u = [u_marks[-1]]
        cm = []
        for c in range(NCH):
            ucols = u_sb[:, c * 512:(c + 1) * 512]
            w = list(dep_u)
            if c == 0 and stats_mm_prev:
                w += stats_mm_prev
            cm.append(ctx.vector.tensor_copy(ubf_sb[:, c * 512:(c + 1) * 512],
                                             ucols, wait=(w if c == 0 else dep_u)))
            cm.append(ctx.scalar.activation(out=usq_sb[:, c * 512:(c + 1) * 512],
                                            in_=ucols, func=AF.Square,
                                            wait=(w if c == 0 else dep_u)))
        dep = [cm[-2], cm[-1]]
        mu, rstd = emit_ln_stats(ctx, 512, LN_EPS_OUTER,
                                 lambda c: ubf_sb[:, c * 512:(c + 1) * 512],
                                 lambda c: usq_sb[:, c * 512:(c + 1) * 512],
                                 H["ones"], ps_a, ps_b, stat_sb, dep)
        stats_mm_prev = ctx.snap("T")
        rstd_done = ctx.snap("S")
        last = None
        for c in range(NCH):
            ucols = u_sb[:, c * 512:(c + 1) * 512]
            w = list(rstd_done) if c == 0 else None
            mk = ctx.vector.tensor_sub(ucols, ucols, mu, wait=w)
            last = ctx.vector.tensor_mul(out_sb[:, c * 512:(c + 1) * 512],
                                         ucols, rstd, wait=[mk])
        norm_prev = [last] + stats_mm_prev
        for c in range(NCH):
            st_prev = ctx.sync.dma_start(
                out=stream_out[s, c, :, ns * 512:(ns + 1) * 512],
                in_=out_sb[:, c * 512:(c + 1) * 512],
                wait=([last] if c == 0 else None))
    ctx.pop_scope()
    ctx.sync.wait_ge("Y", ctx.c("Y"))


def prep_C(inputs):
    W = {}
    W["wqT"] = tiles_lhsT(np.ascontiguousarray(inputs["L1_attn_Wq"].T))
    W["woT"] = tiles_lhsT(np.ascontiguousarray(inputs["L1_attn_Wo"].T))
    W["wkT_rhs"] = np.ascontiguousarray(inputs["L1_attn_Wk"].T).reshape(8, 128, 1024)
    W["wvT_rhs"] = np.ascontiguousarray(inputs["L1_attn_Wv"].T).reshape(8, 128, 1024)
    W["bo2d"] = vec2d(inputs["L1_attn_bo"])
    return W


def consts_C_np():
    selq = np.zeros((128, 128), dtype=BF16NP)
    selb = np.zeros((16, 1024), dtype=BF16NP)
    for c in range(8):
        for p in range(128):
            h = 2 * c + (p // 64)
            selq[p, c * 16 + h] = 1
            selb[h, c * 128 + p] = 1
    b2 = np.zeros((128, 2), dtype=BF16NP)
    b2[0:64, 0] = 1
    b2[64:128, 1] = 1
    return {"selq16": selq, "selb16": selb, "b2ind": b2}


def declare_C_consts(nc):
    return {"selq16": nc.dram_tensor("selq16", [128, 128], BF16, kind="ExternalInput").ap(),
            "selb16": nc.dram_tensor("selb16", [16, 1024], BF16, kind="ExternalInput").ap(),
            "b2ind": nc.dram_tensor("b2ind", [128, 2], BF16, kind="ExternalInput").ap()}


def declare_C_weights(nc):
    aps = {}
    aps["wqT"] = nc.dram_tensor("CwqT", [8, 128, 1024], F32, kind="ExternalInput").ap()
    aps["woT"] = nc.dram_tensor("CwoT", [8, 128, 1024], F32, kind="ExternalInput").ap()
    aps["wkT_rhs"] = nc.dram_tensor("CwkT_rhs", [8, 128, 1024], F32, kind="ExternalInput").ap()
    aps["wvT_rhs"] = nc.dram_tensor("CwvT_rhs", [8, 128, 1024], F32, kind="ExternalInput").ap()
    aps["bo2d"] = nc.dram_tensor("Cbo2d", [128, 8], F32, kind="ExternalInput").ap()
    return aps


def build_C_standalone(debug=False):
    nc = bass.Bass("TRN2", target_bir_lowering=False)
    stream_in = nc.dram_tensor("stream_in", [SEQ, NCH, 128, T], BF16,
                               kind="ExternalInput").ap()
    stream_out = nc.dram_tensor("stream_out", [SEQ, NCH, 128, T], BF16,
                                kind="ExternalOutput").ap()
    dbg = None
    if debug:
        dbg = {
            "sq": nc.dram_tensor("dbg_sq", [NCH, 128, T], BF16, kind="ExternalOutput").ap(),
            "fk": nc.dram_tensor("dbg_fk", [NTT, 128, 1024], BF16, kind="ExternalOutput").ap(),
            "v": nc.dram_tensor("dbg_v", [NTT, 128, 1024], BF16, kind="ExternalOutput").ap(),
            "rkc": nc.dram_tensor("dbg_rkc", [128, 256], F32, kind="ExternalOutput").ap(),
            "rq": nc.dram_tensor("dbg_rq", [16, T], BF16, kind="ExternalOutput").ap(),
            "ctxt0": nc.dram_tensor("dbg_ctxt0", [64, 64], BF16, kind="ExternalOutput").ap(),
            "inj_sq": nc.dram_tensor("inj_sq", [NCH, 128, T], BF16, kind="ExternalInput").ap(),
            "inj_fk": nc.dram_tensor("inj_fk", [NTT, 128, 1024], BF16, kind="ExternalInput").ap(),
            "inj_v": nc.dram_tensor("inj_v", [NTT, 128, 1024], BF16, kind="ExternalInput").ap(),
            "inj_rkc": nc.dram_tensor("inj_rkc", [128, 256], F32, kind="ExternalInput").ap(),

            "attn": nc.dram_tensor("dbg_attn", [NCH, 128, T], BF16, kind="ExternalOutput").ap(),
        }
    with contextlib.ExitStack() as es:
        ctx = Ctx(nc, es)
        ctx.init_epscols()
        W = declare_C_weights(nc)
        consts = declare_consts(nc)
        consts.update(declare_C_consts(nc))
        ctx.push_scope()
        H = emit_phase_C_consts(ctx, W, consts)
        for s in range(SEQ):
            ctx.push_scope()
            emit_phase_C_seq(ctx, W, H, stream_in, stream_out, s, dbg=dbg)
            ctx.flush(f"C_s{s}")
            ctx.pop_scope()
        ctx.pop_scope()
    return nc


def run_C(inputs, stream_bf16_percore, debug=False):
    nc = build_C_standalone(debug)
    W = dmap_prefix(prep_C(inputs), "C")
    in_maps = []
    for c in range(NCORES):
        m = dict(W)
        m.update(consts_np())
        m.update(consts_C_np())
        m["stream_in"] = stream_bf16_percore[c]
        if debug:
            m["inj_sq"] = np.zeros((NCH, 128, T), BF16NP)
            m["inj_fk"] = np.zeros((NTT, 128, 1024), BF16NP)
            m["inj_v"] = np.zeros((NTT, 128, 1024), BF16NP)
            m["inj_rkc"] = np.zeros((128, 256), np.float32)
        in_maps.append(m)
    res = run_bass_kernel_spmd(nc, in_maps, core_ids=list(range(NCORES)))
    if debug:
        return res.results
    return [r["stream_out"] for r in res.results]



A_SCALE = 64.0 ** (-0.5)


def emit_phase_A_consts(ctx, W, consts):
    H = {}
    H["ident"] = ctx.sbuf("Aid", (128, 128), BF16)
    H["ones"] = ctx.sbuf("Aones", (128, 128), BF16)
    H["b2d"] = ctx.sbuf("Ab2d", (128, 8 + 8 + 8 + 8), F32)
    H["bqk2d"] = H["b2d"][:, 0:8]
    H["bout2d"] = H["b2d"][:, 8:16]
    H["g2d"] = H["b2d"][:, 16:24]
    H["be2d"] = H["b2d"][:, 24:32]
    H["bvrow"] = ctx.sbuf("Abv", (1, 512), BF16)
    H["onesk1"] = ctx.sbuf("Ak1", (1, 128), BF16)
    H["bvrep"] = ctx.sbuf("Abvr", (128, 512), F32)
    ctx.sync.dma_start(out=H["ident"][:, :], in_=consts["ident128b"])
    ctx.sync.dma_start(out=H["ones"][:, :], in_=consts["ones128"])
    ctx.sync.dma_start(out=H["b2d"][:, 0:8], in_=W["bqk2d"])
    ctx.sync.dma_start(out=H["b2d"][:, 8:16], in_=W["bout2d"])
    ctx.sync.dma_start(out=H["b2d"][:, 16:24], in_=W["g2d"])
    ctx.sync.dma_start(out=H["b2d"][:, 24:32], in_=W["be2d"])
    ctx.gpsimd.dma_start(out=H["bvrow"][0:1, :], in_=W["bv_row"])
    ctx.gpsimd.dma_start(out=H["onesk1"][0:1, :], in_=consts["ones128"][0:1, :])
    ld = ctx.snap("Y", "G")
    H["wqk"] = ctx.sbuf("Awqk", (128, NCH * 1024), BF16)
    H["wv"] = ctx.sbuf("Awv", (128, NCH * 512), BF16)
    H["wout"] = ctx.sbuf("Awout", (128, 4 * 1024), BF16)
    for c in range(NCH):
        ctx.gpsimd.dma_start(out=H["wqk"][:, c * 1024:(c + 1) * 1024],
                             in_=W["wqkT"][c])
        ctx.gpsimd.dma_start(out=H["wv"][:, c * 512:(c + 1) * 512],
                             in_=W["wv_rhs"][c])
    for m in range(NCH):
        ctx.gpsimd.dma_start(out=H["wout"][:, m * 512:(m + 1) * 512],
                             in_=W["woutT"][m])
    H["ready"] = ctx.snap("Y", "G")
    ctx.push_scope()
    ps = ctx.psum("Abvps", (128, 512), F32)
    mk = ctx.tensor.matmul(ps[:, :], H["onesk1"][0:1, :], H["bvrow"][0:1, :],
                           start=True, stop=True, wait=ld, tinc=True)
    H["bv_done"] = [ctx.scalar.activation(out=H["bvrep"][:, :], in_=ps[:, :],
                                          func=AF.Identity, wait=[mk])]
    ctx.pop_scope()
    return H


def emit_phase_A_seq(ctx, W, H, x_fm, stream_out, s):
    qk_sb = ctx.sbuf("Aqk", (128, NCH * T), BF16)
    v_sb = ctx.sbuf("Av", (128, NTT * 512), BF16)
    o_fm = ctx.sbuf("Aofm", (128, 4 * T), BF16)

    ps_s = [ctx.psum(f"Aps{i}", (128, 512), F32) for i in range(4)]
    ps_t = [ctx.psum(f"Apt{i}", (128, 512), BF16) for i in range(2)]
    ps_pv = ctx.psum("Apv", (128, 64), F32)
    ps_ot = ctx.psum("Aot", (128, 128), BF16)

    ctx.push_scope()
    x_sb = ctx.sbuf("Ax", (128, NCH * T), BF16)
    for c in range(NCH):
        ctx.gpsimd.dma_start(out=x_sb[:, c * T:(c + 1) * T],
                             in_=x_fm[s, c, :, 0:T],
                             wait=(H["ready"] if c == 0 else None))
    x_ready = ctx.snap("G")

    ev_pr = [None, None, None, None]
    for m in range(NCH):
        for ns in range(4):
            pslot = ns % 4
            pp = ps_s[pslot]
            for k in range(NCH):
                w = None
                if k == 0:
                    w = list(x_ready) + H["bv_done"]
                    if ev_pr[pslot]:
                        w = w + [ev_pr[pslot]]
                ctx.tensor.matmul(pp[:, :],
                                  H["wqk"][:, m * 1024 + k * 128: m * 1024 + (k + 1) * 128],
                                  x_sb[:, k * T + ns * 512: k * T + (ns + 1) * 512],
                                  start=(k == 0), stop=(k == NCH - 1),
                                  wait=w, tinc=(k == NCH - 1))
            mm_now = ctx.snap("T")
            ev_pr[pslot] = ctx.scalar.activation(
                out=qk_sb[:, m * T + ns * 512: m * T + (ns + 1) * 512],
                in_=pp[:, :], func=AF.Identity,
                bias=H["bqk2d"][:, m:m + 1], wait=mm_now)
    qk_done = ctx.snap("S")

    for mt in range(NTT):
        pslot = mt % 4
        pp = ps_s[pslot]
        for k in range(NCH):
            w = None
            if k == 0:
                w = [ev_pr[pslot]] if ev_pr[pslot] else list(x_ready)
            ctx.tensor.matmul(pp[:, :],
                              x_sb[:, k * T + mt * 128: k * T + (mt + 1) * 128],
                              H["wv"][:, k * 512:(k + 1) * 512],
                              start=(k == 0), stop=(k == NCH - 1),
                              wait=w, tinc=(k == NCH - 1))
        mm_now = ctx.snap("T")
        ev_pr[pslot] = ctx.vector.tensor_add(v_sb[:, mt * 512:(mt + 1) * 512],
                                             pp[:, :], H["bvrep"][:, :],
                                             wait=mm_now)
    v_done = [ev_pr[i] for i in range(4) if ev_pr[i]]
    ctx.pop_scope()

    ctx.push_scope()
    p_sb = [ctx.sbuf(f"Ap{i}", (128, T), BF16) for i in range(2)]
    pt_sb = [ctx.sbuf(f"Apt{i}", (128, T), BF16) for i in range(2)]
    otm_sb = [ctx.sbuf(f"Aotm{i}", (128, 64), BF16) for i in range(2)]
    sm_sb = ctx.sbuf("Asm", (128, 16), F32)
    mcols = sm_sb[:, 0:4]
    mm_ = sm_sb[:, 4:5]
    msc = sm_sb[:, 5:6]
    acc4 = sm_sb[:, 8:12]
    rs = sm_sb[:, 6:7]
    rr = sm_sb[:, 7:8]

    exp_marks = [None, None, None, None]
    tev_marks = [None, None]
    pv_mm_slot = [None, None]
    pv_mark = None
    ot_mark = None
    otm_mark = [None, None]
    pt_done = [None, None]

    units = [(h, it) for h in range(8) for it in range(NTT)]
    for ui, (h, it) in enumerate(units):
        par = h % 2
        qrows = slice(par * 64, par * 64 + 64)
        qcols = slice((h // 2) * T + it * 128, (h // 2) * T + (it + 1) * 128)
        uslot = ui % 2
        for j in range(4):
            w = []
            if exp_marks[j]:
                w.append(exp_marks[j])
            if ui == 0 and j == 0:
                w = w + qk_done + v_done
            ctx.tensor.matmul(
                ps_s[j][:, :], qk_sb[qrows, qcols],
                qk_sb[qrows, (4 + h // 2) * T + j * 512:(4 + h // 2) * T + (j + 1) * 512],
                start=True, stop=True, wait=(w or None), tinc=True,
                tile_position=(par * 64, 0))
        mm_s = ctx.snap("T")
        mk = None
        for j in range(4):
            mk = ctx.vector.tensor_reduce(mcols[:, j:j + 1], ps_s[j][:, :],
                                          AX.X, ALU.max,
                                          wait=(mm_s if j == 0 else [mk]))
        mk = ctx.vector.tensor_reduce(mm_, mcols, AX.X, ALU.max, wait=[mk])
        mk_msc = ctx.vector.tensor_single_scalar(msc, mm_, -A_SCALE, ALU.mult,
                                                 wait=[mk])
        emk = []
        for j in range(4):
            w = [mk_msc]
            if j == 0 and pt_done[uslot]:
                w.append(pt_done[uslot])
            emk.append(ctx.scalar.activation(
                out=p_sb[uslot][:, j * 512:(j + 1) * 512], in_=ps_s[j][:, :],
                func=AF.Exp, scale=A_SCALE, bias=msc,
                accum_out=acc4[:, j:j + 1],
                wait=(w if j == 0 else [mk_msc])))
            exp_marks[j] = emk[-1]
        mk = ctx.vector.tensor_reduce(rs, acc4, AX.X, ALU.add, wait=[emk[-1]])
        rmk = ctx.vector.reciprocal(rr, rs, wait=[mk])
        tmarks = []
        for jt in range(NTT):
            tb = (jt // 4) % 2
            w = [emk[-1]]
            if tev_marks[tb] and jt % 4 == 0:
                w.append(tev_marks[tb])
            tmarks.append(ctx.tensor.transpose(
                ps_t[tb][:, (jt % 4) * 128:(jt % 4 + 1) * 128],
                p_sb[uslot][:, jt * 128:(jt + 1) * 128],
                H["ident"][:, :], wait=w, tinc=True))
            if jt % 4 == 3:
                wv2 = [tmarks[-1]]
                if jt == 3 and pv_mm_slot[uslot]:
                    wv2 = wv2 + pv_mm_slot[uslot]
                tev_marks[tb] = ctx.vector.tensor_copy(
                    pt_sb[uslot][:, (jt - 3) * 128:(jt + 1) * 128],
                    ps_t[tb][:, :], wait=wv2)
        pt_done[uslot] = tev_marks[(NTT - 1) // 4 % 2]
        for jt in range(NTT):
            w = None
            if jt == 0:
                w = [tev_marks[0], tev_marks[1]]
                if pv_mark:
                    w.append(pv_mark)
            ctx.tensor.matmul(ps_pv[:, :],
                              pt_sb[uslot][:, jt * 128:(jt + 1) * 128],
                              v_sb[:, jt * 512 + h * 64: jt * 512 + (h + 1) * 64],
                              start=(jt == 0), stop=(jt == NTT - 1),
                              wait=w, tinc=(jt == NTT - 1))
        mm_pv = ctx.snap("T")
        pv_mm_slot[uslot] = mm_pv
        w = mm_pv + [rmk]
        if otm_mark[uslot]:
            w.append(otm_mark[uslot])
        pv_mark = ctx.vector.tensor_scalar_mul(otm_sb[uslot][:, :], ps_pv[:, :],
                                               rr, wait=w)
        w = [pv_mark]
        if ot_mark:
            w.append(ot_mark)
        tmk = ctx.tensor.transpose(ps_ot[par * 64:par * 64 + 64, :],
                                   otm_sb[uslot][:, :], H["ident"][:, :],
                                   wait=w, tinc=True,
                                   tile_position=(0, par * 64))
        ot_mark = ctx.scalar.activation(
            out=o_fm[par * 64 + 0:par * 64 + 64,
                     (h // 2) * T + it * 128:(h // 2) * T + (it + 1) * 128],
            in_=ps_ot[par * 64:par * 64 + 64, :], func=AF.Identity, wait=[tmk])
        otm_mark[uslot] = ot_mark
    attn_done = ctx.snap("V", "S", "T")
    ctx.pop_scope()

    ctx.push_scope()
    xt_sb = ctx.sbuf("Axt", (128, NCH * 512), BF16)
    y_sb = ctx.sbuf("Ay", (128, NCH * 512), F32)
    zt_sb = ctx.sbuf("Azt", (128, 512), F32)
    ubf_sb = ctx.sbuf("Aubf", (128, NCH * 512), BF16)
    usq_sb = ctx.sbuf("Ausq", (128, NCH * 512), BF16)
    stat_sb = ctx.sbuf("Astat", (128, 4 * 512), F32)
    out_sb = ctx.sbuf("Aout2", (128, NCH * 512), BF16)
    ps_a = ps_s[0]
    ps_b = ps_s[1]

    stats_mm_prev = None
    norm_prev = None
    st_prev = None
    for ns in range(4):
        xw = list(attn_done)
        if st_prev:
            xw.append(st_prev)
        for c in range(NCH):
            ctx.gpsimd.dma_start(out=xt_sb[:, c * 512:(c + 1) * 512],
                                 in_=x_fm[s, c, :, ns * 512:(ns + 1) * 512],
                                 wait=(xw if c == 0 else None))
        xt_ready = ctx.snap("G")
        u_marks = []
        for m in range(NCH):
            pp = ps_s[m % 2]
            for k in range(4):
                w = None
                if k == 0:
                    w = xt_ready + attn_done
                    if m >= 2:
                        w = w + [u_marks[m - 2]]
                    elif norm_prev:
                        w = w + norm_prev
                ctx.tensor.matmul(pp[:, :],
                                  H["wout"][:, m * 512 + k * 128: m * 512 + (k + 1) * 128],
                                  o_fm[:, k * T + ns * 512: k * T + (ns + 1) * 512],
                                  start=(k == 0), stop=(k == 3),
                                  wait=w, tinc=(k == 3))
            mm_now = ctx.snap("T")
            u_marks.append(ctx.scalar.activation(
                out=y_sb[:, m * 512:(m + 1) * 512], in_=pp[:, :],
                func=AF.Identity, bias=H["bout2d"][:, m:m + 1], wait=mm_now))
        dep_u = [u_marks[-1]]

        cm = []
        for c in range(NCH):
            ycols = y_sb[:, c * 512:(c + 1) * 512]
            w = list(dep_u)
            if c == 0 and stats_mm_prev:
                w += stats_mm_prev
            cm.append(ctx.vector.tensor_copy(ubf_sb[:, c * 512:(c + 1) * 512],
                                             ycols, wait=(w if c == 0 else dep_u)))
            cm.append(ctx.scalar.activation(out=usq_sb[:, c * 512:(c + 1) * 512],
                                            in_=ycols, func=AF.Square,
                                            wait=(w if c == 0 else dep_u)))
        dep = [cm[-2], cm[-1]] + (stats_mm_prev or [])
        mu, rstd = emit_ln_stats(ctx, 512, LN_EPS_A,
                                 lambda c: ubf_sb[:, c * 512:(c + 1) * 512],
                                 lambda c: usq_sb[:, c * 512:(c + 1) * 512],
                                 H["ones"], ps_a, ps_b, stat_sb, dep)
        stats_mm_prev = ctx.snap("T")
        rstd_done = ctx.snap("S")
        umk = []
        for c in range(NCH):
            ycols = y_sb[:, c * 512:(c + 1) * 512]
            w = list(rstd_done) if c == 0 else None
            mk = ctx.vector.tensor_sub(zt_sb[:, :], ycols, mu, wait=w)
            mk = ctx.vector.tensor_mul(zt_sb[:, :], zt_sb[:, :], rstd, wait=[mk])
            mk = ctx.vector.tensor_scalar(zt_sb[:, :], zt_sb[:, :],
                                          H["g2d"][:, c:c + 1],
                                          H["be2d"][:, c:c + 1],
                                          ALU.mult, ALU.add, wait=[mk])
            mk = ctx.vector.scalar_tensor_tensor(
                ycols, zt_sb[:, :], KAPPA, xt_sb[:, c * 512:(c + 1) * 512],
                ALU.mult, ALU.add, wait=[mk])
            umk.append(mk)
        dep_u2 = [umk[-1]]

        cm = []
        for c in range(NCH):
            ucols = y_sb[:, c * 512:(c + 1) * 512]
            w = list(dep_u2) + stats_mm_prev
            cm.append(ctx.vector.tensor_copy(ubf_sb[:, c * 512:(c + 1) * 512],
                                             ucols, wait=(w if c == 0 else dep_u2)))
            cm.append(ctx.scalar.activation(out=usq_sb[:, c * 512:(c + 1) * 512],
                                            in_=ucols, func=AF.Square,
                                            wait=(w if c == 0 else dep_u2)))
        dep = [cm[-2], cm[-1]] + stats_mm_prev
        mu, rstd = emit_ln_stats(ctx, 512, LN_EPS_OUTER,
                                 lambda c: ubf_sb[:, c * 512:(c + 1) * 512],
                                 lambda c: usq_sb[:, c * 512:(c + 1) * 512],
                                 H["ones"], ps_a, ps_b, stat_sb, dep)
        stats_mm_prev = ctx.snap("T")
        rstd_done = ctx.snap("S")
        last = None
        for c in range(NCH):
            ucols = y_sb[:, c * 512:(c + 1) * 512]
            w = list(rstd_done) if c == 0 else None
            mk = ctx.vector.tensor_sub(ucols, ucols, mu, wait=w)
            last = ctx.vector.tensor_mul(out_sb[:, c * 512:(c + 1) * 512],
                                         ucols, rstd, wait=[mk])
        norm_prev = [last] + stats_mm_prev
        for c in range(NCH):
            st_prev = ctx.sync.dma_start(
                out=stream_out[s, c, :, ns * 512:(ns + 1) * 512],
                in_=out_sb[:, c * 512:(c + 1) * 512],
                wait=([last] if c == 0 else None))
    ctx.pop_scope()
    ctx.sync.wait_ge("Y", ctx.c("Y"))


def prep_A(inputs):
    W = {}
    Wqk = inputs["L0_attn_W_qk"]
    perm = np.zeros(1024, dtype=np.int64)
    for j in range(8):
        for p in range(128):
            if j < 4:
                h = 2 * j + p // 64
                perm[j * 128 + p] = h * 128 + (p % 64) * 2
            else:
                h = 2 * (j - 4) + p // 64
                perm[j * 128 + p] = h * 128 + (p % 64) * 2 + 1
    Wqk_p = np.ascontiguousarray(Wqk[:, perm])
    W["wqkT"] = tiles_lhsT(Wqk_p)
    W["bqk2d"] = vec2d(inputs["L0_attn_b_qk"][perm])
    W["wv_rhs"] = np.ascontiguousarray(inputs["L0_attn_W_v"].reshape(8, 128, 512))
    W["bv_row"] = np.ascontiguousarray(inputs["L0_attn_b_v"].reshape(1, 512))
    W["woutT"] = tiles_lhsT(inputs["L0_attn_W_out"])
    W["bout2d"] = vec2d(inputs["L0_attn_b_out"])
    W["g2d"] = vec2d(inputs["L0_attn_ln_gamma"])
    W["be2d"] = vec2d(inputs["L0_attn_ln_beta"])
    return W


def declare_A_weights(nc):
    aps = {}
    aps["wqkT"] = nc.dram_tensor("AwqkT", [8, 128, 1024], F32, kind="ExternalInput").ap()
    aps["wv_rhs"] = nc.dram_tensor("Awv_rhs", [8, 128, 512], F32, kind="ExternalInput").ap()
    aps["woutT"] = nc.dram_tensor("AwoutT", [8, 128, 512], F32, kind="ExternalInput").ap()
    aps["bv_row"] = nc.dram_tensor("Abv_row", [1, 512], F32, kind="ExternalInput").ap()
    for n in ("bqk2d", "bout2d", "g2d", "be2d"):
        aps[n] = nc.dram_tensor("A" + n, [128, 8], F32, kind="ExternalInput").ap()
    return aps


def build_A_standalone():
    nc = bass.Bass("TRN2", target_bir_lowering=False)
    x_fm = nc.dram_tensor("x_fm", [SEQ, NCH, 128, T], F32, kind="ExternalInput").ap()
    stream_out = nc.dram_tensor("stream_out", [SEQ, NCH, 128, T], BF16,
                                kind="ExternalOutput").ap()
    with contextlib.ExitStack() as es:
        ctx = Ctx(nc, es)
        ctx.init_epscols()
        W = declare_A_weights(nc)
        consts = declare_consts(nc)
        ctx.push_scope()
        H = emit_phase_A_consts(ctx, W, consts)
        for s in range(SEQ):
            ctx.push_scope()
            emit_phase_A_seq(ctx, W, H, x_fm, stream_out, s)
            ctx.flush(f"A_s{s}")
            ctx.pop_scope()
        ctx.pop_scope()
    return nc


def run_A(inputs, x_fm_percore):
    nc = build_A_standalone()
    W = dmap_prefix(prep_A(inputs), "A")
    in_maps = []
    for c in range(NCORES):
        m = dict(W)
        m.update(consts_np())
        m["x_fm"] = x_fm_percore[c]
        in_maps.append(m)
    res = run_bass_kernel_spmd(nc, in_maps, core_ids=list(range(NCORES)))
    return [r["stream_out"] for r in res.results]


def prep_D(inputs):
    W = {}
    W["w1_T"] = tiles_lhsT(np.ascontiguousarray(inputs["L1_ffn_w1_w"].T))
    W["w2_T"] = tiles_lhsT(np.ascontiguousarray(inputs["L1_ffn_w2_w"].T))
    W["h1_T"] = tiles_lhsT(np.ascontiguousarray(inputs["L1_ffn_h1_w"].T))
    W["h2_T"] = tiles_lhsT(np.ascontiguousarray(inputs["L1_ffn_h2_w"].T))
    W["w3_T"] = tiles_lhsT(np.ascontiguousarray(inputs["L1_ffn_w3_w"].T))
    for n in ("w1", "w2", "h1", "h2", "w3"):
        W[n + "_s2d"] = vec2d(inputs[f"L1_ffn_{n}_s"])
    return W


def consts_np():
    sel8 = np.zeros((8, 1024), dtype=BF16NP)
    for e in range(8):
        sel8[e, e * 128:(e + 1) * 128] = 1
    return {"ones128": np.ones((128, 128), dtype=BF16NP),
            "ident128b": np.eye(128, dtype=BF16NP),
            "ident128f": np.eye(128, dtype=np.float32),
            "sel8": sel8}


def declare_consts(nc):
    return {"ones128": nc.dram_tensor("ones128", [128, 128], BF16, kind="ExternalInput").ap(),
            "ident128b": nc.dram_tensor("ident128b", [128, 128], BF16, kind="ExternalInput").ap(),
            "ident128f": nc.dram_tensor("ident128f", [128, 128], F32, kind="ExternalInput").ap(),
            "sel8": nc.dram_tensor("sel8", [8, 1024], BF16, kind="ExternalInput").ap()}


def declare_D_weights(nc):
    aps = {}
    for name, shape in [
        ("w1_T", [HCH, 128, NCH * 128]), ("w2_T", [HCH, 128, NCH * 128]),
        ("h1_T", [HCH, 128, HCH * 128]), ("h2_T", [HCH, 128, HCH * 128]),
        ("w3_T", [NCH, 128, HCH * 128]),
    ]:
        aps[name] = nc.dram_tensor("D" + name, shape, F32, kind="ExternalInput").ap()
    for name in ("w1_s2d", "w2_s2d", "h1_s2d", "h2_s2d"):
        aps[name] = nc.dram_tensor("D" + name, [128, 32], F32, kind="ExternalInput").ap()
    aps["w3_s2d"] = nc.dram_tensor("Dw3_s2d", [128, 8], F32, kind="ExternalInput").ap()
    return aps


def dmap_prefix(W, prefix):
    return {prefix + k: v for k, v in W.items()}


def build_D_standalone():
    nc = bass.Bass("TRN2", target_bir_lowering=False)
    stream_in = nc.dram_tensor("stream_in", [SEQ, NCH, 128, T], BF16,
                               kind="ExternalInput").ap()
    y_fm = nc.dram_tensor("y_fm", [SEQ, NCH, 128, T], F32,
                          kind="ExternalOutput").ap()
    actA = nc.dram_tensor("actA", [HCH, 128, NTOK], BF16, kind="Internal").ap()
    actB = nc.dram_tensor("actB", [HCH, 128, NTOK], BF16, kind="Internal").ap()
    with contextlib.ExitStack() as es:
        ctx = Ctx(nc, es)
        ctx.init_epscols()
        W = declare_D_weights(nc)
        consts = declare_consts(nc)
        emit_phase_D(ctx, W, stream_in, actA, actB, y_fm, consts)
    return nc


def run_D(inputs, stream_bf16_percore):
    nc = build_D_standalone()
    W = dmap_prefix(prep_D(inputs), "D")
    in_maps = []
    for c in range(NCORES):
        m = dict(W)
        m.update(consts_np())
        m["stream_in"] = stream_bf16_percore[c]
        in_maps.append(m)
    res = run_bass_kernel_spmd(nc, in_maps, core_ids=list(range(NCORES)))
    return [r["y_fm"] for r in res.results]


def build_fused():
    nc = bass.Bass("TRN2", target_bir_lowering=False)
    x_fm = nc.dram_tensor("x_fm", [SEQ, NCH, 128, T], F32, kind="ExternalInput").ap()
    y_fm = nc.dram_tensor("y_fm", [SEQ, NCH, 128, T], F32, kind="ExternalOutput").ap()
    stream = nc.dram_tensor("stream", [SEQ, NCH, 128, T], BF16, kind="Internal").ap()
    actA = nc.dram_tensor("actA", [HCH, 128, NTOK], BF16, kind="Internal").ap()
    actB = nc.dram_tensor("actB", [HCH, 128, NTOK], BF16, kind="Internal").ap()
    with contextlib.ExitStack() as es:
        ctx = Ctx(nc, es)
        ctx.init_epscols()
        consts = declare_consts(nc)
        consts.update(declare_C_consts(nc))
        WA = declare_A_weights(nc)
        WB = declare_B_weights(nc)
        WC = declare_C_weights(nc)
        WD = declare_D_weights(nc)
        ctx.push_scope()
        HA = emit_phase_A_consts(ctx, WA, consts)
        for s in range(SEQ):
            ctx.push_scope()
            emit_phase_A_seq(ctx, WA, HA, x_fm, stream, s)
            ctx.flush(f"A{s}")
            ctx.pop_scope()
        ctx.pop_scope()
        ctx.push_scope()
        emit_phase_B(ctx, WB, stream, stream, consts)
        ctx.flush("B")
        ctx.pop_scope()
        ctx.push_scope()
        HC = emit_phase_C_consts(ctx, WC, consts)
        for s in range(SEQ):
            ctx.push_scope()
            emit_phase_C_seq(ctx, WC, HC, stream, stream, s)
            ctx.flush(f"C{s}")
            ctx.pop_scope()
        ctx.pop_scope()
        ctx.push_scope()
        emit_phase_D(ctx, WD, stream, actA, actB, y_fm, consts)
        ctx.pop_scope()
    return nc


def kernel_fused(inputs):
    x = inputs["L0_attn_x"]
    x_fm = [np.ascontiguousarray(
        x[4 * c:4 * c + 4].transpose(0, 2, 1).reshape(SEQ, NCH, 128, T)
        .astype(np.float32)) for c in range(NCORES)]
    nc = build_fused()
    Wall = {}
    Wall.update(dmap_prefix(prep_A(inputs), "A"))
    Wall.update(dmap_prefix(prep_B(inputs), "B"))
    Wall.update(dmap_prefix(prep_C(inputs), "C"))
    Wall.update(dmap_prefix(prep_D(inputs), "D"))
    cn = consts_np()
    cn.update(consts_C_np())
    in_maps = []
    for c in range(NCORES):
        m = dict(Wall)
        m.update(cn)
        m["x_fm"] = x_fm[c]
        in_maps.append(m)
    res = run_bass_kernel_spmd(nc, in_maps, core_ids=list(range(NCORES)))
    full = np.concatenate([np.asarray(r["y_fm"]) for r in res.results], axis=0)
    return np.ascontiguousarray(
        full.reshape(32, 1024, 2048).transpose(0, 2, 1)).astype(np.float32)


def kernel(**inputs):
    inputs = {k: np.asarray(v) for k, v in inputs.items()}
    return kernel_fused(inputs)
